# version 08
# baseline (speedup 1.0000x reference)
import numpy as np
import ml_dtypes
from contextlib import ExitStack

import concourse.bass as bass
import concourse.mybir as mybir
from concourse.bass_utils import run_bass_kernel_spmd
from concourse.alu_op_type import AluOpType as Op

F32 = mybir.dt.float32
F32R = mybir.dt.float32r
BF16 = mybir.dt.bfloat16
AF = mybir.ActivationFunctionType
AX = mybir.AxisListType

D = 2048
T = 4096
NC = 16
NS = 8
TS = 512
NH = 16
QLR = 1536
KVLR = 512
DN = 128
DR = 64
DV = 128
QHD = DN + DR
E_LN = 1e-5
E_RMS = 1e-6
INV_SQ = float(1.0 / np.sqrt(float(QHD)))
SM_SHIFT = -30.0

ALL_PHASES = ("P0", "LN1", "M1", "M2", "M3", "LN2", "FFN1", "LN3",
              "AQ", "AKV", "AB", "AO", "LN4", "FFN2", "OUT")


def _bf(x):
    return np.ascontiguousarray(np.asarray(x).astype(ml_dtypes.bfloat16))


class Sem:
    def __init__(self, h):
        self.h = h


class Prog:

    ENGINES = ("sync", "vector", "scalar", "tensor", "gpsimd")

    def __init__(self, nc, stack, block, npool=64):
        self.nc = nc
        self.stack = stack
        self.block = block
        self._pool = [stack.enter_context(nc.semaphore("pool%d" % i))
                      for i in range(npool)]
        self._free = list(self._pool)
        self._used = []
        self._bar = stack.enter_context(nc.semaphore("barrier"))
        self._bar_n = 0
        self._vdum = stack.enter_context(nc.sbuf_tensor("bar_vdum", [1, 2], F32))
        self._sdum = stack.enter_context(nc.sbuf_tensor("bar_sdum", [1, 2], F32))
        self._gdum = stack.enter_context(nc.sbuf_tensor("bar_gdum", [1, 2], F32))

        @block.vector
        def _(vector):
            vector.memset(self._vdum[:], 0.0)

        @block.scalar
        def _(scalar):
            scalar.memzero(self._sdum[:])

        @block.gpsimd
        def _(gpsimd):
            gpsimd.memset(self._gdum[:], 0.0)

    def sem(self, name=None):
        h = self._free.pop()
        self._used.append(h)
        return Sem(h)

    def barrier(self, tag=None):
        n = len(self.ENGINES)
        tgt = self._bar_n + n

        @self.block.sync
        def _(sync):
            sync.nop().then_inc(self._bar, 1)
            sync.wait_ge(self._bar, tgt)

        @self.block.tensor
        def _(tensor):
            tensor.nop().then_inc(self._bar, 1)
            tensor.wait_ge(self._bar, tgt)

        @self.block.vector
        def _(vector):
            vector.memset(self._vdum[0:1, 0:1], 0.0).then_inc(self._bar, 1)
            vector.wait_ge(self._bar, tgt)

        @self.block.scalar
        def _(scalar):
            scalar.activation(self._sdum[0:1, 0:1], self._sdum[0:1, 1:2],
                              AF.Copy).then_inc(self._bar, 1)
            scalar.wait_ge(self._bar, tgt)

        @self.block.gpsimd
        def _(gpsimd):
            gpsimd.memset(self._gdum[0:1, 0:1], 0.0).then_inc(self._bar, 1)
            gpsimd.wait_ge(self._bar, tgt)

        self._bar_n = tgt

    def recycle(self):
        self.barrier()
        used = self._used
        if used:
            @self.block.sync
            def _(sync):
                for h in used:
                    sync.sem_clear(h)
            self.barrier()
        self._free.extend(self._used)
        self._used = []


def r3(ap, c=NC):
    return ap.rearrange("p (c t) -> p c t", c=c)


def host_constants():
    c = {}
    c["ident_f32"] = np.eye(128, dtype=np.float32)
    c["ident_bf"] = _bf(np.eye(128))
    c["ones_row"] = np.ones((1, 128), dtype=np.float32)
    c["ones_col_f"] = np.ones((128, 1), dtype=np.float32)
    c["ones8_f"] = np.ones((128, 8), dtype=np.float32)
    c["ones_col_bf"] = _bf(np.ones((128, 1)))
    sel = np.zeros((128, 16 * 128), dtype=np.float32)
    for i in range(16):
        for m in range(128):
            sel[8 * i + m // 16, 128 * i + m] = 1.0
    c["sel_repl"] = sel
    s16 = np.zeros((16, 128), dtype=np.float32)
    for m in range(128):
        s16[m % 16, m] = 1.0
    c["sel16"] = s16
    ts = np.zeros((128, 16 * 128), dtype=np.float32)
    for i in range(16):
        for p in range(128):
            ts[p, 128 * i + 8 * i + p // 16] = 1.0
    c["tsel_bf"] = _bf(ts)
    ser = np.zeros((8, 128), dtype=np.float32)
    for m in range(128):
        ser[m // 16, m] = 1.0
    c["sel_er"] = ser
    tril = (np.arange(128)[:, None] <= np.arange(128)[None, :]).astype(np.float32)
    c["tril_bf"] = _bf(tril)
    R = np.zeros((64, 64), dtype=np.float32)
    for i in range(32):
        R[i + 32, i] = -1.0
        R[i, i + 32] = 1.0
    c["rot64_bf"] = _bf(R)
    c["fconsts"] = np.tile(np.array([[E_LN, E_RMS, SM_SHIFT, INV_SQ, -0.5, -1.0, 1.0, 0.0]],
                                    dtype=np.float32), (128, 1))
    inv = 1.0 / (10000.0 ** (np.arange(0, DR, 2, dtype=np.float64) / DR))
    tpos = np.arange(T, dtype=np.float64)
    f = np.outer(tpos, inv)
    emb = np.concatenate([f, f], axis=-1)
    c["ropeC"] = np.ascontiguousarray(np.cos(emb).T.astype(np.float32))
    c["ropeS"] = np.ascontiguousarray(np.sin(emb).T.astype(np.float32))
    return c


def host_prepare(inputs):
    g = host_constants()
    g["wxT"] = np.ascontiguousarray(np.asarray(inputs["L0_attn_a_W_xproj"], np.float32).T)
    al = np.asarray(inputs["L0_attn_a_A_log"], np.float32).reshape(-1)
    g["a_log_cols"] = np.ascontiguousarray(al.reshape(256, 128).T)
    g["d_cols"] = np.ascontiguousarray(
        np.asarray(inputs["L0_attn_a_D"], np.float32).reshape(16, 128).T)
    g["qaT"] = np.ascontiguousarray(np.asarray(inputs["L0_attn_b_q_a_w"], np.float32).T)
    g["qa_lnw_cols"] = np.ascontiguousarray(
        np.asarray(inputs["L0_attn_b_q_a_ln_w"], np.float32).reshape(12, 128).T)
    qb = np.asarray(inputs["L0_attn_b_q_b_w"], np.float32).reshape(NH, QHD, QLR)
    nope = qb[:, :DN, :].reshape(NH * DN, QLR)
    pe = qb[:, DN:, :].reshape(NH * DR, QLR)
    g["qbT"] = np.ascontiguousarray(np.concatenate([nope, pe], 0).T)
    g["kvaT"] = np.ascontiguousarray(np.asarray(inputs["L0_attn_b_kv_a_w"], np.float32).T)
    g["kva_lnw_cols"] = np.ascontiguousarray(
        np.asarray(inputs["L0_attn_b_kv_a_ln_w"], np.float32).reshape(4, 128).T)
    kvb = np.asarray(inputs["L0_attn_b_kv_b_w"], np.float32).reshape(NH, DN + DV, KVLR)
    knope = kvb[:, :DN, :].reshape(NH * DN, KVLR)
    vrows = kvb[:, DN:, :].reshape(NH * DV, KVLR)
    g["kvbT"] = np.ascontiguousarray(np.concatenate([knope, vrows], 0).T)
    g["owT"] = np.ascontiguousarray(np.asarray(inputs["L0_attn_b_o_w"], np.float32).T)
    g["fWT"] = np.ascontiguousarray(np.asarray(inputs["L0_ffn_W"], np.float32).T)
    g["fb_cols"] = np.ascontiguousarray(
        np.asarray(inputs["L0_ffn_b"], np.float32).reshape(16, 128).T)
    lA = np.asarray(inputs["L0_ffn_lora_A"], np.float32).reshape(8 * 16, 2048)
    gatew = np.asarray(inputs["L0_ffn_gate_w"], np.float32)
    g["flAT"] = np.ascontiguousarray(np.concatenate([lA, gatew], 0).T)
    g["gate_b_col"] = np.ascontiguousarray(
        np.asarray(inputs["L0_ffn_gate_b"], np.float32).reshape(8, 1))
    g["flBT"] = np.ascontiguousarray(
        np.asarray(inputs["L0_ffn_lora_B"], np.float32).transpose(0, 2, 1).reshape(128, 2048))
    return g


GLOBAL_INPUTS = {
    "wxT": (2048, 2080), "a_log_cols": (128, 256), "d_cols": (128, 16),
    "qaT": (2048, 1536), "qa_lnw_cols": (128, 12), "qbT": (1536, 3072),
    "kvaT": (2048, 576), "kva_lnw_cols": (128, 4), "kvbT": (512, 4096),
    "owT": (2048, 2048),
    "fWT": (2048, 2048), "fb_cols": (128, 16), "flAT": (2048, 136),
    "gate_b_col": (8, 1), "flBT": (128, 2048),
    "ident_f32": (128, 128), "ones_row": (1, 128), "ones_col_f": (128, 1),
    "ones8_f": (128, 8),
    "sel_repl": (128, 2048), "sel16": (16, 128), "sel_er": (8, 128),
    "ropeC": (64, T), "ropeS": (64, T), "fconsts": (128, 8),
}
GLOBAL_INPUTS_BF = {
    "ident_bf": (128, 128), "tsel_bf": (128, 2048), "tril_bf": (128, 128),
    "ones_col_bf": (128, 1), "rot64_bf": (64, 64),
}
SCRATCH = {
    "resid0": (128, NC * T), "resid1": (128, NC * T), "resid2": (128, NC * T),
    "resid3": (128, NC * T), "resid4": (128, NC * T),
}
SCRATCH_R = {
    "dtT_sc": (128, NC * T), "wmT_sc": (128, NC * T),
}
SCRATCH_BF = {
    "xn1": (128, NC * T), "xn2": (128, NC * T), "xn3": (128, NC * T),
    "xn4": (128, NC * T),
    "qn_sc": (128, NH * T), "qpe_sc": (64, NH * T),
    "kn_sc": (128, NH * T), "kpe_sc": (64, T),
    "v_sc": (128, NH * T), "atoT_sc": (128, NH * T),
}


def emit_transpose_in(P, x0, resid0):
    block, nc = P.block, P.nc
    with ExitStack() as st:
        E = st.enter_context
        s_ld, s_tr, s_ev, s_st = (P.sem() for _ in range(4))
        xbuf = [E(nc.sbuf_tensor("ti_x%d" % j, [128, D], F32)) for j in range(2)]
        obuf = [E(nc.sbuf_tensor("ti_o%d" % j, [128, NC * 128], F32)) for j in range(2)]
        pt = [E(nc.psum_tensor("ti_p%d" % j, [128, 128], F32)) for j in range(2)]
        NTB = T // 128

        @block.sync
        def _(sync):
            for tb in range(NTB):
                if tb >= 2:
                    sync.wait_ge(s_tr.h, (tb - 1) * NC)
                sync.dma_start(xbuf[tb % 2][:], x0[tb * 128:(tb + 1) * 128, :]) \
                    .then_inc(s_ld.h, 16)
                if tb >= 1:
                    sync.wait_ge(s_ev.h, tb * NC)
                    dst = r3(resid0)[:, :, (tb - 1) * 128: tb * 128]
                    sync.dma_start(dst, r3(obuf[(tb - 1) % 2].ap(), c=NC)) \
                        .then_inc(s_st.h, 16)
            sync.wait_ge(s_ev.h, NTB * NC)
            dst = r3(resid0)[:, :, (NTB - 1) * 128: NTB * 128]
            sync.dma_start(dst, r3(obuf[(NTB - 1) % 2].ap(), c=NC)).then_inc(s_st.h, 16)
            sync.wait_ge(s_st.h, NTB * 16)

        @block.tensor
        def _(tensor):
            for tb in range(NTB):
                tensor.wait_ge(s_ld.h, (tb + 1) * 16)
                for dc in range(NC):
                    k = tb * NC + dc
                    if k >= 2:
                        tensor.wait_ge(s_ev.h, k - 1)
                    tensor.transpose(pt[k % 2][:],
                                     xbuf[tb % 2][:, dc * 128:(dc + 1) * 128],
                                     P.ident_f[:]).then_inc(s_tr.h, 1)

        @block.scalar
        def _(scalar):
            for tb in range(NTB):
                if tb >= 2:
                    scalar.wait_ge(s_st.h, (tb - 1) * 16)
                for dc in range(NC):
                    k = tb * NC + dc
                    scalar.wait_ge(s_tr.h, k + 1)
                    scalar.activation(obuf[tb % 2][:, dc * 128:(dc + 1) * 128],
                                      pt[k % 2][:], AF.Copy).then_inc(s_ev.h, 1)


def emit_transpose_out(P, resid, out):
    block, nc = P.block, P.nc
    with ExitStack() as st:
        E = st.enter_context
        s_ld, s_tr, s_ev, s_st = (P.sem() for _ in range(4))
        xbuf = [E(nc.sbuf_tensor("to_x%d" % j, [128, NC * 128], F32)) for j in range(2)]
        obuf = [E(nc.sbuf_tensor("to_o%d" % j, [128, D], F32)) for j in range(2)]
        pt = [E(nc.psum_tensor("to_p%d" % j, [128, 128], F32)) for j in range(2)]
        NTB = T // 128

        @block.sync
        def _(sync):
            for tb in range(NTB):
                if tb >= 2:
                    sync.wait_ge(s_tr.h, (tb - 1) * NC)
                src = r3(resid)[:, :, tb * 128:(tb + 1) * 128]
                sync.dma_start(r3(xbuf[tb % 2].ap(), c=NC), src).then_inc(s_ld.h, 16)
                if tb >= 1:
                    sync.wait_ge(s_ev.h, tb * NC)
                    sync.dma_start(out[(tb - 1) * 128: tb * 128, :],
                                   obuf[(tb - 1) % 2][:]).then_inc(s_st.h, 16)
            sync.wait_ge(s_ev.h, NTB * NC)
            sync.dma_start(out[(NTB - 1) * 128: NTB * 128, :],
                           obuf[(NTB - 1) % 2][:]).then_inc(s_st.h, 16)
            sync.wait_ge(s_st.h, NTB * 16)

        @block.tensor
        def _(tensor):
            for tb in range(NTB):
                tensor.wait_ge(s_ld.h, (tb + 1) * 16)
                for dc in range(NC):
                    k = tb * NC + dc
                    if k >= 2:
                        tensor.wait_ge(s_ev.h, k - 1)
                    tensor.transpose(pt[k % 2][:],
                                     xbuf[tb % 2][:, dc * 128:(dc + 1) * 128],
                                     P.ident_f[:]).then_inc(s_tr.h, 1)

        @block.scalar
        def _(scalar):
            for tb in range(NTB):
                if tb >= 2:
                    scalar.wait_ge(s_st.h, (tb - 1) * 16)
                for dc in range(NC):
                    k = tb * NC + dc
                    scalar.wait_ge(s_tr.h, k + 1)
                    scalar.activation(obuf[tb % 2][:, dc * 128:(dc + 1) * 128],
                                      pt[k % 2][:], AF.Copy).then_inc(s_ev.h, 1)


def emit_ln(P, resid, xn, tag):
    block, nc = P.block, P.nc
    with ExitStack() as st:
        E = st.enter_context
        s_ld, s_cst, s_sq, s_stat, s_sm1, s_rstd, s_sm2, s_rep, s_app, s_st = \
            (P.sem() for _ in range(10))
        xbuf = [E(nc.sbuf_tensor(tag + "_x%d" % j, [128, NC * TS], F32)) for j in range(2)]
        xb16 = E(nc.sbuf_tensor(tag + "_xb", [128, NC * TS], BF16))
        xsq = E(nc.sbuf_tensor(tag + "_sq", [128, NC * TS], BF16))
        onb = E(nc.sbuf_tensor(tag + "_onb", [128, NC * TS], BF16))
        tmp = E(nc.sbuf_tensor(tag + "_tmp", [128, TS], F32))
        small = E(nc.sbuf_tensor(tag + "_small", [1, 5 * TS], F32))
        pm = E(nc.psum_tensor(tag + "_pm", [128, TS], F32))
        pq = E(nc.psum_tensor(tag + "_pq", [128, TS], F32))
        prep = E(nc.psum_tensor(tag + "_prep", [128, 2 * TS], F32))

        def C(dc):
            return slice(dc * TS, (dc + 1) * TS)

        M = small[0:1, 0:TS]
        MSQ = small[0:1, TS:2 * TS]
        VAR = small[0:1, 2 * TS:3 * TS]
        RSTD = small[0:1, 3 * TS:4 * TS]
        NMR = small[0:1, 4 * TS:5 * TS]

        @block.sync
        def _(sync):
            for s in range(NS):
                if s >= 2:
                    sync.wait_ge(s_app.h, (s - 1) * 2 * NC)
                src = r3(resid)[:, :, s * TS:(s + 1) * TS]
                sync.dma_start(r3(xbuf[s % 2].ap(), c=NC), src).then_inc(s_ld.h, 16)
                sync.wait_ge(s_app.h, (s + 1) * 2 * NC)
                dst = r3(xn)[:, :, s * TS:(s + 1) * TS]
                sync.dma_start(dst, r3(onb.ap(), c=NC)).then_inc(s_st.h, 16)
            sync.wait_ge(s_st.h, NS * 16)

        @block.scalar
        def _(scalar):
            for s in range(NS):
                if s >= 1:
                    scalar.wait_ge(s_stat.h, s * 2 * NC)
                for dc in range(NC):
                    scalar.wait_ge(s_cst.h, s * NC + dc + 1)
                    scalar.activation(xsq[:, C(dc)], xb16[:, C(dc)],
                                      AF.Square).then_inc(s_sq.h, 1)
                scalar.wait_ge(s_sm1.h, s + 1)
                if s >= 1:
                    scalar.wait_ge(s_rep.h, 2 * s)
                scalar.activation(RSTD, VAR, AF.Ln, bias=E_LN)
                scalar.activation(RSTD, RSTD, AF.Exp, scale=-0.5).then_inc(s_rstd.h, 1)

        @block.tensor
        def _(tensor):
            for s in range(NS):
                tensor.wait_ge(s_ld.h, (s + 1) * 16)
                if s >= 1:
                    tensor.wait_ge(s_sm1.h, s)
                for dc in range(NC):
                    tensor.wait_ge(s_cst.h, s * NC + dc + 1)
                    tensor.wait_ge(s_sq.h, s * NC + dc + 1)
                    tensor.matmul(pm[0:1, :], P.ones_col_b[:, 0:1],
                                  xb16[:, C(dc)],
                                  start=(dc == 0), stop=(dc == NC - 1))
                    tensor.matmul(pq[0:1, :], P.ones_col_b[:, 0:1],
                                  xsq[:, C(dc)],
                                  start=(dc == 0), stop=(dc == NC - 1)) \
                        .then_inc(s_stat.h, 2)
                tensor.wait_ge(s_sm2.h, s + 1)
                if s >= 1:
                    tensor.wait_ge(s_app.h, s * 2 * NC)
                tensor.matmul(prep[:, 0:TS], P.ones_row[0:1, :], RSTD,
                              start=True, stop=True)
                tensor.matmul(prep[:, TS:2 * TS], P.ones_row[0:1, :], NMR,
                              start=True, stop=True).then_inc(s_rep.h, 2)

        @block.vector
        def _(vector):
            for s in range(NS):
                vector.wait_ge(s_ld.h, (s + 1) * 16)
                if s >= 1:
                    vector.wait_ge(s_stat.h, s * 2 * NC)
                for dc in range(NC):
                    vector.tensor_copy(xb16[:, C(dc)], xbuf[s % 2][:, C(dc)]) \
                        .then_inc(s_cst.h, 1)
                vector.wait_ge(s_stat.h, (s + 1) * 2 * NC)
                vector.tensor_scalar(M, pm[0:1, :], 1.0 / D, None, Op.mult)
                vector.tensor_scalar(MSQ, pq[0:1, :], 1.0 / D, None, Op.mult)
                vector.tensor_tensor(VAR, M, M, Op.mult)
                vector.tensor_tensor(VAR, MSQ, VAR, Op.subtract).then_inc(s_sm1.h, 1)
                vector.wait_ge(s_rstd.h, s + 1)
                if s >= 1:
                    vector.wait_ge(s_rep.h, 2 * s)
                vector.tensor_tensor(NMR, M, RSTD, Op.mult)
                vector.tensor_scalar(NMR, NMR, -1.0, None, Op.mult).then_inc(s_sm2.h, 1)
                vector.wait_ge(s_rep.h, (s + 1) * 2)
                if s >= 1:
                    vector.wait_ge(s_st.h, s * 16)
                for dc in range(NC):
                    vector.tensor_tensor(tmp[:], xbuf[s % 2][:, C(dc)],
                                         prep[:, 0:TS], Op.mult).then_inc(s_app.h, 1)
                    vector.tensor_tensor(onb[:, C(dc)], tmp[:],
                                         prep[:, TS:2 * TS], Op.add).then_inc(s_app.h, 1)


def emit_cast_weights(P, src_dram, dst_sbuf, rows, cols, tag):
    block, nc = P.block, P.nc
    nchunk = rows // 128
    with ExitStack() as st:
        E = st.enter_context
        s_ld, s_cv = P.sem(), P.sem()
        stage = [E(nc.sbuf_tensor(tag + "_wst%d" % j, [128, cols], F32)) for j in range(2)]

        @block.sync
        def _(sync):
            for ch in range(nchunk):
                if ch >= 2:
                    sync.wait_ge(s_cv.h, ch - 1)
                sync.dma_start(stage[ch % 2][:], src_dram[ch * 128:(ch + 1) * 128, :]) \
                    .then_inc(s_ld.h, 16)
            sync.wait_ge(s_cv.h, nchunk)

        @block.scalar
        def _(scalar):
            for ch in range(nchunk):
                scalar.wait_ge(s_ld.h, (ch + 1) * 16)
                scalar.activation(dst_sbuf[:, ch * cols:(ch + 1) * cols],
                                  stage[ch % 2][:], AF.Copy).then_inc(s_cv.h, 1)


def emit_xproj(P, dins, dsc, tag="m1"):
    block, nc = P.block, P.nc
    with ExitStack() as st:
        E = st.enter_context
        wxb = E(nc.sbuf_tensor("wxb", [128, NC * 2080], BF16))
        emit_cast_weights(P, dins["wxT"], wxb, 2048, 2080, tag)
        P.barrier(tag + "w")

        s_ld, s_mm, s_sp, s_dw, s_st = (P.sem() for _ in range(5))
        xnb = [E(nc.sbuf_tensor(tag + "_xn%d" % j, [128, NC * TS], BF16)) for j in range(2)]
        pxp = [E(nc.psum_tensor(tag + "_p%d" % j, [128, TS], F32)) for j in range(2)]
        sp1 = [E(nc.sbuf_tensor(tag + "_s1%d" % j, [128, TS], F32)) for j in range(2)]
        sp2 = [E(nc.sbuf_tensor(tag + "_s2%d" % j, [128, TS], F32)) for j in range(2)]
        dtb = [E(nc.sbuf_tensor(tag + "_dt%d" % j, [128, TS], F32)) for j in range(2)]
        dtr = [E(nc.sbuf_tensor(tag + "_dr%d" % j, [128, TS], F32R)) for j in range(2)]
        wtb = [E(nc.sbuf_tensor(tag + "_wt%d" % j, [128, TS], F32R)) for j in range(2)]
        NM = 18

        def wcols(m):
            if m < 16:
                return (m * 128, (m + 1) * 128)
            if m == 16:
                return (2048, 2064)
            return (2064, 2080)

        @block.sync
        def _(sync):
            for s in range(NS):
                if s >= 2:
                    sync.wait_ge(s_mm.h, (s - 1) * NM)
                    sync.wait_ge(s_dw.h, (s - 1) * 48)
                src = r3(dsc["xn1"])[:, :, s * TS:(s + 1) * TS]
                sync.dma_start(r3(xnb[s % 2].ap(), c=NC), src).then_inc(s_ld.h, 16)
                for m in range(16):
                    km = s * 16 + m
                    sync.wait_ge(s_dw.h, 3 * km + 3)
                    sync.dma_start(dsc["dtT_sc"][:, m * T + s * TS: m * T + (s + 1) * TS],
                                   dtr[km % 2][:]).then_inc(s_st.h, 16)
                    sync.dma_start(dsc["wmT_sc"][:, m * T + s * TS: m * T + (s + 1) * TS],
                                   wtb[km % 2][:]).then_inc(s_st.h, 16)
            sync.wait_ge(s_st.h, NS * 32 * 16)

        @block.tensor
        def _(tensor):
            for s in range(NS):
                tensor.wait_ge(s_ld.h, (s + 1) * 16)
                for m in range(NM):
                    k = s * NM + m
                    if k >= 2:
                        tensor.wait_ge(s_sp.h, k - 1)
                    c0, c1 = wcols(m)
                    rows = 128 if m < 16 else 16
                    for dc in range(NC):
                        ins = tensor.matmul(pxp[k % 2][0:rows, :],
                                            wxb[:, dc * 2080 + c0: dc * 2080 + c1],
                                            xnb[s % 2][:, dc * TS:(dc + 1) * TS],
                                            start=(dc == 0), stop=(dc == NC - 1))
                    ins.then_inc(s_mm.h, 1)

        @block.scalar
        def _(scalar):
            for s in range(NS):
                for m in range(NM):
                    k = s * NM + m
                    scalar.wait_ge(s_mm.h, k + 1)
                    if m < 16:
                        km = s * 16 + m
                        if km >= 2:
                            scalar.wait_ge(s_dw.h, 3 * km - 3)
                        scalar.activation(sp1[km % 2][:], pxp[k % 2][:], AF.Abs)
                        scalar.activation(sp1[km % 2][:], sp1[km % 2][:], AF.Exp, scale=-1.0)
                        scalar.activation(sp1[km % 2][:], sp1[km % 2][:], AF.Ln, bias=1.0)
                        scalar.activation(sp2[km % 2][:], pxp[k % 2][:], AF.Relu) \
                            .then_inc(s_sp.h, 1)
                    elif m == 16:
                        scalar.activation(P.btT[:, s * TS:(s + 1) * TS],
                                          pxp[k % 2][0:16, :], AF.Copy).then_inc(s_sp.h, 1)
                    else:
                        scalar.activation(P.ctT[:, s * TS:(s + 1) * TS],
                                          pxp[k % 2][0:16, :], AF.Copy).then_inc(s_sp.h, 1)

        @block.vector
        def _(vector):
            for s in range(NS):
                vector.wait_ge(s_ld.h, (s + 1) * 16)
                for m in range(16):
                    k = s * NM + m
                    km = s * 16 + m
                    vector.wait_ge(s_sp.h, k + 1)
                    if km >= 2:
                        vector.wait_ge(s_st.h, (km - 1) * 32)
                    vector.tensor_tensor(dtb[km % 2][:], sp1[km % 2][:],
                                         sp2[km % 2][:], Op.add).then_inc(s_dw.h, 1)
                    vector.tensor_tensor(wtb[km % 2][:], dtb[km % 2][:],
                                         xnb[s % 2][:, m * TS:(m + 1) * TS], Op.mult) \
                        .then_inc(s_dw.h, 1)
                    vector.tensor_copy(dtr[km % 2][:], dtb[km % 2][:]) \
                        .then_inc(s_dw.h, 1)


def emit_bcpat(P, tag="m2"):
    block, nc = P.block, P.nc
    with ExitStack() as st:
        E = st.enter_context
        s_mm, s_ev = P.sem(), P.sem()
        pp = [E(nc.psum_tensor(tag + "_p%d" % j, [128, TS], F32)) for j in range(2)]

        @block.tensor
        def _(tensor):
            for s in range(NS):
                for j in range(2):
                    src = P.btT if j == 0 else P.ctT
                    k = s * 2 + j
                    if k >= 2:
                        tensor.wait_ge(s_ev.h, k - 1)
                    tensor.matmul(pp[k % 2][:], P.sel16[0:16, :],
                                  src[:, s * TS:(s + 1) * TS],
                                  start=True, stop=True).then_inc(s_mm.h, 1)

        @block.vector
        def _(vector):
            for s in range(NS):
                for j in range(2):
                    dst = P.bpat if j == 0 else P.cpat
                    k = s * 2 + j
                    vector.wait_ge(s_mm.h, k + 1)
                    vector.tensor_copy(dst[:, s * TS:(s + 1) * TS], pp[k % 2][:]) \
                        .then_inc(s_ev.h, 1)


def emit_scan(P, dsc, tag="m3"):
    block, nc = P.block, P.nc
    HT = T // 2
    with ExitStack() as st:
        E = st.enter_context
        s_ld = P.sem("m3ld")
        s_md = P.sem("m3md")
        s_mw = P.sem("m3mw")
        s_a = P.sem("m3a")
        s_u = P.sem("m3u")
        s_sc = P.sem("m3sc")
        s_po = P.sem("m3po")
        s_sum = P.sem("m3sum")
        s_yf = P.sem("m3yf")
        s_st = P.sem("m3st")

        dtc = E(nc.sbuf_tensor("m3_dt", [128, T], F32R))
        wc = E(nc.sbuf_tensor("m3_w", [128, T], F32R))
        xnc = E(nc.sbuf_tensor("m3_xn", [128, T], BF16))
        rc = [E(nc.sbuf_tensor("m3_rc%d" % j, [128, T], F32)) for j in range(2)]
        abuf = [E(nc.sbuf_tensor("m3_a%d" % j, [128, HT], F32)) for j in range(3)]
        ubuf = [E(nc.sbuf_tensor("m3_u%d" % j, [128, HT], F32)) for j in range(2)]
        hbuf = [E(nc.sbuf_tensor("m3_h%d" % j, [128, HT], F32)) for j in range(3)]
        hp = [E(nc.sbuf_tensor("m3_hp%d" % j, [128, HT], BF16)) for j in range(3)]
        carry = E(nc.sbuf_tensor("m3_carry", [128, 16], F32))
        yt = E(nc.sbuf_tensor("m3_yt", [128, TS], F32))
        yo = [E(nc.sbuf_tensor("m3_yo%d" % j, [128, TS], F32)) for j in range(2)]
        pd = E(nc.psum_tensor("m3_pd", [128, 2 * TS], F32))
        pw = E(nc.psum_tensor("m3_pw", [128, 2 * TS], F32))
        py = [E(nc.psum_tensor("m3_py%d" % j, [128, TS], F32)) for j in range(4)]

        @block.sync
        def _(sync):
            for c in range(16):
                if c >= 1:
                    sync.wait_ge(s_yf.h, c * 8)
                sync.dma_start(xnc[:], dsc["xn1"][:, c * T:(c + 1) * T]) \
                    .then_inc(s_ld.h, 16)
                sync.dma_start(rc[c % 2][:], dsc["resid0"][:, c * T:(c + 1) * T]) \
                    .then_inc(s_ld.h, 16)
                if c >= 1:
                    sync.wait_ge(s_md.h, 128 * c)
                    sync.wait_ge(s_mw.h, 128 * c)
                sync.dma_start(dtc[:], dsc["dtT_sc"][:, c * T:(c + 1) * T]) \
                    .then_inc(s_ld.h, 16)
                sync.dma_start(wc[:], dsc["wmT_sc"][:, c * T:(c + 1) * T]) \
                    .then_inc(s_ld.h, 16)
                for hf in range(2):
                    g = c * 2 + hf
                    for sl in range(4):
                        sync.wait_ge(s_yf.h, g * 4 + sl + 1)
                        t0 = c * T + hf * HT + sl * TS
                        sync.dma_start(dsc["resid1"][:, t0:t0 + TS],
                                       yo[(g * 4 + sl) % 2][:]).then_inc(s_st.h, 16)
            sync.wait_ge(s_st.h, 16 * 128)

        def sum_mms(tensor, g, i, half2):
            if half2 < 0:
                return
            j = g * 16 + i
            tensor.wait_ge(s_po.h, j + 1)
            for sl in (2 * half2, 2 * half2 + 1):
                if i == 0 and g >= 1:
                    tensor.wait_ge(s_yf.h, (g - 1) * 4 + sl + 1)
                tensor.matmul(py[sl][:], P.tsel_b[:, i * 128:(i + 1) * 128],
                              hp[i % 3][:, sl * TS:(sl + 1) * TS],
                              start=(i == 0), stop=(i == 15)).then_inc(s_sum.h, 1)

        @block.tensor
        def _(tensor):
            for c in range(16):
                tensor.wait_ge(s_ld.h, (c + 1) * 64)
                for hf in range(2):
                    g = c * 2 + hf
                    for i in range(16):
                        j = g * 16 + i
                        selL = P.sel_replr[:, i * 128:(i + 1) * 128]
                        for half2 in range(2):
                            for sl in (2 * half2, 2 * half2 + 1):
                                t0 = hf * HT + sl * TS
                                if j >= 1 or sl >= 2:
                                    tensor.wait_ge(s_a.h, 2 * j + sl // 2)
                                tensor.matmul(pd[:, (sl % 2) * TS:(sl % 2 + 1) * TS],
                                              selL,
                                              dtc[:, t0:t0 + TS],
                                              start=True, stop=True).then_inc(s_md.h, 1)
                            for sl in (2 * half2, 2 * half2 + 1):
                                t0 = hf * HT + sl * TS
                                if j >= 1 or sl >= 2:
                                    tensor.wait_ge(s_u.h, 2 * j + sl // 2)
                                tensor.matmul(pw[:, (sl % 2) * TS:(sl % 2 + 1) * TS],
                                              selL,
                                              wc[:, t0:t0 + TS],
                                              start=True, stop=True).then_inc(s_mw.h, 1)
                            if i >= 2:
                                sum_mms(tensor, g, i - 2, half2)
                    for it in (14, 15):
                        sum_mms(tensor, g, it, 0)
                        sum_mms(tensor, g, it, 1)

        @block.scalar
        def _(scalar):
            for c in range(16):
                for hf in range(2):
                    g = c * 2 + hf
                    for i in range(16):
                        j = g * 16 + i
                        acol = P.a_cols[:, 16 * c + i:16 * c + i + 1]
                        scalar.wait_ge(s_md.h, j * 4 + 2)
                        if j >= 3:
                            scalar.wait_ge(s_sc.h, j - 2)
                        scalar.activation(abuf[i % 3][:, 0:2 * TS], pd[:, 0:2 * TS],
                                          AF.Exp, scale=acol).then_inc(s_a.h, 1)
                        scalar.wait_ge(s_md.h, j * 4 + 4)
                        scalar.activation(abuf[i % 3][:, 2 * TS:4 * TS], pd[:, 0:2 * TS],
                                          AF.Exp, scale=acol).then_inc(s_a.h, 1)

        @block.vector
        def _(vector):
            for c in range(16):
                for hf in range(2):
                    g = c * 2 + hf
                    for i in range(16):
                        j = g * 16 + i
                        vector.wait_ge(s_mw.h, j * 4 + 2)
                        vector.tensor_tensor(ubuf[i % 2][:, 0:2 * TS], pw[:, 0:2 * TS],
                                             P.bpat[:, hf * HT:hf * HT + 2 * TS],
                                             Op.mult).then_inc(s_u.h, 1)
                        vector.wait_ge(s_mw.h, j * 4 + 4)
                        vector.tensor_tensor(ubuf[i % 2][:, 2 * TS:4 * TS],
                                             pw[:, 0:2 * TS],
                                             P.bpat[:, hf * HT + 2 * TS:(hf + 1) * HT],
                                             Op.mult).then_inc(s_u.h, 1)
                        vector.wait_ge(s_a.h, 2 * j + 2)
                        if j >= 3:
                            vector.wait_ge(s_po.h, j - 2)
                        init = 0.0 if hf == 0 else carry[:, i:i + 1]
                        vector.tensor_tensor_scan(hbuf[i % 3][:], abuf[i % 3][:],
                                                  ubuf[i % 2][:], init,
                                                  Op.mult, Op.add).then_inc(s_sc.h, 1)
                        if hf == 0:
                            vector.wait_ge(s_sc.h, j + 1)
                            vector.tensor_copy(carry[:, i:i + 1],
                                               hbuf[i % 3][:, HT - 1:HT])
                    for sl in range(4):
                        k = g * 4 + sl
                        vector.wait_ge(s_sum.h, g * 64 + 60 + sl + 1)
                        t0 = hf * HT + sl * TS
                        vector.scalar_tensor_tensor(
                            yt[:], xnc[:, t0:t0 + TS], P.d_cols[:, c:c + 1],
                            py[sl][:], Op.mult, Op.add)
                        if k >= 2:
                            vector.wait_ge(s_st.h, (k - 1) * 16)
                        vector.tensor_tensor(yo[k % 2][:], yt[:],
                                             rc[c % 2][:, t0:t0 + TS], Op.add) \
                            .then_inc(s_yf.h, 1)

        @block.gpsimd
        def _(gpsimd):
            for c in range(16):
                for hf in range(2):
                    g = c * 2 + hf
                    for i in range(16):
                        j = g * 16 + i
                        gpsimd.wait_ge(s_sc.h, j + 1)
                        if j >= 3:
                            gpsimd.wait_ge(s_sum.h, (j - 2) * 4)
                        gpsimd.tensor_tensor(hp[i % 3][:], hbuf[i % 3][:],
                                             P.cpat[:, hf * HT:(hf + 1) * HT],
                                             Op.mult).then_inc(s_po.h, 1)


FFN_LORA = True
FFN_DBG = False


def emit_ffn(P, dins, xn, resid_in, resid_out, tag):
    block, nc = P.block, P.nc
    with ExitStack() as st:
        E = st.enter_context
        fwb = E(nc.sbuf_tensor(tag + "_fwb", [128, NC * 2048], BF16))
        flab = E(nc.sbuf_tensor(tag + "_flab", [128, NC * 136], BF16))
        flbb = E(nc.sbuf_tensor(tag + "_flbb", [128, 2048], BF16))
        emit_cast_weights(P, dins["fWT"], fwb, 2048, 2048, tag + "a")
        emit_cast_weights(P, dins["flAT"], flab, 2048, 136, tag + "b")
        emit_cast_weights(P, dins["flBT"], flbb, 128, 2048, tag + "c")
        P.barrier(tag + "w")

        s_ld, s_mg, s_lg, s_gb, s_t1, s_gd, s_t2, s_wt, s_wp, s_xa, s_z, \
            s_mo, s_ev, s_st, s_gs = (P.sem() for _ in range(15))

        xnb = [E(nc.sbuf_tensor(tag + "_xn%d" % j, [128, NC * TS], BF16)) for j in range(2)]
        rbuf = [E(nc.sbuf_tensor(tag + "_r%d" % j, [128, NC * TS], F32)) for j in range(2)]
        lg = E(nc.sbuf_tensor(tag + "_lg", [8, TS], F32))
        lgt = E(nc.sbuf_tensor(tag + "_lgt", [128, 32], F32))
        lmm = E(nc.sbuf_tensor(tag + "_lmm", [128, 32], F32))
        mk1 = E(nc.sbuf_tensor(tag + "_mk1", [128, 32], F32))
        m1c = E(nc.sbuf_tensor(tag + "_m1c", [128, 4], F32))
        m2c = E(nc.sbuf_tensor(tag + "_m2c", [128, 4], F32))
        e2c = E(nc.sbuf_tensor(tag + "_e2c", [128, 4], F32))
        w1c = E(nc.sbuf_tensor(tag + "_w1c", [128, 4], F32))
        w2c = E(nc.sbuf_tensor(tag + "_w2c", [128, 4], F32))
        wv = E(nc.sbuf_tensor(tag + "_wv", [128, 32], F32))
        wT = E(nc.sbuf_tensor(tag + "_wT", [8, TS], F32))
        xat = E(nc.sbuf_tensor(tag + "_xat", [128, TS], BF16))
        zt = E(nc.sbuf_tensor(tag + "_zt", [128, TS], BF16))
        ot = [E(nc.sbuf_tensor(tag + "_ot%d" % j, [128, TS], F32)) for j in range(2)]
        pxa = E(nc.psum_tensor(tag + "_pxa", [128, TS], F32))
        pg = E(nc.psum_tensor(tag + "_pg", [8, TS], F32))
        ptg = E(nc.psum_tensor(tag + "_ptg", [128, 32], F32))
        ptw = E(nc.psum_tensor(tag + "_ptw", [8, TS], F32))
        pwp = E(nc.psum_tensor(tag + "_pwp", [128, TS], F32))
        pbase = [E(nc.psum_tensor(tag + "_pb%d" % j, [128, TS], F32)) for j in range(2)]

        @block.sync
        def _(sync):
            for s in range(NS):
                if s >= 2:
                    sync.wait_ge(s_mo.h, (s - 1) * NC)
                    sync.wait_ge(s_ev.h, (s - 1) * NC)
                src = r3(xn)[:, :, s * TS:(s + 1) * TS]
                sync.dma_start(r3(xnb[s % 2].ap(), c=NC), src).then_inc(s_ld.h, 16)
                src2 = r3(resid_in)[:, :, s * TS:(s + 1) * TS]
                sync.dma_start(r3(rbuf[s % 2].ap(), c=NC), src2).then_inc(s_ld.h, 16)
                if FFN_DBG and tag == "f1":
                    sync.wait_ge(s_z.h, s + 1)
                    sync.dma_start(P.dsc["dbg_lgt"][:, s * 32:(s + 1) * 32], lgt[:]).then_inc(s_st.h, 16)
                    sync.dma_start(P.dsc["dbg_wv"][:, s * 32:(s + 1) * 32], wv[:]).then_inc(s_st.h, 16)
                    sync.dma_start(P.dsc["dbg_wT"][0:8, s * TS:(s + 1) * TS], wT[0:8, :]).then_inc(s_st.h, 16)
                    sync.dma_start(P.dsc["dbg_zt"][:, s * TS:(s + 1) * TS], zt[:]).then_inc(s_st.h, 16)
                    sync.dma_start(P.dsc["dbg_xat"][:, s * TS:(s + 1) * TS], xat[:]).then_inc(s_st.h, 16)
                for oc in range(NC):
                    ko = s * NC + oc
                    sync.wait_ge(s_ev.h, ko + 1)
                    dst = resid_out[:, oc * T + s * TS: oc * T + (s + 1) * TS]
                    sync.dma_start(dst, ot[ko % 2][:]).then_inc(s_st.h, 16)
            sync.wait_ge(s_st.h, NS * NC * 16 + (NS * 80 if (FFN_DBG and tag == "f1") else 0))

        @block.tensor
        def _(tensor):
            for s in range(NS):
                tensor.wait_ge(s_ld.h, (s + 1) * 32)
                if s >= 1:
                    tensor.wait_ge(s_xa.h, s)
                    tensor.wait_ge(s_lg.h, 2 * s - 1)
                for dc in range(NC):
                    tensor.matmul(pxa[:], flab[:, dc * 136:dc * 136 + 128],
                                  xnb[s % 2][:, dc * TS:(dc + 1) * TS],
                                  start=(dc == 0), stop=(dc == NC - 1))
                for dc in range(NC):
                    ins = tensor.matmul(pg[0:8, :], flab[:, dc * 136 + 128:(dc + 1) * 136],
                                        xnb[s % 2][:, dc * TS:(dc + 1) * TS],
                                        start=(dc == 0), stop=(dc == NC - 1))
                ins.then_inc(s_mg.h, 1)
                tensor.wait_ge(s_gb.h, s + 1)
                if s >= 1:
                    tensor.wait_ge(s_gs.h, 36 * (s - 1) + 1)
                for tb in range(4):
                    tensor.transpose(ptg[:, tb * 8:(tb + 1) * 8],
                                     lg[0:8, tb * 128:(tb + 1) * 128],
                                     P.ident_f[0:8, 0:8]).then_inc(s_t1.h, 1)
                tensor.wait_ge(s_gs.h, 36 * s + 36)
                if s >= 1:
                    tensor.wait_ge(s_wt.h, s)
                for tb in range(4):
                    tensor.transpose(ptw[0:8, tb * 128:(tb + 1) * 128],
                                     wv[:, tb * 8:(tb + 1) * 8],
                                     P.ident_f[:]).then_inc(s_t2.h, 1)
                tensor.wait_ge(s_wt.h, s + 1)
                if s >= 1:
                    tensor.wait_ge(s_z.h, s)
                tensor.matmul(pwp[:], P.sel_er[0:8, :], wT[0:8, :],
                              start=True, stop=True).then_inc(s_wp.h, 1)
                for oc in range(NC):
                    ko = s * NC + oc
                    if ko >= 2:
                        tensor.wait_ge(s_ev.h, ko - 1)
                    for dc in range(NC):
                        ins = tensor.matmul(pbase[ko % 2][:],
                                      fwb[:, dc * 2048 + oc * 128: dc * 2048 + (oc + 1) * 128],
                                      xnb[s % 2][:, dc * TS:(dc + 1) * TS],
                                      start=(dc == 0), stop=(not FFN_LORA and dc == NC - 1))
                    if FFN_LORA:
                        if oc == 0:
                            tensor.wait_ge(s_z.h, s + 1)
                        tensor.matmul(pbase[ko % 2][:], flbb[:, oc * 128:(oc + 1) * 128],
                                      zt[:], start=False, stop=True).then_inc(s_mo.h, 1)
                    else:
                        ins.then_inc(s_mo.h, 1)

        @block.scalar
        def _(scalar):
            for s in range(NS):
                scalar.wait_ge(s_mg.h, s + 1)
                if s >= 1:
                    scalar.wait_ge(s_t1.h, 4 * s)
                scalar.activation(lg[0:8, :], pg[0:8, :], AF.Copy).then_inc(s_lg.h, 1)
                scalar.wait_ge(s_gs.h, 36 * s + 21)
                scalar.activation(e2c[:], m2c[:], AF.Exp).then_inc(s_lg.h, 1)
                scalar.wait_ge(s_t2.h, (s + 1) * 4)
                if s >= 1:
                    scalar.wait_ge(s_wp.h, s)
                scalar.activation(wT[0:8, :], ptw[0:8, :], AF.Copy).then_inc(s_wt.h, 1)
                scalar.wait_ge(s_mg.h, s + 1)
                if s >= 1:
                    scalar.wait_ge(s_z.h, s)
                scalar.activation(xat[:], pxa[:], AF.Copy).then_inc(s_xa.h, 1)

        @block.vector
        def _(vector):
            for s in range(NS):
                vector.wait_ge(s_lg.h, 2 * s + 1)
                vector.tensor_scalar(lg[0:8, :], lg[0:8, :], P.gate_b[0:8, 0:1],
                                     None, Op.add).then_inc(s_gb.h, 1)
                gctr = [36 * s]

                def gv(ins):
                    gctr[0] += 1
                    ins.then_inc(s_gs.h, 1)
                    vector.wait_ge(s_gs.h, gctr[0])

                vector.wait_ge(s_t1.h, (s + 1) * 4)
                gv(vector.tensor_copy(lgt[:], ptg[:]))
                for tb in range(4):
                    sl8 = slice(tb * 8, (tb + 1) * 8)
                    gv(vector.reduce_max(m1c[:, tb:tb + 1], lgt[:, sl8], axis=AX.X))
                for tb in range(4):
                    sl8 = slice(tb * 8, (tb + 1) * 8)
                    gv(vector.tensor_scalar(lgt[:, sl8], lgt[:, sl8], m1c[:, tb:tb + 1],
                                            None, Op.subtract))
                    gv(vector.tensor_scalar(mk1[:, sl8], lgt[:, sl8], 0.0, None, Op.is_ge))
                    gv(vector.scalar_tensor_tensor(lmm[:, sl8], mk1[:, sl8], -1e30,
                                                   lgt[:, sl8], Op.mult, Op.add))
                    gv(vector.reduce_max(m2c[:, tb:tb + 1], lmm[:, sl8], axis=AX.X))
                vector.wait_ge(s_lg.h, 2 * s + 2)
                if s >= 1:
                    vector.wait_ge(s_t2.h, 4 * s)
                gv(vector.tensor_scalar(e2c[:], e2c[:], 1.0, None, Op.add))
                gv(vector.reciprocal(w1c[:], e2c[:]))
                gv(vector.tensor_scalar(w2c[:], w1c[:], -1.0, 1.0, Op.mult, Op.add))
                for tb in range(4):
                    sl8 = slice(tb * 8, (tb + 1) * 8)
                    gv(vector.tensor_scalar(wv[:, sl8], lmm[:, sl8], m2c[:, tb:tb + 1],
                                            0.0, Op.subtract, Op.is_ge))
                    gv(vector.tensor_scalar(wv[:, sl8], wv[:, sl8], w2c[:, tb:tb + 1],
                                            None, Op.mult))
                    ins = vector.scalar_tensor_tensor(wv[:, sl8], mk1[:, sl8],
                                                      w1c[:, tb:tb + 1], wv[:, sl8],
                                                      Op.mult, Op.add)
                    if tb < 3:
                        gv(ins)
                    else:
                        gctr[0] += 1
                        ins.then_inc(s_gs.h, 1)
                vector.wait_ge(s_wp.h, s + 1)
                vector.wait_ge(s_xa.h, s + 1)
                if s >= 1:
                    vector.wait_ge(s_mo.h, s * NC)
                vector.tensor_tensor(zt[:], xat[:], pwp[:], Op.mult).then_inc(s_z.h, 1)
                for oc in range(NC):
                    ko = s * NC + oc
                    vector.wait_ge(s_mo.h, ko + 1)
                    if ko >= 2:
                        vector.wait_ge(s_st.h, (ko - 1) * 16)
                    vector.scalar_tensor_tensor(
                        ot[ko % 2][:], pbase[ko % 2][:], P.fb_cols[:, oc:oc + 1],
                        rbuf[s % 2][:, oc * TS:(oc + 1) * TS], Op.add, Op.add) \
                        .then_inc(s_ev.h, 1)


def emit_attn_q(P, dins, dsc, tag="aq"):
    block, nc = P.block, P.nc
    with ExitStack() as st:
        E = st.enter_context
        qab = E(nc.sbuf_tensor("qab", [128, NC * 1536], BF16))
        qbb = E(nc.sbuf_tensor("qbb", [128, 12 * 3072], BF16))
        emit_cast_weights(P, dins["qaT"], qab, 2048, 1536, tag + "a")
        emit_cast_weights(P, dins["qbT"], qbb, 1536, 3072, tag + "b")
        P.barrier(tag + "w")

        s_ld, s_mq, s_eq, s_ns, s_ms, s_rs, s_rp, s_qn, s_mh, s_eh, s_st = \
            (P.sem() for _ in range(11))
        xnb = [E(nc.sbuf_tensor(tag + "_xn%d" % j, [128, NC * TS], BF16)) for j in range(2)]
        qlf = E(nc.sbuf_tensor(tag + "_qlf", [128, 12 * TS], BF16))
        qsq = E(nc.sbuf_tensor(tag + "_qsq", [128, TS], BF16))
        qln = E(nc.sbuf_tensor(tag + "_qln", [128, 12 * TS], BF16))
        qot = [E(nc.sbuf_tensor(tag + "_qo%d" % j, [128, TS], BF16)) for j in range(2)]
        small = E(nc.sbuf_tensor(tag + "_small", [1, 2 * TS], F32))
        pql = [E(nc.psum_tensor(tag + "_pql%d" % j, [128, TS], F32)) for j in range(2)]
        pns = E(nc.psum_tensor(tag + "_pns", [128, TS], F32))
        prs = E(nc.psum_tensor(tag + "_prs", [128, TS], F32))
        pq = [E(nc.psum_tensor(tag + "_pq%d" % j, [128, TS], F32)) for j in range(2)]
        RSTD = small[0:1, 0:TS]
        MS = small[0:1, TS:2 * TS]
        NHM = NH + 8

        @block.sync
        def _(sync):
            for s in range(NS):
                if s >= 2:
                    sync.wait_ge(s_mh.h, (s - 1) * NHM)
                src = r3(dsc["xn3"])[:, :, s * TS:(s + 1) * TS]
                sync.dma_start(r3(xnb[s % 2].ap(), c=NC), src).then_inc(s_ld.h, 16)
                for m in range(NHM):
                    km = s * NHM + m
                    sync.wait_ge(s_eh.h, km + 1)
                    if m < NH:
                        dst = dsc["qn_sc"][:, m * T + s * TS: m * T + (s + 1) * TS]
                        sync.dma_start(dst, qot[km % 2][:]).then_inc(s_st.h, 16)
                    else:
                        mm = m - NH
                        base = (2 * mm) * T + s * TS
                        dst = bass.AP(dsc["qpe_sc"].tensor, base,
                                      [[T, 2], [NH * T, 64], [1, TS]])
                        sync.dma_start(dst, qot[km % 2][:]).then_inc(s_st.h, 16)
            sync.wait_ge(s_st.h, NS * NHM * 16)

        @block.tensor
        def _(tensor):
            for s in range(NS):
                tensor.wait_ge(s_ld.h, (s + 1) * 16)
                for m in range(12):
                    k = s * 12 + m
                    if k >= 2:
                        tensor.wait_ge(s_eq.h, 2 * (k - 1))
                    for dc in range(NC):
                        ins = tensor.matmul(pql[k % 2][:],
                                            qab[:, dc * 1536 + m * 128: dc * 1536 + (m + 1) * 128],
                                            xnb[s % 2][:, dc * TS:(dc + 1) * TS],
                                            start=(dc == 0), stop=(dc == NC - 1))
                    ins.then_inc(s_mq.h, 1)
                    tensor.wait_ge(s_eq.h, 2 * k + 2)
                    if m == 0 and s >= 1:
                        tensor.wait_ge(s_ms.h, s)
                    tensor.matmul(pns[0:1, :], P.ones_col_b[:, 0:1],
                                  qsq[:],
                                  start=(m == 0), stop=(m == 11)).then_inc(s_ns.h, 1)
                tensor.wait_ge(s_rs.h, s + 1)
                if s >= 1:
                    tensor.wait_ge(s_qn.h, s * 12)
                tensor.matmul(prs[:], P.ones_row[0:1, :], RSTD,
                              start=True, stop=True).then_inc(s_rp.h, 1)
                for m in range(NHM):
                    km = s * NHM + m
                    if km >= 2:
                        tensor.wait_ge(s_eh.h, km - 1)
                    if m == 0:
                        tensor.wait_ge(s_qn.h, (s + 1) * 12)
                    if m < NH:
                        cols0 = m * 128
                    else:
                        cols0 = 2048 + (m - NH) * 128
                    for kl in range(12):
                        ins = tensor.matmul(pq[km % 2][:],
                                            qbb[:, kl * 3072 + cols0: kl * 3072 + cols0 + 128],
                                            qln[:, kl * TS:(kl + 1) * TS],
                                            start=(kl == 0), stop=(kl == 11))
                    ins.then_inc(s_mh.h, 1)

        @block.scalar
        def _(scalar):
            for s in range(NS):
                for m in range(12):
                    k = s * 12 + m
                    scalar.wait_ge(s_mq.h, k + 1)
                    if k >= 1:
                        scalar.wait_ge(s_ns.h, k)
                    if s >= 1 and m == 0:
                        scalar.wait_ge(s_qn.h, s * 12)
                    scalar.activation(qlf[:, m * TS:(m + 1) * TS], pql[k % 2][:],
                                      AF.Copy)
                    scalar.activation(qsq[:], pql[k % 2][:], AF.Square) \
                        .then_inc(s_eq.h, 2)
                scalar.wait_ge(s_ms.h, s + 1)
                if s >= 1:
                    scalar.wait_ge(s_rp.h, s)
                scalar.activation(RSTD, MS, AF.Ln, bias=E_RMS)
                scalar.activation(RSTD, RSTD, AF.Exp, scale=-0.5).then_inc(s_rs.h, 1)
                for m in range(NHM):
                    km = s * NHM + m
                    scalar.wait_ge(s_mh.h, km + 1)
                    if km >= 2:
                        scalar.wait_ge(s_st.h, (km - 1) * 16)
                    scalar.activation(qot[km % 2][:], pq[km % 2][:], AF.Copy) \
                        .then_inc(s_eh.h, 1)

        @block.vector
        def _(vector):
            for s in range(NS):
                vector.wait_ge(s_ns.h, (s + 1) * 12)
                vector.tensor_scalar(MS, pns[0:1, :], 1.0 / QLR, None, Op.mult) \
                    .then_inc(s_ms.h, 1)
                vector.wait_ge(s_rp.h, s + 1)
                if s >= 1:
                    vector.wait_ge(s_mh.h, s * NHM)
                for m in range(12):
                    vector.scalar_tensor_tensor(qln[:, m * TS:(m + 1) * TS],
                                                qlf[:, m * TS:(m + 1) * TS],
                                                P.qa_lnw[:, m:m + 1], prs[:],
                                                Op.mult, Op.mult).then_inc(s_qn.h, 1)


def emit_attn_kv(P, dins, dsc, tag="ak"):
    block, nc = P.block, P.nc
    with ExitStack() as st:
        E = st.enter_context
        kvab = E(nc.sbuf_tensor("kvab", [128, NC * 576], BF16))
        kvbb = E(nc.sbuf_tensor("kvbb", [128, 4 * 4096], BF16))
        emit_cast_weights(P, dins["kvaT"], kvab, 2048, 576, tag + "a")
        emit_cast_weights(P, dins["kvbT"], kvbb, 512, 4096, tag + "b")
        P.barrier(tag + "w")

        s_ld, s_mc, s_ec, s_ns, s_ms, s_rs, s_rp, s_cn, s_mh, s_eh, s_rope, s_st, \
            s_rot = (P.sem() for _ in range(13))
        xnb = [E(nc.sbuf_tensor(tag + "_xn%d" % j, [128, NC * TS], BF16)) for j in range(2)]
        ropbC = E(nc.sbuf_tensor(tag + "_ropC", [64, TS], F32))
        ropbS = E(nc.sbuf_tensor(tag + "_ropS", [64, TS], F32))
        cf = E(nc.sbuf_tensor(tag + "_cf", [128, 4 * TS], BF16))
        csq = E(nc.sbuf_tensor(tag + "_csq", [128, TS], BF16))
        cnb = E(nc.sbuf_tensor(tag + "_cnb", [128, 4 * TS], BF16))
        kpf = E(nc.sbuf_tensor(tag + "_kpf", [64, TS], BF16))
        rt1 = E(nc.sbuf_tensor(tag + "_rt1", [64, TS], F32))
        rt2 = E(nc.sbuf_tensor(tag + "_rt2", [64, TS], F32))
        prot = E(nc.psum_tensor(tag + "_prot", [64, TS], F32))
        kpo = E(nc.sbuf_tensor(tag + "_kpo", [64, TS], BF16))
        kot = [E(nc.sbuf_tensor(tag + "_ko%d" % j, [128, TS], BF16)) for j in range(2)]
        small = E(nc.sbuf_tensor(tag + "_small", [1, 2 * TS], F32))
        pc = [E(nc.psum_tensor(tag + "_pc%d" % j, [128, TS], F32)) for j in range(2)]
        pns = E(nc.psum_tensor(tag + "_pns", [128, TS], F32))
        prs = E(nc.psum_tensor(tag + "_prs", [128, TS], F32))
        pk = [E(nc.psum_tensor(tag + "_pk%d" % j, [128, TS], F32)) for j in range(2)]
        RSTD = small[0:1, 0:TS]
        MS = small[0:1, TS:2 * TS]
        NKM = NH + 16

        @block.sync
        def _(sync):
            for s in range(NS):
                if s >= 2:
                    sync.wait_ge(s_mh.h, (s - 1) * NKM)
                src = r3(dsc["xn3"])[:, :, s * TS:(s + 1) * TS]
                sync.dma_start(r3(xnb[s % 2].ap(), c=NC), src).then_inc(s_ld.h, 32)
                sync.dma_start(ropbC[:], dins["ropeC"][:, s * TS:(s + 1) * TS]) \
                    .then_inc(s_ld.h, 32)
                sync.dma_start(ropbS[:], dins["ropeS"][:, s * TS:(s + 1) * TS]) \
                    .then_inc(s_ld.h, 32)
                for m in range(NKM):
                    km = s * NKM + m
                    sync.wait_ge(s_eh.h, km + 1)
                    if m < NH:
                        dst = dsc["kn_sc"][:, m * T + s * TS: m * T + (s + 1) * TS]
                        sync.dma_start(dst, kot[km % 2][:]).then_inc(s_st.h, 16)
                    else:
                        mv = m - NH
                        tb, vc = mv // 4, mv % 4
                        base = (4 * vc) * T + 128 * (4 * s + tb)
                        dst = bass.AP(dsc["v_sc"].tensor, base,
                                      [[NH * T, 128], [T, 4], [1, 128]])
                        sync.dma_start(dst, kot[km % 2][:]).then_inc(s_st.h, 16)
                sync.wait_ge(s_rope.h, s + 1)
                sync.dma_start(dsc["kpe_sc"][:, s * TS:(s + 1) * TS], kpo[0:64, :]) \
                    .then_inc(s_st.h, 16)
            sync.wait_ge(s_st.h, NS * (NKM + 1) * 16)

        @block.tensor
        def _(tensor):
            for s in range(NS):
                tensor.wait_ge(s_ld.h, (s + 1) * 96)
                for m in range(5):
                    k = s * 5 + m
                    if k >= 2:
                        tensor.wait_ge(s_ec.h, 2 * (k - 1))
                    rows = 128 if m < 4 else 64
                    for dc in range(NC):
                        ins = tensor.matmul(pc[k % 2][0:rows, :],
                                            kvab[:, dc * 576 + m * 128:
                                                 dc * 576 + m * 128 + rows],
                                            xnb[s % 2][:, dc * TS:(dc + 1) * TS],
                                            start=(dc == 0), stop=(dc == NC - 1))
                    ins.then_inc(s_mc.h, 1)
                    if m < 4:
                        tensor.wait_ge(s_ec.h, 2 * k + 2)
                        if m == 0 and s >= 1:
                            tensor.wait_ge(s_ms.h, s)
                        tensor.matmul(pns[0:1, :], P.ones_col_b[:, 0:1],
                                      csq[:],
                                      start=(m == 0), stop=(m == 3)).then_inc(s_ns.h, 1)
                tensor.wait_ge(s_ec.h, s * 10 + 10)
                if s >= 1:
                    tensor.wait_ge(s_rope.h, s)
                tensor.matmul(prot[0:64, :], P.rot64[0:64, :], kpf[0:64, :],
                              start=True, stop=True).then_inc(s_rot.h, 1)
                tensor.wait_ge(s_rs.h, s + 1)
                if s >= 1:
                    tensor.wait_ge(s_cn.h, s * 4)
                tensor.matmul(prs[:], P.ones_row[0:1, :], RSTD,
                              start=True, stop=True).then_inc(s_rp.h, 1)
                for m in range(NKM):
                    km = s * NKM + m
                    if km >= 2:
                        tensor.wait_ge(s_eh.h, km - 1)
                    if m == 0:
                        tensor.wait_ge(s_cn.h, (s + 1) * 4)
                    if m < NH:
                        for kl in range(4):
                            ins = tensor.matmul(pk[km % 2][:],
                                          kvbb[:, kl * 4096 + m * 128:
                                               kl * 4096 + (m + 1) * 128],
                                          cnb[:, kl * TS:(kl + 1) * TS],
                                          start=(kl == 0), stop=(kl == 3))
                    else:
                        mv = m - NH
                        tb, vc = mv // 4, mv % 4
                        for kl in range(4):
                            ins = tensor.matmul(pk[km % 2][:],
                                          cnb[:, kl * TS + tb * 128: kl * TS + (tb + 1) * 128],
                                          kvbb[:, kl * 4096 + 2048 + vc * TS:
                                               kl * 4096 + 2048 + (vc + 1) * TS],
                                          start=(kl == 0), stop=(kl == 3))
                    ins.then_inc(s_mh.h, 1)

        @block.scalar
        def _(scalar):
            for s in range(NS):
                for m in range(5):
                    k = s * 5 + m
                    scalar.wait_ge(s_mc.h, k + 1)
                    if m < 4:
                        if s * 4 + m >= 1:
                            scalar.wait_ge(s_ns.h, s * 4 + m)
                        if s >= 1 and m == 0:
                            scalar.wait_ge(s_cn.h, s * 4)
                        scalar.activation(cf[:, m * TS:(m + 1) * TS], pc[k % 2][:],
                                          AF.Copy)
                        scalar.activation(csq[:], pc[k % 2][:], AF.Square) \
                            .then_inc(s_ec.h, 2)
                    else:
                        scalar.activation(kpf[0:64, :], pc[k % 2][0:64, :], AF.Copy) \
                            .then_inc(s_ec.h, 2)
                scalar.wait_ge(s_ms.h, s + 1)
                if s >= 1:
                    scalar.wait_ge(s_rp.h, s)
                scalar.activation(RSTD, MS, AF.Ln, bias=E_RMS)
                scalar.activation(RSTD, RSTD, AF.Exp, scale=-0.5).then_inc(s_rs.h, 1)
                for m in range(NKM):
                    km = s * NKM + m
                    scalar.wait_ge(s_mh.h, km + 1)
                    if km >= 2:
                        scalar.wait_ge(s_st.h, (km - 1) * 16)
                    scalar.activation(kot[km % 2][:], pk[km % 2][:], AF.Copy) \
                        .then_inc(s_eh.h, 1)

        @block.vector
        def _(vector):
            for s in range(NS):
                vector.wait_ge(s_ns.h, (s + 1) * 4)
                vector.tensor_scalar(MS, pns[0:1, :], 1.0 / KVLR, None, Op.mult) \
                    .then_inc(s_ms.h, 1)
                vector.wait_ge(s_rp.h, s + 1)
                if s >= 1:
                    vector.wait_ge(s_mh.h, s * NKM)
                for m in range(4):
                    vector.scalar_tensor_tensor(cnb[:, m * TS:(m + 1) * TS],
                                                cf[:, m * TS:(m + 1) * TS],
                                                P.kva_lnw[:, m:m + 1], prs[:],
                                                Op.mult, Op.mult).then_inc(s_cn.h, 1)
                vector.wait_ge(s_rot.h, s + 1)
                if s >= 1:
                    vector.wait_ge(s_st.h, s * (NKM + 1) * 16)
                vector.tensor_tensor(rt1[0:64, :], kpf[0:64, :], ropbC[0:64, :], Op.mult)
                vector.tensor_tensor(rt2[0:64, :], prot[0:64, :], ropbS[0:64, :], Op.mult)
                vector.tensor_tensor(kpo[0:64, :], rt1[0:64, :], rt2[0:64, :], Op.add) \
                    .then_inc(s_rope.h, 1)


def _ab_schedule():
    sched = []
    for qb in range(NS):
        if qb == 0:
            order = [0, 1, 2, 3]
        else:
            order = [0] + [4 * qb + jj for jj in range(4)] + list(range(1, 4 * qb))
        for i, kc in enumerate(order):
            qv0 = max(0, kc - 4 * qb) * 128
            sched.append((qb, kc, qv0, i == len(order) - 1))
    return sched, len(sched)


def emit_attn_core(P, dins, dsc, tag="ab"):
    block, nc = P.block, P.nc
    sched, SC_H = _ab_schedule()
    with ExitStack() as st:
        E = st.enter_context
        s_ld = P.sem("abld")
        s_rope = P.sem("abrope")
        s_rot = P.sem("abrot")
        s_rte = P.sem("abrte")
        s_sc = P.sem("absc")
        s_ex = P.sem("abex")
        s_mk = P.sem("abmk")
        s_mz = P.sem("abmz")
        s_av = P.sem("abav")
        s_dn = P.sem("abdn")
        s_dc = P.sem("abdc")
        s_pd = P.sem("abpd")
        s_rc = P.sem("abrc")
        s_rp = P.sem("abrp")
        s_rs = P.sem("abrs")
        s_no = P.sem("abno")
        s_st = P.sem("abst")

        qn = [E(nc.sbuf_tensor("ab_qn%d" % j, [128, T], BF16)) for j in range(2)]
        qpe = [E(nc.sbuf_tensor("ab_qpe%d" % j, [64, T], BF16)) for j in range(2)]
        qpr = [E(nc.sbuf_tensor("ab_qpr%d" % j, [64, T], BF16)) for j in range(2)]
        rotb = E(nc.sbuf_tensor("ab_rotb", [64, T], BF16))
        kn = [E(nc.sbuf_tensor("ab_kn%d" % j, [128, T], BF16)) for j in range(2)]
        vb = [E(nc.sbuf_tensor("ab_vb%d" % j, [128, T], BF16)) for j in range(2)]
        kpe = E(nc.sbuf_tensor("ab_kpe", [64, T], BF16))
        ropC = E(nc.sbuf_tensor("ab_ropC", [64, T], F32))
        ropS = E(nc.sbuf_tensor("ab_ropS", [64, T], F32))
        rr1 = E(nc.sbuf_tensor("ab_rr1", [64, T], BF16))
        rr2 = E(nc.sbuf_tensor("ab_rr2", [64, T], BF16))
        pt = [E(nc.sbuf_tensor("ab_pt%d" % j, [128, TS], BF16)) for j in range(6)]
        den = [E(nc.sbuf_tensor("ab_den%d" % j, [128, TS], F32)) for j in range(2)]
        denr = E(nc.sbuf_tensor("ab_denr", [128, TS], F32R))
        rcp = E(nc.sbuf_tensor("ab_rcp", [8, TS], F32))
        rcpr = E(nc.sbuf_tensor("ab_rcpr", [8, TS], F32R))
        rS = [E(nc.sbuf_tensor("ab_rs%d" % j, [128, TS], F32)) for j in range(2)]
        ato = [E(nc.sbuf_tensor("ab_ato%d" % j, [128, T], BF16)) for j in range(2)]
        psc = [E(nc.psum_tensor("ab_psc%d" % j, [128, TS], F32)) for j in range(5)]
        pav = [E(nc.psum_tensor("ab_pav%d" % j, [128, TS], F32)) for j in range(2)]
        pdn = E(nc.psum_tensor("ab_pdn", [128, TS], F32))
        pden = pdn
        prep = pdn
        protq = psc[4]

        @block.sync
        def _(sync):
            sync.dma_start(ropC[:], dins["ropeC"]).then_inc(s_ld.h, 16)
            sync.dma_start(ropS[:], dins["ropeS"]).then_inc(s_ld.h, 16)
            sync.dma_start(kpe[:], dsc["kpe_sc"]).then_inc(s_ld.h, 16)
            for h in range(NH):
                if h >= 2:
                    sync.wait_ge(s_sc.h, (h - 1) * SC_H)
                    sync.wait_ge(s_av.h, (h - 1) * SC_H)
                    sync.wait_ge(s_rope.h, h - 1)
                    sync.wait_ge(s_rot.h, (h - 1) * 8)
                sync.dma_start(qn[h % 2][:], dsc["qn_sc"][:, h * T:(h + 1) * T]) \
                    .then_inc(s_ld.h, 16)
                sync.dma_start(qpe[h % 2][:], dsc["qpe_sc"][:, h * T:(h + 1) * T]) \
                    .then_inc(s_ld.h, 16)
                sync.dma_start(kn[h % 2][:], dsc["kn_sc"][:, h * T:(h + 1) * T]) \
                    .then_inc(s_ld.h, 16)
                sync.dma_start(vb[h % 2][:], dsc["v_sc"][:, h * T:(h + 1) * T]) \
                    .then_inc(s_ld.h, 16)
                if h >= 1:
                    sync.wait_ge(s_no.h, h * NS)
                    sync.dma_start(dsc["atoT_sc"][:, (h - 1) * T: h * T],
                                   ato[(h - 1) % 2][:]).then_inc(s_st.h, 16)
            sync.wait_ge(s_no.h, NH * NS)
            sync.dma_start(dsc["atoT_sc"][:, (NH - 1) * T: NH * T],
                           ato[(NH - 1) % 2][:]).then_inc(s_st.h, 16)
            sync.wait_ge(s_st.h, NH * 16)

        @block.gpsimd
        def _(gpsimd):
            for h in range(NH):
                gpsimd.wait_ge(s_ld.h, 48 + (h + 1) * 64)
                gpsimd.wait_ge(s_rte.h, 8 * (h + 1))
                if h >= 2:
                    gpsimd.wait_ge(s_sc.h, (h - 1) * SC_H)
                gpsimd.tensor_tensor(rr1[0:64, :], qpe[h % 2][0:64, :], ropC[0:64, :],
                                     Op.mult)
                gpsimd.tensor_tensor(rr2[0:64, :], rotb[0:64, :], ropS[0:64, :],
                                     Op.mult)
                gpsimd.tensor_tensor(qpr[h % 2][0:64, :], rr1[0:64, :], rr2[0:64, :],
                                     Op.add).then_inc(s_rope.h, 1)
                for kc in (1, 2, 3):
                    k = h * SC_H + kc
                    if k >= 6:
                        gpsimd.wait_ge(s_av.h, k - 5)
                        gpsimd.wait_ge(s_dn.h, k - 5)
                    gpsimd.memset(pt[k % 6][:, 0:kc * 128], 0.0).then_inc(s_mz.h, 1)

        @block.tensor
        def _(tensor):
            G = 4
            NG = SC_H // G

            def emit_av_group(gi, h):
                kg = h * SC_H + gi * G
                tensor.wait_ge(s_ex.h, kg + G)
                nmask = sum(1 for (qb, kc, _, _) in sched[:gi * G + G]
                            if kc >= 4 * qb)
                if nmask:
                    tensor.wait_ge(s_mk.h, h * 32 + nmask)
                mz = [kc for (qb, kc, _, _) in sched[gi * G: gi * G + G]
                      if qb == 0 and kc >= 1]
                if mz:
                    tensor.wait_ge(s_mz.h, h * 3 + max(mz))
                for ci_av in range(gi * G, gi * G + G):
                    qb, kc, qv0, last = sched[ci_av]
                    k = h * SC_H + ci_av
                    gq = h * NS + qb
                    av0 = 0 if qb == 0 else qv0
                    if kc == 0 and gq >= 2:
                        tensor.wait_ge(s_no.h, gq - 1)
                    tensor.matmul(pav[qb % 2][:, av0:TS],
                                  vb[h % 2][:, kc * 128:(kc + 1) * 128],
                                  pt[k % 6][:, av0:TS],
                                  start=(kc == 0), stop=last).then_inc(s_av.h, 1)
                    if last:
                        tensor.wait_ge(s_dc.h, gq + 1)
                        if gq >= 1:
                            tensor.wait_ge(s_rs.h, gq)
                        tensor.matmul(pden[0:8, :], P.ones8_r[:, 0:8], denr[:],
                                      start=True, stop=True).then_inc(s_pd.h, 1)
                        tensor.wait_ge(s_rc.h, gq + 1)
                        tensor.matmul(prep[:], P.ones_row_r[0:1, :], rcpr[0:1, :],
                                      start=True, stop=True).then_inc(s_rp.h, 1)

            for h in range(NH):
                tensor.wait_ge(s_ld.h, 48 + (h + 1) * 64)
                if h >= 1:
                    tensor.wait_ge(s_ex.h, h * SC_H)
                for kq in range(8):
                    kk = h * 8 + kq
                    if kk >= 1:
                        tensor.wait_ge(s_rte.h, kk)
                    tensor.matmul(protq[0:64, :], P.rot64[0:64, :],
                                  qpe[h % 2][:, kq * TS:(kq + 1) * TS],
                                  start=True, stop=True).then_inc(s_rot.h, 1)
                tensor.wait_ge(s_rope.h, h + 1)
                for gi in range(NG):
                    kg = h * SC_H + gi * G
                    if kg >= 1:
                        tensor.wait_ge(s_ex.h, kg if gi == 0 else kg - 1)
                    if gi == 1:
                        tensor.wait_ge(s_rte.h, 8 * (h + 1))
                    for ci in range(gi * G, gi * G + G):
                        qb, kc, qv0, last = sched[ci]
                        q0 = qb * TS
                        tensor.matmul(psc[ci % 5][:, qv0:TS],
                                      kn[h % 2][:, kc * 128:(kc + 1) * 128],
                                      qn[h % 2][:, q0 + qv0: q0 + TS],
                                      start=True, stop=False)
                        tensor.matmul(psc[ci % 5][:, qv0:TS],
                                      kpe[:, kc * 128:(kc + 1) * 128],
                                      qpr[h % 2][:, q0 + qv0: q0 + TS],
                                      start=False, stop=True).then_inc(s_sc.h, 1)
                    if gi >= 1:
                        emit_av_group(gi - 1, h)
                emit_av_group(NG - 1, h)

        @block.scalar
        def _(scalar):
            for h in range(NH):
                for ci, (qb, kc, qv0, last) in enumerate(sched):
                    k = h * SC_H + ci
                    gq = h * NS + qb
                    scalar.wait_ge(s_sc.h, k + 1)
                    if k >= 6:
                        scalar.wait_ge(s_av.h, k - 5)
                        scalar.wait_ge(s_dn.h, k - 5)
                    scalar.activation(pt[k % 6][:, qv0:TS], psc[ci % 5][:, qv0:TS],
                                      AF.Exp, bias=SM_SHIFT, scale=INV_SQ) \
                        .then_inc(s_ex.h, 1)
                    if last:
                        scalar.wait_ge(s_rp.h, gq + 1)
                        if gq >= 2:
                            scalar.wait_ge(s_no.h, gq - 1)
                        scalar.activation(rS[qb % 2][:], prep[:], AF.Copy) \
                            .then_inc(s_rs.h, 1)

        @block.vector
        def _(vector):
            for h in range(NH):
                for kq in range(8):
                    kk = h * 8 + kq
                    vector.wait_ge(s_rot.h, kk + 1)
                    if kq == 0 and h >= 1:
                        vector.wait_ge(s_rope.h, h)
                    vector.tensor_copy(rotb[:, kq * TS:(kq + 1) * TS], protq[0:64, :]) \
                        .then_inc(s_rte.h, 1)
                for ci, (qb, kc, qv0, last) in enumerate(sched):
                    k = h * SC_H + ci
                    gq = h * NS + qb
                    vector.wait_ge(s_ex.h, k + 1)
                    if kc >= 4 * qb:
                        jj = kc - 4 * qb
                        vector.tensor_tensor(pt[k % 6][:, jj * 128:(jj + 1) * 128],
                                             pt[k % 6][:, jj * 128:(jj + 1) * 128],
                                             P.tril_b[:], Op.mult).then_inc(s_mk.h, 1)
                    if kc == 0:
                        vector.tensor_copy(den[qb % 2][:], pt[k % 6][:, 0:TS]) \
                            .then_inc(s_dn.h, 1)
                    else:
                        vector.tensor_tensor(den[qb % 2][:, qv0:TS],
                                             den[qb % 2][:, qv0:TS],
                                             pt[k % 6][:, qv0:TS], Op.add) \
                            .then_inc(s_dn.h, 1)
                    if last:
                        if gq >= 1:
                            vector.wait_ge(s_pd.h, gq)
                        vector.tensor_copy(denr[:], den[qb % 2][:]).then_inc(s_dc.h, 1)
                        vector.wait_ge(s_pd.h, gq + 1)
                        if gq >= 1:
                            vector.wait_ge(s_rp.h, gq)
                        vector.reciprocal(rcp[0:8, :], pden[0:8, :])
                        vector.tensor_copy(rcpr[0:8, :], rcp[0:8, :]).then_inc(s_rc.h, 1)
                        vector.wait_ge(s_av.h, k + 1)
                        vector.wait_ge(s_rs.h, gq + 1)
                        if h >= 2 and qb == 0:
                            vector.wait_ge(s_st.h, (h - 1) * 16)
                        vector.tensor_tensor(ato[h % 2][:, qb * TS:(qb + 1) * TS],
                                             pav[qb % 2][:], rS[qb % 2][:], Op.mult) \
                            .then_inc(s_no.h, 1)


def emit_oproj(P, dins, dsc, tag="ao"):
    block, nc = P.block, P.nc
    with ExitStack() as st:
        E = st.enter_context
        owb = E(nc.sbuf_tensor("owb", [128, NC * 2048], BF16))
        emit_cast_weights(P, dins["owT"], owb, 2048, 2048, tag)
        P.barrier(tag + "w")
        s_ld, s_mo, s_ev, s_st = (P.sem() for _ in range(4))
        atb = [E(nc.sbuf_tensor(tag + "_at%d" % j, [128, NH * TS], BF16)) for j in range(2)]
        rbuf = [E(nc.sbuf_tensor(tag + "_r%d" % j, [128, NC * TS], F32)) for j in range(2)]
        ot = [E(nc.sbuf_tensor(tag + "_ot%d" % j, [128, TS], F32)) for j in range(2)]
        po = [E(nc.psum_tensor(tag + "_po%d" % j, [128, TS], F32)) for j in range(2)]

        @block.sync
        def _(sync):
            for s in range(NS):
                if s >= 2:
                    sync.wait_ge(s_mo.h, (s - 1) * NC)
                    sync.wait_ge(s_ev.h, (s - 1) * NC)
                for h in range(NH):
                    sync.dma_start(atb[s % 2][:, h * TS:(h + 1) * TS],
                                   dsc["atoT_sc"][:, h * T + s * TS: h * T + (s + 1) * TS]) \
                        .then_inc(s_ld.h, 16)
                src2 = r3(dsc["resid2"])[:, :, s * TS:(s + 1) * TS]
                sync.dma_start(r3(rbuf[s % 2].ap(), c=NC), src2).then_inc(s_ld.h, 16)
                for oc in range(NC):
                    ko = s * NC + oc
                    sync.wait_ge(s_ev.h, ko + 1)
                    dst = dsc["resid3"][:, oc * T + s * TS: oc * T + (s + 1) * TS]
                    sync.dma_start(dst, ot[ko % 2][:]).then_inc(s_st.h, 16)
            sync.wait_ge(s_st.h, NS * NC * 16)

        @block.tensor
        def _(tensor):
            for s in range(NS):
                tensor.wait_ge(s_ld.h, (s + 1) * 17 * 16)
                for oc in range(NC):
                    ko = s * NC + oc
                    if ko >= 2:
                        tensor.wait_ge(s_ev.h, ko - 1)
                    for h in range(NH):
                        ins = tensor.matmul(po[ko % 2][:],
                                            owb[:, h * 2048 + oc * 128: h * 2048 + (oc + 1) * 128],
                                            atb[s % 2][:, h * TS:(h + 1) * TS],
                                            start=(h == 0), stop=(h == NH - 1))
                    ins.then_inc(s_mo.h, 1)

        @block.vector
        def _(vector):
            for s in range(NS):
                for oc in range(NC):
                    ko = s * NC + oc
                    vector.wait_ge(s_mo.h, ko + 1)
                    if ko >= 2:
                        vector.wait_ge(s_st.h, (ko - 1) * 16)
                    vector.tensor_tensor(ot[ko % 2][:], po[ko % 2][:],
                                         rbuf[s % 2][:, oc * TS:(oc + 1) * TS], Op.add) \
                        .then_inc(s_ev.h, 1)


def build_program(phases=ALL_PHASES, debug=False, race_detect=False):
    nc = bass.Bass("TRN2", target_bir_lowering=False,
                   detect_race_conditions=race_detect)
    dins = {}
    for k, shp in GLOBAL_INPUTS.items():
        dins[k] = nc.dram_tensor(k, list(shp), F32, kind="ExternalInput").ap()
    for k, shp in GLOBAL_INPUTS_BF.items():
        dins[k] = nc.dram_tensor(k, list(shp), BF16, kind="ExternalInput").ap()
    dins["x0"] = nc.dram_tensor("x0", [T, D], F32, kind="ExternalInput").ap()
    out = nc.dram_tensor("out", [T, D], F32, kind="ExternalOutput").ap()
    skind = "ExternalOutput" if debug else "Internal"
    dsc = {}
    for k, shp in SCRATCH.items():
        dsc[k] = nc.dram_tensor(k, list(shp), F32, kind=skind).ap()
    for k, shp in SCRATCH_R.items():
        dsc[k] = nc.dram_tensor(k, list(shp), F32R, kind="Internal").ap()
    if FFN_DBG:
        dsc["dbg_lgt"] = nc.dram_tensor("dbg_lgt", [128, NS * 32], F32, kind="ExternalOutput").ap()
        dsc["dbg_wv"] = nc.dram_tensor("dbg_wv", [128, NS * 32], F32, kind="ExternalOutput").ap()
        dsc["dbg_wT"] = nc.dram_tensor("dbg_wT", [8, NS * TS], F32, kind="ExternalOutput").ap()
        dsc["dbg_zt"] = nc.dram_tensor("dbg_zt", [128, NS * TS], BF16, kind="ExternalOutput").ap()
        dsc["dbg_xat"] = nc.dram_tensor("dbg_xat", [128, NS * TS], BF16, kind="ExternalOutput").ap()
    for k, shp in SCRATCH_BF.items():
        dsc[k] = nc.dram_tensor(k, list(shp), BF16, kind=skind).ap()

    with ExitStack() as stack:
        block = stack.enter_context(nc.Block())
        P = Prog(nc, stack, block)
        P.dsc = dsc
        P.ident_f = stack.enter_context(nc.sbuf_tensor("ident_f", [128, 128], F32))
        P.ident_b = stack.enter_context(nc.sbuf_tensor("ident_b", [128, 128], BF16))
        P.ones_row = stack.enter_context(nc.sbuf_tensor("ones_rowS", [1, 128], F32))
        P.ones_col_b = stack.enter_context(nc.sbuf_tensor("ones_col_bS", [128, 1], BF16))
        P.ones_col_f = stack.enter_context(nc.sbuf_tensor("ones_col_fS", [128, 1], F32))
        P.ones_col_r = stack.enter_context(nc.sbuf_tensor("ones_col_rS", [128, 1], F32R))
        P.ones_row_r = stack.enter_context(nc.sbuf_tensor("ones_row_rS", [1, 128], F32R))
        P.ones8_f = stack.enter_context(nc.sbuf_tensor("ones8_fS", [128, 8], F32))
        P.ones8_r = stack.enter_context(nc.sbuf_tensor("ones8_rS", [128, 8], F32R))
        P.sel_replr = stack.enter_context(nc.sbuf_tensor("sel_replrS", [128, 2048], F32R))
        P.sel16 = stack.enter_context(nc.sbuf_tensor("sel16S", [16, 128], F32))
        P.sel_er = stack.enter_context(nc.sbuf_tensor("sel_erS", [8, 128], F32))
        P.tsel_b = stack.enter_context(nc.sbuf_tensor("tsel_bS", [128, 2048], BF16))
        P.tril_b = stack.enter_context(nc.sbuf_tensor("tril_bS", [128, 128], BF16))
        P.a_cols = stack.enter_context(nc.sbuf_tensor("a_colsS", [128, 256], F32))
        P.d_cols = stack.enter_context(nc.sbuf_tensor("d_colsS", [128, 16], F32))
        P.qa_lnw = stack.enter_context(nc.sbuf_tensor("qa_lnwS", [128, 12], F32))
        P.kva_lnw = stack.enter_context(nc.sbuf_tensor("kva_lnwS", [128, 4], F32))
        P.fb_cols = stack.enter_context(nc.sbuf_tensor("fb_colsS", [128, 16], F32))
        P.gate_b = stack.enter_context(nc.sbuf_tensor("gate_bS", [8, 1], F32))
        P.rot64 = stack.enter_context(nc.sbuf_tensor("rot64S", [64, 64], BF16))
        P.fconsts = stack.enter_context(nc.sbuf_tensor("fconstsS", [128, 8], F32))
        for i, v in enumerate([E_LN, E_RMS, SM_SHIFT, INV_SQ, -0.5, -1.0, 1.0, 0.0]):
            nc.const_aps.register(F32, float(v), P.fconsts[:, i:i + 1])

        sd = P.sem("dma0")
        consts = [("ident_f", P.ident_f, "ident_f32"), ("ident_b", P.ident_b, "ident_bf"),
                  ("ones_row", P.ones_row, "ones_row"), ("ones_col_b", P.ones_col_b, "ones_col_bf"),
                  ("ones_col_f", P.ones_col_f, "ones_col_f"),
                  ("ones8_f", P.ones8_f, "ones8_f"),
                  ("sel16", P.sel16, "sel16"),
                  ("sel_er", P.sel_er, "sel_er"), ("tsel_b", P.tsel_b, "tsel_bf"),
                  ("tril_b", P.tril_b, "tril_bf"), ("a_cols", P.a_cols, "a_log_cols"),
                  ("d_cols", P.d_cols, "d_cols"), ("qa_lnw", P.qa_lnw, "qa_lnw_cols"),
                  ("kva_lnw", P.kva_lnw, "kva_lnw_cols"), ("fb_cols", P.fb_cols, "fb_cols"),
                  ("gate_b", P.gate_b, "gate_b_col"), ("rot64", P.rot64, "rot64_bf"),
                  ("fconsts", P.fconsts, "fconsts")]

        @block.sync
        def _(sync):
            for _, sb, name in consts:
                sync.dma_start(sb[:], dins[name]).then_inc(sd.h, 16)

        nload = 16 * len(consts)

        @block.scalar
        def _(scalar):
            scalar.wait_ge(sd.h, nload)
            scalar.activation(P.a_cols[:], P.a_cols[:], AF.Exp)

        @block.vector
        def _(vector):
            vector.wait_ge(sd.h, nload)

        P.barrier("c0")

        with nc.sbuf_tensor("sel_replT", [128, 2048], F32) as sel_tmp:
            @block.sync
            def _(sync):
                sync.dma_start(sel_tmp[:], dins["sel_repl"]).then_inc(sd.h, 16)

            @block.vector
            def _(vector):
                vector.wait_ge(sd.h, nload + 16)
                vector.tensor_scalar(P.a_cols[:], P.a_cols[:], -1.0, None, Op.mult)
                vector.tensor_copy(P.sel_replr[:], sel_tmp[:])
                vector.tensor_copy(P.ones_col_r[:], P.ones_col_f[:])
                vector.tensor_copy(P.ones_row_r[0:1, :], P.ones_row[0:1, :])
                vector.tensor_copy(P.ones8_r[:], P.ones8_f[:])

            P.recycle()

        if "P0" in phases:
            emit_transpose_in(P, dins["x0"], dsc["resid0"])
            P.recycle()
        if "LN1" in phases:
            emit_ln(P, dsc["resid0"], dsc["xn1"], "ln1")
            P.recycle()
        with ExitStack() as mscope2:
            if "M2" in phases:
                P.bpat = mscope2.enter_context(nc.sbuf_tensor("bpatS", [128, T], F32))
                P.cpat = mscope2.enter_context(nc.sbuf_tensor("cpatS", [128, T], F32))
            with ExitStack() as mscope1:
                if "M1" in phases:
                    P.btT = mscope1.enter_context(nc.sbuf_tensor("btT", [16, T], F32))
                    P.ctT = mscope1.enter_context(nc.sbuf_tensor("ctT", [16, T], F32))
                    emit_xproj(P, dins, dsc)
                    P.recycle()
                if "M2" in phases:
                    emit_bcpat(P)
                    P.recycle()
            if "M3" in phases:
                emit_scan(P, dsc)
                P.recycle()
        if "LN2" in phases:
            emit_ln(P, dsc["resid1"], dsc["xn2"], "ln2")
            P.recycle()
        if "FFN1" in phases:
            emit_ffn(P, dins, dsc["xn2"], dsc["resid1"], dsc["resid2"], "f1")
            P.recycle()
        if "LN3" in phases:
            emit_ln(P, dsc["resid2"], dsc["xn3"], "ln3")
            P.recycle()
        if "AQ" in phases:
            emit_attn_q(P, dins, dsc)
            P.recycle()
        if "AKV" in phases:
            emit_attn_kv(P, dins, dsc)
            P.recycle()
        if "AB" in phases:
            emit_attn_core(P, dins, dsc)
            P.recycle()
        if "AO" in phases:
            emit_oproj(P, dins, dsc)
            P.recycle()
        if "LN4" in phases:
            emit_ln(P, dsc["resid3"], dsc["xn4"], "ln4")
            P.recycle()
        if "FFN2" in phases:
            emit_ffn(P, dins, dsc["xn4"], dsc["resid3"], dsc["resid4"], "f2")
            P.recycle()
        if "OUT" in phases:
            src = dsc["resid4"] if "FFN2" in phases else dsc["resid0"]
            emit_transpose_out(P, src, out)
            P.recycle()

    return nc


def kernel(**inputs):
    g = host_prepare(inputs)
    x = np.asarray(inputs["L0_attn_a_x"], np.float32)
    nc = build_program()
    in_maps = []
    for b in range(8):
        m = dict(g)
        m["x0"] = np.ascontiguousarray(x[b])
        in_maps.append(m)
    res = run_bass_kernel_spmd(nc, in_maps, core_ids=list(range(8)))
    outs = [np.asarray(res.results[b]["out"], np.float32) for b in range(8)]
    return np.stack(outs, 0)


# version 10 (final)
# speedup vs baseline: 1.0493x; 1.0493x over previous
import numpy as np
import ml_dtypes
from contextlib import ExitStack

import concourse.bass as bass
import concourse.mybir as mybir
from concourse.bass_utils import run_bass_kernel_spmd
from concourse.alu_op_type import AluOpType as Op

F32 = mybir.dt.float32
F32R = mybir.dt.float32r
BF16 = mybir.dt.bfloat16
AF = mybir.ActivationFunctionType
AX = mybir.AxisListType

D = 2048
T = 4096
NC = 16
NS = 8
TS = 512
NH = 16
QLR = 1536
KVLR = 512
DN = 128
DR = 64
DV = 128
QHD = DN + DR
E_LN = 1e-5
E_RMS = 1e-6
INV_SQ = float(1.0 / np.sqrt(float(QHD)))
SM_SHIFT = -30.0

ALL_PHASES = ("P0", "LN1", "M1", "M2", "M3", "LN2", "FFN1", "LN3",
              "AQ", "AKV", "AB", "AO", "LN4", "FFN2", "OUT")


def _bf(x):
    return np.ascontiguousarray(np.asarray(x).astype(ml_dtypes.bfloat16))


class Sem:
    def __init__(self, h):
        self.h = h


class Prog:

    ENGINES = ("sync", "vector", "scalar", "tensor", "gpsimd")

    def __init__(self, nc, stack, block, npool=64):
        self.nc = nc
        self.stack = stack
        self.block = block
        self._pool = [stack.enter_context(nc.semaphore("pool%d" % i))
                      for i in range(npool)]
        self._free = list(self._pool)
        self._used = []
        self._bar = stack.enter_context(nc.semaphore("barrier"))
        self._bar_n = 0
        self._vdum = stack.enter_context(nc.sbuf_tensor("bar_vdum", [1, 2], F32))
        self._sdum = stack.enter_context(nc.sbuf_tensor("bar_sdum", [1, 2], F32))
        self._gdum = stack.enter_context(nc.sbuf_tensor("bar_gdum", [1, 2], F32))

        @block.vector
        def _(vector):
            vector.memset(self._vdum[:], 0.0)

        @block.scalar
        def _(scalar):
            scalar.memzero(self._sdum[:])

        @block.gpsimd
        def _(gpsimd):
            gpsimd.memset(self._gdum[:], 0.0)

    def sem(self, name=None):
        h = self._free.pop()
        self._used.append(h)
        return Sem(h)

    def barrier(self, tag=None):
        n = len(self.ENGINES)
        tgt = self._bar_n + n

        @self.block.sync
        def _(sync):
            sync.nop().then_inc(self._bar, 1)
            sync.wait_ge(self._bar, tgt)

        @self.block.tensor
        def _(tensor):
            tensor.nop().then_inc(self._bar, 1)
            tensor.wait_ge(self._bar, tgt)

        @self.block.vector
        def _(vector):
            vector.memset(self._vdum[0:1, 0:1], 0.0).then_inc(self._bar, 1)
            vector.wait_ge(self._bar, tgt)

        @self.block.scalar
        def _(scalar):
            scalar.activation(self._sdum[0:1, 0:1], self._sdum[0:1, 1:2],
                              AF.Copy).then_inc(self._bar, 1)
            scalar.wait_ge(self._bar, tgt)

        @self.block.gpsimd
        def _(gpsimd):
            gpsimd.memset(self._gdum[0:1, 0:1], 0.0).then_inc(self._bar, 1)
            gpsimd.wait_ge(self._bar, tgt)

        self._bar_n = tgt

    def recycle(self):
        self.barrier()
        used = self._used
        if used:
            @self.block.sync
            def _(sync):
                for h in used:
                    sync.sem_clear(h)
            self.barrier()
        self._free.extend(self._used)
        self._used = []


def r3(ap, c=NC):
    return ap.rearrange("p (c t) -> p c t", c=c)


def host_constants():
    c = {}
    c["ident_f32"] = np.eye(128, dtype=np.float32)
    c["ident_bf"] = _bf(np.eye(128))
    c["ones_row"] = np.ones((1, 128), dtype=np.float32)
    c["ones_col_f"] = np.ones((128, 1), dtype=np.float32)
    c["ones8_f"] = np.ones((128, 8), dtype=np.float32)
    c["ones_col_bf"] = _bf(np.ones((128, 1)))
    sel = np.zeros((128, 16 * 128), dtype=np.float32)
    for i in range(16):
        for m in range(128):
            sel[8 * i + m // 16, 128 * i + m] = 1.0
    c["sel_repl"] = sel
    s16 = np.zeros((16, 128), dtype=np.float32)
    for m in range(128):
        s16[m % 16, m] = 1.0
    c["sel16"] = s16
    ts = np.zeros((128, 16 * 128), dtype=np.float32)
    for i in range(16):
        for p in range(128):
            ts[p, 128 * i + 8 * i + p // 16] = 1.0
    c["tsel_bf"] = _bf(ts)
    ser = np.zeros((8, 128), dtype=np.float32)
    for m in range(128):
        ser[m // 16, m] = 1.0
    c["sel_er"] = ser
    tril = (np.arange(128)[:, None] <= np.arange(128)[None, :]).astype(np.float32)
    c["tril_bf"] = _bf(tril)
    R = np.zeros((64, 64), dtype=np.float32)
    for i in range(32):
        R[i + 32, i] = -1.0
        R[i, i + 32] = 1.0
    c["rot64_bf"] = _bf(R)
    c["fconsts"] = np.tile(np.array([[E_LN, E_RMS, SM_SHIFT, INV_SQ, -0.5, -1.0, 1.0, 0.0]],
                                    dtype=np.float32), (128, 1))
    inv = 1.0 / (10000.0 ** (np.arange(0, DR, 2, dtype=np.float64) / DR))
    tpos = np.arange(T, dtype=np.float64)
    f = np.outer(tpos, inv)
    emb = np.concatenate([f, f], axis=-1)
    c["ropeC"] = np.ascontiguousarray(np.cos(emb).T.astype(np.float32))
    c["ropeS"] = np.ascontiguousarray(np.sin(emb).T.astype(np.float32))
    return c


def host_prepare(inputs):
    g = host_constants()
    g["wxT"] = np.ascontiguousarray(np.asarray(inputs["L0_attn_a_W_xproj"], np.float32).T)
    al = np.asarray(inputs["L0_attn_a_A_log"], np.float32).reshape(-1)
    g["a_log_cols"] = np.ascontiguousarray(al.reshape(256, 128).T)
    g["d_cols"] = np.ascontiguousarray(
        np.asarray(inputs["L0_attn_a_D"], np.float32).reshape(16, 128).T)
    g["qaT"] = np.ascontiguousarray(np.asarray(inputs["L0_attn_b_q_a_w"], np.float32).T)
    g["qa_lnw_cols"] = np.ascontiguousarray(
        np.asarray(inputs["L0_attn_b_q_a_ln_w"], np.float32).reshape(12, 128).T)
    qb = np.asarray(inputs["L0_attn_b_q_b_w"], np.float32).reshape(NH, QHD, QLR)
    nope = qb[:, :DN, :].reshape(NH * DN, QLR)
    pe = qb[:, DN:, :].reshape(NH * DR, QLR)
    g["qbT"] = np.ascontiguousarray(np.concatenate([nope, pe], 0).T)
    g["kvaT"] = np.ascontiguousarray(np.asarray(inputs["L0_attn_b_kv_a_w"], np.float32).T)
    g["kva_lnw_cols"] = np.ascontiguousarray(
        np.asarray(inputs["L0_attn_b_kv_a_ln_w"], np.float32).reshape(4, 128).T)
    kvb = np.asarray(inputs["L0_attn_b_kv_b_w"], np.float32).reshape(NH, DN + DV, KVLR)
    knope = kvb[:, :DN, :].reshape(NH * DN, KVLR)
    vrows = kvb[:, DN:, :].reshape(NH * DV, KVLR)
    g["kvbT"] = np.ascontiguousarray(np.concatenate([knope, vrows], 0).T)
    g["owT"] = np.ascontiguousarray(np.asarray(inputs["L0_attn_b_o_w"], np.float32).T)
    g["fWT"] = np.ascontiguousarray(np.asarray(inputs["L0_ffn_W"], np.float32).T)
    g["fb_cols"] = np.ascontiguousarray(
        np.asarray(inputs["L0_ffn_b"], np.float32).reshape(16, 128).T)
    lA = np.asarray(inputs["L0_ffn_lora_A"], np.float32).reshape(8 * 16, 2048)
    gatew = np.asarray(inputs["L0_ffn_gate_w"], np.float32)
    g["flAT"] = np.ascontiguousarray(np.concatenate([lA, gatew], 0).T)
    g["gate_b_col"] = np.ascontiguousarray(
        np.asarray(inputs["L0_ffn_gate_b"], np.float32).reshape(8, 1))
    g["flBT"] = np.ascontiguousarray(
        np.asarray(inputs["L0_ffn_lora_B"], np.float32).transpose(0, 2, 1).reshape(128, 2048))
    return g


GLOBAL_INPUTS = {
    "wxT": (2048, 2080), "a_log_cols": (128, 256), "d_cols": (128, 16),
    "qaT": (2048, 1536), "qa_lnw_cols": (128, 12), "qbT": (1536, 3072),
    "kvaT": (2048, 576), "kva_lnw_cols": (128, 4), "kvbT": (512, 4096),
    "owT": (2048, 2048),
    "fWT": (2048, 2048), "fb_cols": (128, 16), "flAT": (2048, 136),
    "gate_b_col": (8, 1), "flBT": (128, 2048),
    "ident_f32": (128, 128), "ones_row": (1, 128), "ones_col_f": (128, 1),
    "ones8_f": (128, 8),
    "sel_repl": (128, 2048), "sel16": (16, 128), "sel_er": (8, 128),
    "ropeC": (64, T), "ropeS": (64, T), "fconsts": (128, 8),
}
GLOBAL_INPUTS_BF = {
    "ident_bf": (128, 128), "tsel_bf": (128, 2048), "tril_bf": (128, 128),
    "ones_col_bf": (128, 1), "rot64_bf": (64, 64),
}
SCRATCH = {
    "resid0": (128, NC * T), "resid1": (128, NC * T), "resid2": (128, NC * T),
    "resid3": (128, NC * T), "resid4": (128, NC * T),
}
SCRATCH_R = {
    "dtT_sc": (128, NC * T), "wmT_sc": (128, NC * T),
}
SCRATCH_BF = {
    "xn1": (128, NC * T), "xn2": (128, NC * T), "xn3": (128, NC * T),
    "xn4": (128, NC * T),
    "qn_sc": (128, NH * T), "qpe_sc": (64, NH * T),
    "kn_sc": (128, NH * T), "kpe_sc": (64, T),
    "v_sc": (128, NH * T), "atoT_sc": (128, NH * T),
}


def emit_transpose_in(P, x0, resid0):
    block, nc = P.block, P.nc
    with ExitStack() as st:
        E = st.enter_context
        s_ld, s_tr, s_ev, s_st = (P.sem() for _ in range(4))
        xbuf = [E(nc.sbuf_tensor("ti_x%d" % j, [128, D], F32)) for j in range(2)]
        obuf = [E(nc.sbuf_tensor("ti_o%d" % j, [128, NC * 128], F32)) for j in range(2)]
        pt = [E(nc.psum_tensor("ti_p%d" % j, [128, 128], F32)) for j in range(2)]
        NTB = T // 128

        @block.sync
        def _(sync):
            for tb in range(NTB):
                if tb >= 2:
                    sync.wait_ge(s_tr.h, (tb - 1) * NC)
                sync.dma_start(xbuf[tb % 2][:], x0[tb * 128:(tb + 1) * 128, :]) \
                    .then_inc(s_ld.h, 16)
                if tb >= 1:
                    sync.wait_ge(s_ev.h, tb * NC)
                    dst = r3(resid0)[:, :, (tb - 1) * 128: tb * 128]
                    sync.dma_start(dst, r3(obuf[(tb - 1) % 2].ap(), c=NC)) \
                        .then_inc(s_st.h, 16)
            sync.wait_ge(s_ev.h, NTB * NC)
            dst = r3(resid0)[:, :, (NTB - 1) * 128: NTB * 128]
            sync.dma_start(dst, r3(obuf[(NTB - 1) % 2].ap(), c=NC)).then_inc(s_st.h, 16)
            sync.wait_ge(s_st.h, NTB * 16)

        @block.tensor
        def _(tensor):
            for tb in range(NTB):
                tensor.wait_ge(s_ld.h, (tb + 1) * 16)
                for dc in range(NC):
                    k = tb * NC + dc
                    if k >= 2:
                        tensor.wait_ge(s_ev.h, k - 1)
                    tensor.transpose(pt[k % 2][:],
                                     xbuf[tb % 2][:, dc * 128:(dc + 1) * 128],
                                     P.ident_f[:]).then_inc(s_tr.h, 1)

        @block.scalar
        def _(scalar):
            for tb in range(NTB):
                if tb >= 2:
                    scalar.wait_ge(s_st.h, (tb - 1) * 16)
                for dc in range(NC):
                    k = tb * NC + dc
                    scalar.wait_ge(s_tr.h, k + 1)
                    scalar.activation(obuf[tb % 2][:, dc * 128:(dc + 1) * 128],
                                      pt[k % 2][:], AF.Copy).then_inc(s_ev.h, 1)


def emit_transpose_out(P, resid, out):
    block, nc = P.block, P.nc
    with ExitStack() as st:
        E = st.enter_context
        s_ld, s_tr, s_ev, s_st = (P.sem() for _ in range(4))
        xbuf = [E(nc.sbuf_tensor("to_x%d" % j, [128, NC * 128], F32)) for j in range(2)]
        obuf = [E(nc.sbuf_tensor("to_o%d" % j, [128, D], F32)) for j in range(2)]
        pt = [E(nc.psum_tensor("to_p%d" % j, [128, 128], F32)) for j in range(2)]
        NTB = T // 128

        @block.sync
        def _(sync):
            for tb in range(NTB):
                if tb >= 2:
                    sync.wait_ge(s_tr.h, (tb - 1) * NC)
                src = r3(resid)[:, :, tb * 128:(tb + 1) * 128]
                sync.dma_start(r3(xbuf[tb % 2].ap(), c=NC), src).then_inc(s_ld.h, 16)
                if tb >= 1:
                    sync.wait_ge(s_ev.h, tb * NC)
                    sync.dma_start(out[(tb - 1) * 128: tb * 128, :],
                                   obuf[(tb - 1) % 2][:]).then_inc(s_st.h, 16)
            sync.wait_ge(s_ev.h, NTB * NC)
            sync.dma_start(out[(NTB - 1) * 128: NTB * 128, :],
                           obuf[(NTB - 1) % 2][:]).then_inc(s_st.h, 16)
            sync.wait_ge(s_st.h, NTB * 16)

        @block.tensor
        def _(tensor):
            for tb in range(NTB):
                tensor.wait_ge(s_ld.h, (tb + 1) * 16)
                for dc in range(NC):
                    k = tb * NC + dc
                    if k >= 2:
                        tensor.wait_ge(s_ev.h, k - 1)
                    tensor.transpose(pt[k % 2][:],
                                     xbuf[tb % 2][:, dc * 128:(dc + 1) * 128],
                                     P.ident_f[:]).then_inc(s_tr.h, 1)

        @block.scalar
        def _(scalar):
            for tb in range(NTB):
                if tb >= 2:
                    scalar.wait_ge(s_st.h, (tb - 1) * 16)
                for dc in range(NC):
                    k = tb * NC + dc
                    scalar.wait_ge(s_tr.h, k + 1)
                    scalar.activation(obuf[tb % 2][:, dc * 128:(dc + 1) * 128],
                                      pt[k % 2][:], AF.Copy).then_inc(s_ev.h, 1)


def emit_ln(P, resid, xn, tag):
    block, nc = P.block, P.nc
    with ExitStack() as st:
        E = st.enter_context
        s_ld, s_cst, s_sq, s_stat, s_sm1, s_rstd, s_sm2, s_rep, s_app, s_st = \
            (P.sem() for _ in range(10))
        xbuf = [E(nc.sbuf_tensor(tag + "_x%d" % j, [128, NC * TS], F32)) for j in range(2)]
        xb16 = E(nc.sbuf_tensor(tag + "_xb", [128, NC * TS], BF16))
        xsq = E(nc.sbuf_tensor(tag + "_sq", [128, NC * TS], BF16))
        onb = E(nc.sbuf_tensor(tag + "_onb", [128, NC * TS], BF16))
        tmp = E(nc.sbuf_tensor(tag + "_tmp", [128, TS], F32))
        small = E(nc.sbuf_tensor(tag + "_small", [1, 5 * TS], F32))
        pm = E(nc.psum_tensor(tag + "_pm", [128, TS], F32))
        pq = E(nc.psum_tensor(tag + "_pq", [128, TS], F32))
        prep = E(nc.psum_tensor(tag + "_prep", [128, 2 * TS], F32))

        def C(dc):
            return slice(dc * TS, (dc + 1) * TS)

        M = small[0:1, 0:TS]
        MSQ = small[0:1, TS:2 * TS]
        VAR = small[0:1, 2 * TS:3 * TS]
        RSTD = small[0:1, 3 * TS:4 * TS]
        NMR = small[0:1, 4 * TS:5 * TS]

        @block.sync
        def _(sync):
            for s in range(NS):
                if s >= 2:
                    sync.wait_ge(s_app.h, (s - 1) * 2 * NC)
                src = r3(resid)[:, :, s * TS:(s + 1) * TS]
                sync.dma_start(r3(xbuf[s % 2].ap(), c=NC), src).then_inc(s_ld.h, 16)
                sync.wait_ge(s_app.h, (s + 1) * 2 * NC)
                dst = r3(xn)[:, :, s * TS:(s + 1) * TS]
                sync.dma_start(dst, r3(onb.ap(), c=NC)).then_inc(s_st.h, 16)
            sync.wait_ge(s_st.h, NS * 16)

        @block.scalar
        def _(scalar):
            for s in range(NS):
                if s >= 1:
                    scalar.wait_ge(s_stat.h, s * 2 * NC)
                for dc in range(NC):
                    scalar.wait_ge(s_cst.h, s * NC + dc + 1)
                    scalar.activation(xsq[:, C(dc)], xb16[:, C(dc)],
                                      AF.Square).then_inc(s_sq.h, 1)
                scalar.wait_ge(s_sm1.h, s + 1)
                if s >= 1:
                    scalar.wait_ge(s_rep.h, 2 * s)
                scalar.activation(RSTD, VAR, AF.Ln, bias=E_LN)
                scalar.activation(RSTD, RSTD, AF.Exp, scale=-0.5).then_inc(s_rstd.h, 1)

        @block.tensor
        def _(tensor):
            for s in range(NS):
                tensor.wait_ge(s_ld.h, (s + 1) * 16)
                if s >= 1:
                    tensor.wait_ge(s_sm1.h, s)
                for dc in range(NC):
                    tensor.wait_ge(s_cst.h, s * NC + dc + 1)
                    tensor.wait_ge(s_sq.h, s * NC + dc + 1)
                    tensor.matmul(pm[0:1, :], P.ones_col_b[:, 0:1],
                                  xb16[:, C(dc)],
                                  start=(dc == 0), stop=(dc == NC - 1))
                    tensor.matmul(pq[0:1, :], P.ones_col_b[:, 0:1],
                                  xsq[:, C(dc)],
                                  start=(dc == 0), stop=(dc == NC - 1)) \
                        .then_inc(s_stat.h, 2)
                tensor.wait_ge(s_sm2.h, s + 1)
                if s >= 1:
                    tensor.wait_ge(s_app.h, s * 2 * NC)
                tensor.matmul(prep[:, 0:TS], P.ones_row[0:1, :], RSTD,
                              start=True, stop=True)
                tensor.matmul(prep[:, TS:2 * TS], P.ones_row[0:1, :], NMR,
                              start=True, stop=True).then_inc(s_rep.h, 2)

        @block.vector
        def _(vector):
            for s in range(NS):
                vector.wait_ge(s_ld.h, (s + 1) * 16)
                if s >= 1:
                    vector.wait_ge(s_stat.h, s * 2 * NC)
                for dc in range(NC):
                    vector.tensor_copy(xb16[:, C(dc)], xbuf[s % 2][:, C(dc)]) \
                        .then_inc(s_cst.h, 1)
                vector.wait_ge(s_stat.h, (s + 1) * 2 * NC)
                vector.tensor_scalar(M, pm[0:1, :], 1.0 / D, None, Op.mult)
                vector.tensor_scalar(MSQ, pq[0:1, :], 1.0 / D, None, Op.mult)
                vector.tensor_tensor(VAR, M, M, Op.mult)
                vector.tensor_tensor(VAR, MSQ, VAR, Op.subtract).then_inc(s_sm1.h, 1)
                vector.wait_ge(s_rstd.h, s + 1)
                if s >= 1:
                    vector.wait_ge(s_rep.h, 2 * s)
                vector.tensor_tensor(NMR, M, RSTD, Op.mult)
                vector.tensor_scalar(NMR, NMR, -1.0, None, Op.mult).then_inc(s_sm2.h, 1)
                vector.wait_ge(s_rep.h, (s + 1) * 2)
                if s >= 1:
                    vector.wait_ge(s_st.h, s * 16)
                for dc in range(NC):
                    vector.tensor_tensor(tmp[:], xbuf[s % 2][:, C(dc)],
                                         prep[:, 0:TS], Op.mult).then_inc(s_app.h, 1)
                    vector.tensor_tensor(onb[:, C(dc)], tmp[:],
                                         prep[:, TS:2 * TS], Op.add).then_inc(s_app.h, 1)


def emit_cast_weights(P, src_dram, dst_sbuf, rows, cols, tag):
    block, nc = P.block, P.nc
    nchunk = rows // 128
    with ExitStack() as st:
        E = st.enter_context
        s_ld, s_cv = P.sem(), P.sem()
        stage = [E(nc.sbuf_tensor(tag + "_wst%d" % j, [128, cols], F32)) for j in range(2)]

        @block.sync
        def _(sync):
            for ch in range(nchunk):
                if ch >= 2:
                    sync.wait_ge(s_cv.h, ch - 1)
                sync.dma_start(stage[ch % 2][:], src_dram[ch * 128:(ch + 1) * 128, :]) \
                    .then_inc(s_ld.h, 16)
            sync.wait_ge(s_cv.h, nchunk)

        @block.scalar
        def _(scalar):
            for ch in range(nchunk):
                scalar.wait_ge(s_ld.h, (ch + 1) * 16)
                scalar.activation(dst_sbuf[:, ch * cols:(ch + 1) * cols],
                                  stage[ch % 2][:], AF.Copy).then_inc(s_cv.h, 1)


def emit_xproj(P, dins, dsc, tag="m1"):
    block, nc = P.block, P.nc
    with ExitStack() as st:
        E = st.enter_context
        wxb = E(nc.sbuf_tensor("wxb", [128, NC * 2080], BF16))
        emit_cast_weights(P, dins["wxT"], wxb, 2048, 2080, tag)
        P.barrier(tag + "w")

        s_ld, s_mm, s_sp, s_dw, s_st = (P.sem() for _ in range(5))
        xnb = [E(nc.sbuf_tensor(tag + "_xn%d" % j, [128, NC * TS], BF16)) for j in range(2)]
        pxp = [E(nc.psum_tensor(tag + "_p%d" % j, [128, TS], F32)) for j in range(2)]
        sp1 = [E(nc.sbuf_tensor(tag + "_s1%d" % j, [128, TS], F32)) for j in range(2)]
        sp2 = [E(nc.sbuf_tensor(tag + "_s2%d" % j, [128, TS], F32)) for j in range(2)]
        dtb = [E(nc.sbuf_tensor(tag + "_dt%d" % j, [128, TS], F32)) for j in range(2)]
        dtr = [E(nc.sbuf_tensor(tag + "_dr%d" % j, [128, TS], F32R)) for j in range(2)]
        wtb = [E(nc.sbuf_tensor(tag + "_wt%d" % j, [128, TS], F32R)) for j in range(2)]
        NM = 18

        def wcols(m):
            if m < 16:
                return (m * 128, (m + 1) * 128)
            if m == 16:
                return (2048, 2064)
            return (2064, 2080)

        @block.sync
        def _(sync):
            for s in range(NS):
                if s >= 2:
                    sync.wait_ge(s_mm.h, (s - 1) * NM)
                    sync.wait_ge(s_dw.h, (s - 1) * 48)
                src = r3(dsc["xn1"])[:, :, s * TS:(s + 1) * TS]
                sync.dma_start(r3(xnb[s % 2].ap(), c=NC), src).then_inc(s_ld.h, 16)
                for m in range(16):
                    km = s * 16 + m
                    sync.wait_ge(s_dw.h, 3 * km + 3)
                    sync.dma_start(dsc["dtT_sc"][:, m * T + s * TS: m * T + (s + 1) * TS],
                                   dtr[km % 2][:]).then_inc(s_st.h, 16)
                    sync.dma_start(dsc["wmT_sc"][:, m * T + s * TS: m * T + (s + 1) * TS],
                                   wtb[km % 2][:]).then_inc(s_st.h, 16)
            sync.wait_ge(s_st.h, NS * 32 * 16)

        @block.tensor
        def _(tensor):
            for s in range(NS):
                tensor.wait_ge(s_ld.h, (s + 1) * 16)
                for m in range(NM):
                    k = s * NM + m
                    if k >= 2:
                        tensor.wait_ge(s_sp.h, k - 1)
                    c0, c1 = wcols(m)
                    rows = 128 if m < 16 else 16
                    for dc in range(NC):
                        ins = tensor.matmul(pxp[k % 2][0:rows, :],
                                            wxb[:, dc * 2080 + c0: dc * 2080 + c1],
                                            xnb[s % 2][:, dc * TS:(dc + 1) * TS],
                                            start=(dc == 0), stop=(dc == NC - 1))
                    ins.then_inc(s_mm.h, 1)

        @block.scalar
        def _(scalar):
            for s in range(NS):
                for m in range(NM):
                    k = s * NM + m
                    scalar.wait_ge(s_mm.h, k + 1)
                    if m < 16:
                        km = s * 16 + m
                        if km >= 2:
                            scalar.wait_ge(s_dw.h, 3 * km - 3)
                        scalar.activation(sp1[km % 2][:], pxp[k % 2][:], AF.Abs)
                        scalar.activation(sp1[km % 2][:], sp1[km % 2][:], AF.Exp, scale=-1.0)
                        scalar.activation(sp1[km % 2][:], sp1[km % 2][:], AF.Ln, bias=1.0)
                        scalar.activation(sp2[km % 2][:], pxp[k % 2][:], AF.Relu) \
                            .then_inc(s_sp.h, 1)
                    elif m == 16:
                        scalar.activation(P.btT[:, s * TS:(s + 1) * TS],
                                          pxp[k % 2][0:16, :], AF.Copy).then_inc(s_sp.h, 1)
                    else:
                        scalar.activation(P.ctT[:, s * TS:(s + 1) * TS],
                                          pxp[k % 2][0:16, :], AF.Copy).then_inc(s_sp.h, 1)

        @block.vector
        def _(vector):
            for s in range(NS):
                vector.wait_ge(s_ld.h, (s + 1) * 16)
                for m in range(16):
                    k = s * NM + m
                    km = s * 16 + m
                    vector.wait_ge(s_sp.h, k + 1)
                    if km >= 2:
                        vector.wait_ge(s_st.h, (km - 1) * 32)
                    vector.tensor_tensor(dtb[km % 2][:], sp1[km % 2][:],
                                         sp2[km % 2][:], Op.add).then_inc(s_dw.h, 1)
                    vector.tensor_tensor(wtb[km % 2][:], dtb[km % 2][:],
                                         xnb[s % 2][:, m * TS:(m + 1) * TS], Op.mult) \
                        .then_inc(s_dw.h, 1)
                    vector.tensor_copy(dtr[km % 2][:], dtb[km % 2][:]) \
                        .then_inc(s_dw.h, 1)


def emit_bcpat(P, tag="m2"):
    block, nc = P.block, P.nc
    with ExitStack() as st:
        E = st.enter_context
        s_mm, s_ev = P.sem(), P.sem()
        pp = [E(nc.psum_tensor(tag + "_p%d" % j, [128, TS], F32)) for j in range(2)]

        @block.tensor
        def _(tensor):
            for s in range(NS):
                for j in range(2):
                    src = P.btT if j == 0 else P.ctT
                    k = s * 2 + j
                    if k >= 2:
                        tensor.wait_ge(s_ev.h, k - 1)
                    tensor.matmul(pp[k % 2][:], P.sel16[0:16, :],
                                  src[:, s * TS:(s + 1) * TS],
                                  start=True, stop=True).then_inc(s_mm.h, 1)

        @block.vector
        def _(vector):
            for s in range(NS):
                for j in range(2):
                    dst = P.bpat if j == 0 else P.cpat
                    k = s * 2 + j
                    vector.wait_ge(s_mm.h, k + 1)
                    vector.tensor_copy(dst[:, s * TS:(s + 1) * TS], pp[k % 2][:]) \
                        .then_inc(s_ev.h, 1)


def emit_scan(P, dsc, tag="m3"):
    block, nc = P.block, P.nc
    HT = T // 2
    with ExitStack() as st:
        E = st.enter_context
        s_ld = P.sem("m3ld")
        s_md = P.sem("m3md")
        s_mw = P.sem("m3mw")
        s_a = P.sem("m3a")
        s_u = P.sem("m3u")
        s_sc = P.sem("m3sc")
        s_po = P.sem("m3po")
        s_sum = P.sem("m3sum")
        s_yf = P.sem("m3yf")
        s_st = P.sem("m3st")

        dtc = E(nc.sbuf_tensor("m3_dt", [128, T], F32R))
        wc = E(nc.sbuf_tensor("m3_w", [128, T], F32R))
        xnc = E(nc.sbuf_tensor("m3_xn", [128, T], BF16))
        rc = [E(nc.sbuf_tensor("m3_rc%d" % j, [128, T], F32)) for j in range(2)]
        abuf = [E(nc.sbuf_tensor("m3_a%d" % j, [128, HT], F32)) for j in range(3)]
        ubuf = [E(nc.sbuf_tensor("m3_u%d" % j, [128, HT], F32)) for j in range(2)]
        hbuf = [E(nc.sbuf_tensor("m3_h%d" % j, [128, HT], F32)) for j in range(3)]
        hp = [E(nc.sbuf_tensor("m3_hp%d" % j, [128, HT], BF16)) for j in range(3)]
        carry = E(nc.sbuf_tensor("m3_carry", [128, 16], F32))
        yt = E(nc.sbuf_tensor("m3_yt", [128, TS], F32))
        yo = [E(nc.sbuf_tensor("m3_yo%d" % j, [128, TS], F32)) for j in range(2)]
        pd = E(nc.psum_tensor("m3_pd", [128, 2 * TS], F32))
        pw = E(nc.psum_tensor("m3_pw", [128, 2 * TS], F32))
        py = [E(nc.psum_tensor("m3_py%d" % j, [128, TS], F32)) for j in range(4)]

        @block.sync
        def _(sync):
            for c in range(16):
                if c >= 1:
                    sync.wait_ge(s_yf.h, c * 8)
                sync.dma_start(xnc[:], dsc["xn1"][:, c * T:(c + 1) * T]) \
                    .then_inc(s_ld.h, 16)
                sync.dma_start(rc[c % 2][:], dsc["resid0"][:, c * T:(c + 1) * T]) \
                    .then_inc(s_ld.h, 16)
                if c >= 1:
                    sync.wait_ge(s_md.h, 128 * c)
                    sync.wait_ge(s_mw.h, 128 * c)
                sync.dma_start(dtc[:], dsc["dtT_sc"][:, c * T:(c + 1) * T]) \
                    .then_inc(s_ld.h, 16)
                sync.dma_start(wc[:], dsc["wmT_sc"][:, c * T:(c + 1) * T]) \
                    .then_inc(s_ld.h, 16)
                for hf in range(2):
                    g = c * 2 + hf
                    for sl in range(4):
                        sync.wait_ge(s_yf.h, g * 4 + sl + 1)
                        t0 = c * T + hf * HT + sl * TS
                        sync.dma_start(dsc["resid1"][:, t0:t0 + TS],
                                       yo[(g * 4 + sl) % 2][:]).then_inc(s_st.h, 16)
            sync.wait_ge(s_st.h, 16 * 128)

        def sum_mms(tensor, g, i, half2):
            if half2 < 0:
                return
            j = g * 16 + i
            tensor.wait_ge(s_po.h, j + 1)
            for sl in (2 * half2, 2 * half2 + 1):
                if i == 0 and g >= 1:
                    tensor.wait_ge(s_yf.h, (g - 1) * 4 + sl + 1)
                tensor.matmul(py[sl][:], P.tsel_b[:, i * 128:(i + 1) * 128],
                              hp[i % 3][:, sl * TS:(sl + 1) * TS],
                              start=(i == 0), stop=(i == 15)).then_inc(s_sum.h, 1)

        @block.tensor
        def _(tensor):
            for c in range(16):
                tensor.wait_ge(s_ld.h, (c + 1) * 64)
                for hf in range(2):
                    g = c * 2 + hf
                    for i in range(16):
                        j = g * 16 + i
                        selL = P.sel_replr[:, i * 128:(i + 1) * 128]
                        for half2 in range(2):
                            for sl in (2 * half2, 2 * half2 + 1):
                                t0 = hf * HT + sl * TS
                                if j >= 1 or sl >= 2:
                                    tensor.wait_ge(s_a.h, 2 * j + sl // 2)
                                tensor.matmul(pd[:, (sl % 2) * TS:(sl % 2 + 1) * TS],
                                              selL,
                                              dtc[:, t0:t0 + TS],
                                              start=True, stop=True).then_inc(s_md.h, 1)
                            for sl in (2 * half2, 2 * half2 + 1):
                                t0 = hf * HT + sl * TS
                                if j >= 1 or sl >= 2:
                                    tensor.wait_ge(s_u.h, 2 * j + sl // 2)
                                tensor.matmul(pw[:, (sl % 2) * TS:(sl % 2 + 1) * TS],
                                              selL,
                                              wc[:, t0:t0 + TS],
                                              start=True, stop=True).then_inc(s_mw.h, 1)
                            if i >= 2:
                                sum_mms(tensor, g, i - 2, half2)
                    for it in (14, 15):
                        sum_mms(tensor, g, it, 0)
                        sum_mms(tensor, g, it, 1)

        @block.scalar
        def _(scalar):
            for c in range(16):
                for hf in range(2):
                    g = c * 2 + hf
                    for i in range(16):
                        j = g * 16 + i
                        acol = P.a_cols[:, 16 * c + i:16 * c + i + 1]
                        scalar.wait_ge(s_md.h, j * 4 + 2)
                        if j >= 3:
                            scalar.wait_ge(s_sc.h, j - 2)
                        scalar.activation(abuf[i % 3][:, 0:2 * TS], pd[:, 0:2 * TS],
                                          AF.Exp, scale=acol).then_inc(s_a.h, 1)
                        scalar.wait_ge(s_md.h, j * 4 + 4)
                        scalar.activation(abuf[i % 3][:, 2 * TS:4 * TS], pd[:, 0:2 * TS],
                                          AF.Exp, scale=acol).then_inc(s_a.h, 1)

        @block.vector
        def _(vector):
            for c in range(16):
                for hf in range(2):
                    g = c * 2 + hf
                    for i in range(16):
                        j = g * 16 + i
                        vector.wait_ge(s_mw.h, j * 4 + 2)
                        vector.tensor_tensor(ubuf[i % 2][:, 0:2 * TS], pw[:, 0:2 * TS],
                                             P.bpat[:, hf * HT:hf * HT + 2 * TS],
                                             Op.mult).then_inc(s_u.h, 1)
                        vector.wait_ge(s_mw.h, j * 4 + 4)
                        vector.tensor_tensor(ubuf[i % 2][:, 2 * TS:4 * TS],
                                             pw[:, 0:2 * TS],
                                             P.bpat[:, hf * HT + 2 * TS:(hf + 1) * HT],
                                             Op.mult).then_inc(s_u.h, 1)
                        vector.wait_ge(s_a.h, 2 * j + 2)
                        if j >= 3:
                            vector.wait_ge(s_po.h, j - 2)
                        init = 0.0 if hf == 0 else carry[:, i:i + 1]
                        vector.tensor_tensor_scan(hbuf[i % 3][:], abuf[i % 3][:],
                                                  ubuf[i % 2][:], init,
                                                  Op.mult, Op.add).then_inc(s_sc.h, 1)
                        if hf == 0:
                            vector.wait_ge(s_sc.h, j + 1)
                            vector.tensor_copy(carry[:, i:i + 1],
                                               hbuf[i % 3][:, HT - 1:HT])
                    for sl in range(4):
                        k = g * 4 + sl
                        vector.wait_ge(s_sum.h, g * 64 + 60 + sl + 1)
                        t0 = hf * HT + sl * TS
                        vector.scalar_tensor_tensor(
                            yt[:], xnc[:, t0:t0 + TS], P.d_cols[:, c:c + 1],
                            py[sl][:], Op.mult, Op.add)
                        if k >= 2:
                            vector.wait_ge(s_st.h, (k - 1) * 16)
                        vector.tensor_tensor(yo[k % 2][:], yt[:],
                                             rc[c % 2][:, t0:t0 + TS], Op.add) \
                            .then_inc(s_yf.h, 1)

        @block.gpsimd
        def _(gpsimd):
            for c in range(16):
                for hf in range(2):
                    g = c * 2 + hf
                    for i in range(16):
                        j = g * 16 + i
                        gpsimd.wait_ge(s_sc.h, j + 1)
                        if j >= 3:
                            gpsimd.wait_ge(s_sum.h, (j - 2) * 4)
                        gpsimd.tensor_tensor(hp[i % 3][:], hbuf[i % 3][:],
                                             P.cpat[:, hf * HT:(hf + 1) * HT],
                                             Op.mult).then_inc(s_po.h, 1)


FFN_LORA = True
FFN_DBG = False


def emit_ffn(P, dins, xn, resid_in, resid_out, tag):
    block, nc = P.block, P.nc
    with ExitStack() as st:
        E = st.enter_context
        fwb = E(nc.sbuf_tensor(tag + "_fwb", [128, NC * 2048], BF16))
        flab = E(nc.sbuf_tensor(tag + "_flab", [128, NC * 136], BF16))
        flbb = E(nc.sbuf_tensor(tag + "_flbb", [128, 2048], BF16))
        emit_cast_weights(P, dins["fWT"], fwb, 2048, 2048, tag + "a")
        emit_cast_weights(P, dins["flAT"], flab, 2048, 136, tag + "b")
        emit_cast_weights(P, dins["flBT"], flbb, 128, 2048, tag + "c")
        P.barrier(tag + "w")

        s_ld, s_mg, s_lg, s_gb, s_t1, s_gd, s_t2, s_wt, s_wp, s_xa, s_z, \
            s_mo, s_ev, s_st, s_gs = (P.sem() for _ in range(15))

        xnb = [E(nc.sbuf_tensor(tag + "_xn%d" % j, [128, NC * TS], BF16)) for j in range(2)]
        rbuf = [E(nc.sbuf_tensor(tag + "_r%d" % j, [128, NC * TS], F32)) for j in range(2)]
        lg = E(nc.sbuf_tensor(tag + "_lg", [8, TS], F32))
        lgt = E(nc.sbuf_tensor(tag + "_lgt", [128, 32], F32))
        lmm = E(nc.sbuf_tensor(tag + "_lmm", [128, 32], F32))
        mk1 = E(nc.sbuf_tensor(tag + "_mk1", [128, 32], F32))
        m1c = E(nc.sbuf_tensor(tag + "_m1c", [128, 4], F32))
        m2c = E(nc.sbuf_tensor(tag + "_m2c", [128, 4], F32))
        e2c = E(nc.sbuf_tensor(tag + "_e2c", [128, 4], F32))
        w1c = E(nc.sbuf_tensor(tag + "_w1c", [128, 4], F32))
        w2c = E(nc.sbuf_tensor(tag + "_w2c", [128, 4], F32))
        wv = E(nc.sbuf_tensor(tag + "_wv", [128, 32], F32))
        wT = E(nc.sbuf_tensor(tag + "_wT", [8, TS], F32))
        xat = E(nc.sbuf_tensor(tag + "_xat", [128, TS], BF16))
        zt = E(nc.sbuf_tensor(tag + "_zt", [128, TS], BF16))
        ot = [E(nc.sbuf_tensor(tag + "_ot%d" % j, [128, TS], F32)) for j in range(2)]
        pxa = E(nc.psum_tensor(tag + "_pxa", [128, TS], F32))
        pg = E(nc.psum_tensor(tag + "_pg", [8, TS], F32))
        ptg = E(nc.psum_tensor(tag + "_ptg", [128, 32], F32))
        ptw = E(nc.psum_tensor(tag + "_ptw", [8, TS], F32))
        pwp = E(nc.psum_tensor(tag + "_pwp", [128, TS], F32))
        pbase = [E(nc.psum_tensor(tag + "_pb%d" % j, [128, TS], F32)) for j in range(2)]

        @block.sync
        def _(sync):
            for s in range(NS):
                if s >= 2:
                    sync.wait_ge(s_mo.h, (s - 1) * NC)
                    sync.wait_ge(s_ev.h, (s - 1) * NC)
                src = r3(xn)[:, :, s * TS:(s + 1) * TS]
                sync.dma_start(r3(xnb[s % 2].ap(), c=NC), src).then_inc(s_ld.h, 16)
                src2 = r3(resid_in)[:, :, s * TS:(s + 1) * TS]
                sync.dma_start(r3(rbuf[s % 2].ap(), c=NC), src2).then_inc(s_ld.h, 16)
                if FFN_DBG and tag == "f1":
                    sync.wait_ge(s_z.h, s + 1)
                    sync.dma_start(P.dsc["dbg_lgt"][:, s * 32:(s + 1) * 32], lgt[:]).then_inc(s_st.h, 16)
                    sync.dma_start(P.dsc["dbg_wv"][:, s * 32:(s + 1) * 32], wv[:]).then_inc(s_st.h, 16)
                    sync.dma_start(P.dsc["dbg_wT"][0:8, s * TS:(s + 1) * TS], wT[0:8, :]).then_inc(s_st.h, 16)
                    sync.dma_start(P.dsc["dbg_zt"][:, s * TS:(s + 1) * TS], zt[:]).then_inc(s_st.h, 16)
                    sync.dma_start(P.dsc["dbg_xat"][:, s * TS:(s + 1) * TS], xat[:]).then_inc(s_st.h, 16)
                for oc in range(NC):
                    ko = s * NC + oc
                    sync.wait_ge(s_ev.h, ko + 1)
                    dst = resid_out[:, oc * T + s * TS: oc * T + (s + 1) * TS]
                    sync.dma_start(dst, ot[ko % 2][:]).then_inc(s_st.h, 16)
            sync.wait_ge(s_st.h, NS * NC * 16 + (NS * 80 if (FFN_DBG and tag == "f1") else 0))

        @block.tensor
        def _(tensor):
            for s in range(NS):
                tensor.wait_ge(s_ld.h, (s + 1) * 32)
                if s >= 1:
                    tensor.wait_ge(s_xa.h, s)
                    tensor.wait_ge(s_lg.h, 2 * s - 1)
                for dc in range(NC):
                    tensor.matmul(pxa[:], flab[:, dc * 136:dc * 136 + 128],
                                  xnb[s % 2][:, dc * TS:(dc + 1) * TS],
                                  start=(dc == 0), stop=(dc == NC - 1))
                for dc in range(NC):
                    ins = tensor.matmul(pg[0:8, :], flab[:, dc * 136 + 128:(dc + 1) * 136],
                                        xnb[s % 2][:, dc * TS:(dc + 1) * TS],
                                        start=(dc == 0), stop=(dc == NC - 1))
                ins.then_inc(s_mg.h, 1)
                tensor.wait_ge(s_gb.h, s + 1)
                if s >= 1:
                    tensor.wait_ge(s_gs.h, 36 * (s - 1) + 1)
                for tb in range(4):
                    tensor.transpose(ptg[:, tb * 8:(tb + 1) * 8],
                                     lg[0:8, tb * 128:(tb + 1) * 128],
                                     P.ident_f[0:8, 0:8]).then_inc(s_t1.h, 1)
                tensor.wait_ge(s_gs.h, 36 * s + 36)
                if s >= 1:
                    tensor.wait_ge(s_wt.h, s)
                for tb in range(4):
                    tensor.transpose(ptw[0:8, tb * 128:(tb + 1) * 128],
                                     wv[:, tb * 8:(tb + 1) * 8],
                                     P.ident_f[:]).then_inc(s_t2.h, 1)
                tensor.wait_ge(s_wt.h, s + 1)
                if s >= 1:
                    tensor.wait_ge(s_z.h, s)
                tensor.matmul(pwp[:], P.sel_er[0:8, :], wT[0:8, :],
                              start=True, stop=True).then_inc(s_wp.h, 1)
                for oc in range(NC):
                    ko = s * NC + oc
                    if ko >= 2:
                        tensor.wait_ge(s_ev.h, ko - 1)
                    for dc in range(NC):
                        ins = tensor.matmul(pbase[ko % 2][:],
                                      fwb[:, dc * 2048 + oc * 128: dc * 2048 + (oc + 1) * 128],
                                      xnb[s % 2][:, dc * TS:(dc + 1) * TS],
                                      start=(dc == 0), stop=(not FFN_LORA and dc == NC - 1))
                    if FFN_LORA:
                        if oc == 0:
                            tensor.wait_ge(s_z.h, s + 1)
                        tensor.matmul(pbase[ko % 2][:], flbb[:, oc * 128:(oc + 1) * 128],
                                      zt[:], start=False, stop=True).then_inc(s_mo.h, 1)
                    else:
                        ins.then_inc(s_mo.h, 1)

        @block.scalar
        def _(scalar):
            for s in range(NS):
                scalar.wait_ge(s_mg.h, s + 1)
                if s >= 1:
                    scalar.wait_ge(s_t1.h, 4 * s)
                scalar.activation(lg[0:8, :], pg[0:8, :], AF.Copy).then_inc(s_lg.h, 1)
                scalar.wait_ge(s_gs.h, 36 * s + 21)
                scalar.activation(e2c[:], m2c[:], AF.Exp).then_inc(s_lg.h, 1)
                scalar.wait_ge(s_t2.h, (s + 1) * 4)
                if s >= 1:
                    scalar.wait_ge(s_wp.h, s)
                scalar.activation(wT[0:8, :], ptw[0:8, :], AF.Copy).then_inc(s_wt.h, 1)
                scalar.wait_ge(s_mg.h, s + 1)
                if s >= 1:
                    scalar.wait_ge(s_z.h, s)
                scalar.activation(xat[:], pxa[:], AF.Copy).then_inc(s_xa.h, 1)

        @block.vector
        def _(vector):
            for s in range(NS):
                vector.wait_ge(s_lg.h, 2 * s + 1)
                vector.tensor_scalar(lg[0:8, :], lg[0:8, :], P.gate_b[0:8, 0:1],
                                     None, Op.add).then_inc(s_gb.h, 1)
                gctr = [36 * s]

                def gv(ins):
                    gctr[0] += 1
                    ins.then_inc(s_gs.h, 1)
                    vector.wait_ge(s_gs.h, gctr[0])

                vector.wait_ge(s_t1.h, (s + 1) * 4)
                gv(vector.tensor_copy(lgt[:], ptg[:]))
                for tb in range(4):
                    sl8 = slice(tb * 8, (tb + 1) * 8)
                    gv(vector.reduce_max(m1c[:, tb:tb + 1], lgt[:, sl8], axis=AX.X))
                for tb in range(4):
                    sl8 = slice(tb * 8, (tb + 1) * 8)
                    gv(vector.tensor_scalar(lgt[:, sl8], lgt[:, sl8], m1c[:, tb:tb + 1],
                                            None, Op.subtract))
                    gv(vector.tensor_scalar(mk1[:, sl8], lgt[:, sl8], 0.0, None, Op.is_ge))
                    gv(vector.scalar_tensor_tensor(lmm[:, sl8], mk1[:, sl8], -1e30,
                                                   lgt[:, sl8], Op.mult, Op.add))
                    gv(vector.reduce_max(m2c[:, tb:tb + 1], lmm[:, sl8], axis=AX.X))
                vector.wait_ge(s_lg.h, 2 * s + 2)
                if s >= 1:
                    vector.wait_ge(s_t2.h, 4 * s)
                gv(vector.tensor_scalar(e2c[:], e2c[:], 1.0, None, Op.add))
                gv(vector.reciprocal(w1c[:], e2c[:]))
                gv(vector.tensor_scalar(w2c[:], w1c[:], -1.0, 1.0, Op.mult, Op.add))
                for tb in range(4):
                    sl8 = slice(tb * 8, (tb + 1) * 8)
                    gv(vector.tensor_scalar(wv[:, sl8], lmm[:, sl8], m2c[:, tb:tb + 1],
                                            0.0, Op.subtract, Op.is_ge))
                    gv(vector.tensor_scalar(wv[:, sl8], wv[:, sl8], w2c[:, tb:tb + 1],
                                            None, Op.mult))
                    ins = vector.scalar_tensor_tensor(wv[:, sl8], mk1[:, sl8],
                                                      w1c[:, tb:tb + 1], wv[:, sl8],
                                                      Op.mult, Op.add)
                    if tb < 3:
                        gv(ins)
                    else:
                        gctr[0] += 1
                        ins.then_inc(s_gs.h, 1)
                vector.wait_ge(s_wp.h, s + 1)
                vector.wait_ge(s_xa.h, s + 1)
                if s >= 1:
                    vector.wait_ge(s_mo.h, s * NC)
                vector.tensor_tensor(zt[:], xat[:], pwp[:], Op.mult).then_inc(s_z.h, 1)
                for oc in range(NC):
                    ko = s * NC + oc
                    vector.wait_ge(s_mo.h, ko + 1)
                    if ko >= 2:
                        vector.wait_ge(s_st.h, (ko - 1) * 16)
                    vector.scalar_tensor_tensor(
                        ot[ko % 2][:], pbase[ko % 2][:], P.fb_cols[:, oc:oc + 1],
                        rbuf[s % 2][:, oc * TS:(oc + 1) * TS], Op.add, Op.add) \
                        .then_inc(s_ev.h, 1)


def emit_attn_q(P, dins, dsc, tag="aq"):
    block, nc = P.block, P.nc
    with ExitStack() as st:
        E = st.enter_context
        qab = E(nc.sbuf_tensor("qab", [128, NC * 1536], BF16))
        qbb = E(nc.sbuf_tensor("qbb", [128, 12 * 3072], BF16))
        emit_cast_weights(P, dins["qaT"], qab, 2048, 1536, tag + "a")
        emit_cast_weights(P, dins["qbT"], qbb, 1536, 3072, tag + "b")
        P.barrier(tag + "w")

        s_ld, s_mq, s_eq, s_ns, s_ms, s_rs, s_rp, s_qn, s_mh, s_eh, s_st = \
            (P.sem() for _ in range(11))
        xnb = [E(nc.sbuf_tensor(tag + "_xn%d" % j, [128, NC * TS], BF16)) for j in range(2)]
        qlf = E(nc.sbuf_tensor(tag + "_qlf", [128, 12 * TS], BF16))
        qsq = E(nc.sbuf_tensor(tag + "_qsq", [128, TS], BF16))
        qln = E(nc.sbuf_tensor(tag + "_qln", [128, 12 * TS], BF16))
        qot = [E(nc.sbuf_tensor(tag + "_qo%d" % j, [128, TS], BF16)) for j in range(2)]
        small = E(nc.sbuf_tensor(tag + "_small", [1, 2 * TS], F32))
        pql = [E(nc.psum_tensor(tag + "_pql%d" % j, [128, TS], F32)) for j in range(2)]
        pns = E(nc.psum_tensor(tag + "_pns", [128, TS], F32))
        prs = E(nc.psum_tensor(tag + "_prs", [128, TS], F32))
        pq = [E(nc.psum_tensor(tag + "_pq%d" % j, [128, TS], F32)) for j in range(2)]
        RSTD = small[0:1, 0:TS]
        MS = small[0:1, TS:2 * TS]
        NHM = NH + 8

        @block.sync
        def _(sync):
            for s in range(NS):
                if s >= 2:
                    sync.wait_ge(s_mh.h, (s - 1) * NHM)
                src = r3(dsc["xn3"])[:, :, s * TS:(s + 1) * TS]
                sync.dma_start(r3(xnb[s % 2].ap(), c=NC), src).then_inc(s_ld.h, 16)
                for m in range(NHM):
                    km = s * NHM + m
                    sync.wait_ge(s_eh.h, km + 1)
                    if m < NH:
                        dst = dsc["qn_sc"][:, m * T + s * TS: m * T + (s + 1) * TS]
                        sync.dma_start(dst, qot[km % 2][:]).then_inc(s_st.h, 16)
                    else:
                        mm = m - NH
                        base = (2 * mm) * T + s * TS
                        dst = bass.AP(dsc["qpe_sc"].tensor, base,
                                      [[T, 2], [NH * T, 64], [1, TS]])
                        sync.dma_start(dst, qot[km % 2][:]).then_inc(s_st.h, 16)
            sync.wait_ge(s_st.h, NS * NHM * 16)

        @block.tensor
        def _(tensor):
            for s in range(NS):
                tensor.wait_ge(s_ld.h, (s + 1) * 16)
                for m in range(12):
                    k = s * 12 + m
                    if k >= 2:
                        tensor.wait_ge(s_eq.h, 2 * (k - 1))
                    for dc in range(NC):
                        ins = tensor.matmul(pql[k % 2][:],
                                            qab[:, dc * 1536 + m * 128: dc * 1536 + (m + 1) * 128],
                                            xnb[s % 2][:, dc * TS:(dc + 1) * TS],
                                            start=(dc == 0), stop=(dc == NC - 1))
                    ins.then_inc(s_mq.h, 1)
                    tensor.wait_ge(s_eq.h, 2 * k + 2)
                    if m == 0 and s >= 1:
                        tensor.wait_ge(s_ms.h, s)
                    tensor.matmul(pns[0:1, :], P.ones_col_b[:, 0:1],
                                  qsq[:],
                                  start=(m == 0), stop=(m == 11)).then_inc(s_ns.h, 1)
                tensor.wait_ge(s_rs.h, s + 1)
                if s >= 1:
                    tensor.wait_ge(s_qn.h, s * 12)
                tensor.matmul(prs[:], P.ones_row[0:1, :], RSTD,
                              start=True, stop=True).then_inc(s_rp.h, 1)
                for m in range(NHM):
                    km = s * NHM + m
                    if km >= 2:
                        tensor.wait_ge(s_eh.h, km - 1)
                    if m == 0:
                        tensor.wait_ge(s_qn.h, (s + 1) * 12)
                    if m < NH:
                        cols0 = m * 128
                    else:
                        cols0 = 2048 + (m - NH) * 128
                    for kl in range(12):
                        ins = tensor.matmul(pq[km % 2][:],
                                            qbb[:, kl * 3072 + cols0: kl * 3072 + cols0 + 128],
                                            qln[:, kl * TS:(kl + 1) * TS],
                                            start=(kl == 0), stop=(kl == 11))
                    ins.then_inc(s_mh.h, 1)

        @block.scalar
        def _(scalar):
            for s in range(NS):
                for m in range(12):
                    k = s * 12 + m
                    scalar.wait_ge(s_mq.h, k + 1)
                    if k >= 1:
                        scalar.wait_ge(s_ns.h, k)
                    if s >= 1 and m == 0:
                        scalar.wait_ge(s_qn.h, s * 12)
                    scalar.activation(qlf[:, m * TS:(m + 1) * TS], pql[k % 2][:],
                                      AF.Copy)
                    scalar.activation(qsq[:], pql[k % 2][:], AF.Square) \
                        .then_inc(s_eq.h, 2)
                scalar.wait_ge(s_ms.h, s + 1)
                if s >= 1:
                    scalar.wait_ge(s_rp.h, s)
                scalar.activation(RSTD, MS, AF.Ln, bias=E_RMS)
                scalar.activation(RSTD, RSTD, AF.Exp, scale=-0.5).then_inc(s_rs.h, 1)
                for m in range(NHM):
                    km = s * NHM + m
                    scalar.wait_ge(s_mh.h, km + 1)
                    if km >= 2:
                        scalar.wait_ge(s_st.h, (km - 1) * 16)
                    scalar.activation(qot[km % 2][:], pq[km % 2][:], AF.Copy) \
                        .then_inc(s_eh.h, 1)

        @block.vector
        def _(vector):
            for s in range(NS):
                vector.wait_ge(s_ns.h, (s + 1) * 12)
                vector.tensor_scalar(MS, pns[0:1, :], 1.0 / QLR, None, Op.mult) \
                    .then_inc(s_ms.h, 1)
                vector.wait_ge(s_rp.h, s + 1)
                if s >= 1:
                    vector.wait_ge(s_mh.h, s * NHM)
                for m in range(12):
                    vector.scalar_tensor_tensor(qln[:, m * TS:(m + 1) * TS],
                                                qlf[:, m * TS:(m + 1) * TS],
                                                P.qa_lnw[:, m:m + 1], prs[:],
                                                Op.mult, Op.mult).then_inc(s_qn.h, 1)


def emit_attn_kv(P, dins, dsc, tag="ak"):
    block, nc = P.block, P.nc
    with ExitStack() as st:
        E = st.enter_context
        kvab = E(nc.sbuf_tensor("kvab", [128, NC * 576], BF16))
        kvbb = E(nc.sbuf_tensor("kvbb", [128, 4 * 4096], BF16))
        emit_cast_weights(P, dins["kvaT"], kvab, 2048, 576, tag + "a")
        emit_cast_weights(P, dins["kvbT"], kvbb, 512, 4096, tag + "b")
        P.barrier(tag + "w")

        s_ld, s_mc, s_ec, s_ns, s_ms, s_rs, s_rp, s_cn, s_mh, s_eh, s_rope, s_st, \
            s_rot = (P.sem() for _ in range(13))
        xnb = [E(nc.sbuf_tensor(tag + "_xn%d" % j, [128, NC * TS], BF16)) for j in range(2)]
        ropbC = E(nc.sbuf_tensor(tag + "_ropC", [64, TS], F32))
        ropbS = E(nc.sbuf_tensor(tag + "_ropS", [64, TS], F32))
        cf = E(nc.sbuf_tensor(tag + "_cf", [128, 4 * TS], BF16))
        csq = E(nc.sbuf_tensor(tag + "_csq", [128, TS], BF16))
        cnb = E(nc.sbuf_tensor(tag + "_cnb", [128, 4 * TS], BF16))
        kpf = E(nc.sbuf_tensor(tag + "_kpf", [64, TS], BF16))
        rt1 = E(nc.sbuf_tensor(tag + "_rt1", [64, TS], F32))
        rt2 = E(nc.sbuf_tensor(tag + "_rt2", [64, TS], F32))
        prot = E(nc.psum_tensor(tag + "_prot", [64, TS], F32))
        kpo = E(nc.sbuf_tensor(tag + "_kpo", [64, TS], BF16))
        kot = [E(nc.sbuf_tensor(tag + "_ko%d" % j, [128, TS], BF16)) for j in range(2)]
        small = E(nc.sbuf_tensor(tag + "_small", [1, 2 * TS], F32))
        pc = [E(nc.psum_tensor(tag + "_pc%d" % j, [128, TS], F32)) for j in range(2)]
        pns = E(nc.psum_tensor(tag + "_pns", [128, TS], F32))
        prs = E(nc.psum_tensor(tag + "_prs", [128, TS], F32))
        pk = [E(nc.psum_tensor(tag + "_pk%d" % j, [128, TS], F32)) for j in range(2)]
        RSTD = small[0:1, 0:TS]
        MS = small[0:1, TS:2 * TS]
        NKM = NH + 16

        @block.sync
        def _(sync):
            for s in range(NS):
                if s >= 2:
                    sync.wait_ge(s_mh.h, (s - 1) * NKM)
                src = r3(dsc["xn3"])[:, :, s * TS:(s + 1) * TS]
                sync.dma_start(r3(xnb[s % 2].ap(), c=NC), src).then_inc(s_ld.h, 32)
                sync.dma_start(ropbC[:], dins["ropeC"][:, s * TS:(s + 1) * TS]) \
                    .then_inc(s_ld.h, 32)
                sync.dma_start(ropbS[:], dins["ropeS"][:, s * TS:(s + 1) * TS]) \
                    .then_inc(s_ld.h, 32)
                for m in range(NKM):
                    km = s * NKM + m
                    sync.wait_ge(s_eh.h, km + 1)
                    if m < NH:
                        dst = dsc["kn_sc"][:, m * T + s * TS: m * T + (s + 1) * TS]
                        sync.dma_start(dst, kot[km % 2][:]).then_inc(s_st.h, 16)
                    else:
                        mv = m - NH
                        tb, vc = mv // 4, mv % 4
                        base = (4 * vc) * T + 128 * (4 * s + tb)
                        dst = bass.AP(dsc["v_sc"].tensor, base,
                                      [[NH * T, 128], [T, 4], [1, 128]])
                        sync.dma_start(dst, kot[km % 2][:]).then_inc(s_st.h, 16)
                sync.wait_ge(s_rope.h, s + 1)
                sync.dma_start(dsc["kpe_sc"][:, s * TS:(s + 1) * TS], kpo[0:64, :]) \
                    .then_inc(s_st.h, 16)
            sync.wait_ge(s_st.h, NS * (NKM + 1) * 16)

        @block.tensor
        def _(tensor):
            for s in range(NS):
                tensor.wait_ge(s_ld.h, (s + 1) * 96)
                for m in range(5):
                    k = s * 5 + m
                    if k >= 2:
                        tensor.wait_ge(s_ec.h, 2 * (k - 1))
                    rows = 128 if m < 4 else 64
                    for dc in range(NC):
                        ins = tensor.matmul(pc[k % 2][0:rows, :],
                                            kvab[:, dc * 576 + m * 128:
                                                 dc * 576 + m * 128 + rows],
                                            xnb[s % 2][:, dc * TS:(dc + 1) * TS],
                                            start=(dc == 0), stop=(dc == NC - 1))
                    ins.then_inc(s_mc.h, 1)
                    if m < 4:
                        tensor.wait_ge(s_ec.h, 2 * k + 2)
                        if m == 0 and s >= 1:
                            tensor.wait_ge(s_ms.h, s)
                        tensor.matmul(pns[0:1, :], P.ones_col_b[:, 0:1],
                                      csq[:],
                                      start=(m == 0), stop=(m == 3)).then_inc(s_ns.h, 1)
                tensor.wait_ge(s_ec.h, s * 10 + 10)
                if s >= 1:
                    tensor.wait_ge(s_rope.h, s)
                tensor.matmul(prot[0:64, :], P.rot64[0:64, :], kpf[0:64, :],
                              start=True, stop=True).then_inc(s_rot.h, 1)
                tensor.wait_ge(s_rs.h, s + 1)
                if s >= 1:
                    tensor.wait_ge(s_cn.h, s * 4)
                tensor.matmul(prs[:], P.ones_row[0:1, :], RSTD,
                              start=True, stop=True).then_inc(s_rp.h, 1)
                for m in range(NKM):
                    km = s * NKM + m
                    if km >= 2:
                        tensor.wait_ge(s_eh.h, km - 1)
                    if m == 0:
                        tensor.wait_ge(s_cn.h, (s + 1) * 4)
                    if m < NH:
                        for kl in range(4):
                            ins = tensor.matmul(pk[km % 2][:],
                                          kvbb[:, kl * 4096 + m * 128:
                                               kl * 4096 + (m + 1) * 128],
                                          cnb[:, kl * TS:(kl + 1) * TS],
                                          start=(kl == 0), stop=(kl == 3))
                    else:
                        mv = m - NH
                        tb, vc = mv // 4, mv % 4
                        for kl in range(4):
                            ins = tensor.matmul(pk[km % 2][:],
                                          cnb[:, kl * TS + tb * 128: kl * TS + (tb + 1) * 128],
                                          kvbb[:, kl * 4096 + 2048 + vc * TS:
                                               kl * 4096 + 2048 + (vc + 1) * TS],
                                          start=(kl == 0), stop=(kl == 3))
                    ins.then_inc(s_mh.h, 1)

        @block.scalar
        def _(scalar):
            for s in range(NS):
                for m in range(5):
                    k = s * 5 + m
                    scalar.wait_ge(s_mc.h, k + 1)
                    if m < 4:
                        if s * 4 + m >= 1:
                            scalar.wait_ge(s_ns.h, s * 4 + m)
                        if s >= 1 and m == 0:
                            scalar.wait_ge(s_cn.h, s * 4)
                        scalar.activation(cf[:, m * TS:(m + 1) * TS], pc[k % 2][:],
                                          AF.Copy)
                        scalar.activation(csq[:], pc[k % 2][:], AF.Square) \
                            .then_inc(s_ec.h, 2)
                    else:
                        scalar.activation(kpf[0:64, :], pc[k % 2][0:64, :], AF.Copy) \
                            .then_inc(s_ec.h, 2)
                scalar.wait_ge(s_ms.h, s + 1)
                if s >= 1:
                    scalar.wait_ge(s_rp.h, s)
                scalar.activation(RSTD, MS, AF.Ln, bias=E_RMS)
                scalar.activation(RSTD, RSTD, AF.Exp, scale=-0.5).then_inc(s_rs.h, 1)
                for m in range(NKM):
                    km = s * NKM + m
                    scalar.wait_ge(s_mh.h, km + 1)
                    if km >= 2:
                        scalar.wait_ge(s_st.h, (km - 1) * 16)
                    scalar.activation(kot[km % 2][:], pk[km % 2][:], AF.Copy) \
                        .then_inc(s_eh.h, 1)

        @block.vector
        def _(vector):
            for s in range(NS):
                vector.wait_ge(s_ns.h, (s + 1) * 4)
                vector.tensor_scalar(MS, pns[0:1, :], 1.0 / KVLR, None, Op.mult) \
                    .then_inc(s_ms.h, 1)
                vector.wait_ge(s_rp.h, s + 1)
                if s >= 1:
                    vector.wait_ge(s_mh.h, s * NKM)
                for m in range(4):
                    vector.scalar_tensor_tensor(cnb[:, m * TS:(m + 1) * TS],
                                                cf[:, m * TS:(m + 1) * TS],
                                                P.kva_lnw[:, m:m + 1], prs[:],
                                                Op.mult, Op.mult).then_inc(s_cn.h, 1)
                vector.wait_ge(s_rot.h, s + 1)
                if s >= 1:
                    vector.wait_ge(s_st.h, s * (NKM + 1) * 16)
                vector.tensor_tensor(rt1[0:64, :], kpf[0:64, :], ropbC[0:64, :], Op.mult)
                vector.tensor_tensor(rt2[0:64, :], prot[0:64, :], ropbS[0:64, :], Op.mult)
                vector.tensor_tensor(kpo[0:64, :], rt1[0:64, :], rt2[0:64, :], Op.add) \
                    .then_inc(s_rope.h, 1)


def _ab_schedule():
    sched = []
    for qb in range(NS):
        if qb == 0:
            order = [0, 1, 2, 3]
        else:
            order = [0] + [4 * qb + jj for jj in range(4)] + list(range(1, 4 * qb))
        for i, kc in enumerate(order):
            qv0 = max(0, kc - 4 * qb) * 128
            sched.append((qb, kc, qv0, i == len(order) - 1))
    return sched, len(sched)


def emit_attn_core(P, dins, dsc, tag="ab"):
    block, nc = P.block, P.nc
    sched, SC_H = _ab_schedule()
    with ExitStack() as st:
        E = st.enter_context
        s_ld = P.sem("abld")
        s_rope = P.sem("abrope")
        s_rot = P.sem("abrot")
        s_rte = P.sem("abrte")
        s_sc = P.sem("absc")
        s_ex = P.sem("abex")
        s_mk = P.sem("abmk")
        s_mz = P.sem("abmz")
        s_av = P.sem("abav")
        s_dn = P.sem("abdn")
        s_dc = P.sem("abdc")
        s_pd = P.sem("abpd")
        s_rc = P.sem("abrc")
        s_rp = P.sem("abrp")
        s_rs = P.sem("abrs")
        s_no = P.sem("abno")
        s_st = P.sem("abst")

        qn = [E(nc.sbuf_tensor("ab_qn%d" % j, [128, T], BF16)) for j in range(2)]
        qpe = [E(nc.sbuf_tensor("ab_qpe%d" % j, [64, T], BF16)) for j in range(2)]
        qpr = [E(nc.sbuf_tensor("ab_qpr%d" % j, [64, T], BF16)) for j in range(2)]
        rotb = E(nc.sbuf_tensor("ab_rotb", [64, T], BF16))
        kn = [E(nc.sbuf_tensor("ab_kn%d" % j, [128, T], BF16)) for j in range(2)]
        vb = [E(nc.sbuf_tensor("ab_vb%d" % j, [128, T], BF16)) for j in range(2)]
        kpe = E(nc.sbuf_tensor("ab_kpe", [64, T], BF16))
        ropC = E(nc.sbuf_tensor("ab_ropC", [64, T], F32))
        ropS = E(nc.sbuf_tensor("ab_ropS", [64, T], F32))
        rr1 = E(nc.sbuf_tensor("ab_rr1", [64, T], BF16))
        rr2 = E(nc.sbuf_tensor("ab_rr2", [64, T], BF16))
        pt = [E(nc.sbuf_tensor("ab_pt%d" % j, [128, TS], BF16)) for j in range(12)]
        den = [E(nc.sbuf_tensor("ab_den%d" % j, [128, TS], F32)) for j in range(2)]
        denr = E(nc.sbuf_tensor("ab_denr", [128, TS], F32R))
        rcp = E(nc.sbuf_tensor("ab_rcp", [8, TS], F32))
        rcpr = E(nc.sbuf_tensor("ab_rcpr", [8, TS], F32R))
        rS = [E(nc.sbuf_tensor("ab_rs%d" % j, [128, TS], F32)) for j in range(2)]
        ato = [E(nc.sbuf_tensor("ab_ato%d" % j, [128, T], BF16)) for j in range(2)]
        psc = [E(nc.psum_tensor("ab_psc%d" % j, [128, TS], F32)) for j in range(5)]
        pav = [E(nc.psum_tensor("ab_pav%d" % j, [128, TS], F32)) for j in range(2)]
        pdn = E(nc.psum_tensor("ab_pdn", [128, TS], F32))
        pden = pdn
        prep = pdn
        protq = psc[4]

        @block.sync
        def _(sync):
            sync.dma_start(ropC[:], dins["ropeC"]).then_inc(s_ld.h, 16)
            sync.dma_start(ropS[:], dins["ropeS"]).then_inc(s_ld.h, 16)
            sync.dma_start(kpe[:], dsc["kpe_sc"]).then_inc(s_ld.h, 16)
            for h in range(NH):
                if h >= 2:
                    sync.wait_ge(s_sc.h, (h - 1) * SC_H)
                    sync.wait_ge(s_av.h, (h - 1) * SC_H)
                    sync.wait_ge(s_rope.h, h - 1)
                    sync.wait_ge(s_rot.h, (h - 1) * 8)
                sync.dma_start(qn[h % 2][:], dsc["qn_sc"][:, h * T:(h + 1) * T]) \
                    .then_inc(s_ld.h, 16)
                sync.dma_start(qpe[h % 2][:], dsc["qpe_sc"][:, h * T:(h + 1) * T]) \
                    .then_inc(s_ld.h, 16)
                sync.dma_start(kn[h % 2][:], dsc["kn_sc"][:, h * T:(h + 1) * T]) \
                    .then_inc(s_ld.h, 16)
                sync.dma_start(vb[h % 2][:], dsc["v_sc"][:, h * T:(h + 1) * T]) \
                    .then_inc(s_ld.h, 16)
                if h >= 1:
                    sync.wait_ge(s_no.h, h * NS)
                    sync.dma_start(dsc["atoT_sc"][:, (h - 1) * T: h * T],
                                   ato[(h - 1) % 2][:]).then_inc(s_st.h, 16)
            sync.wait_ge(s_no.h, NH * NS)
            sync.dma_start(dsc["atoT_sc"][:, (NH - 1) * T: NH * T],
                           ato[(NH - 1) % 2][:]).then_inc(s_st.h, 16)
            sync.wait_ge(s_st.h, NH * 16)

        @block.gpsimd
        def _(gpsimd):
            for h in range(NH):
                gpsimd.wait_ge(s_ld.h, 48 + (h + 1) * 64)
                gpsimd.wait_ge(s_rte.h, 8 * (h + 1))
                if h >= 2:
                    gpsimd.wait_ge(s_sc.h, (h - 1) * SC_H)
                gpsimd.tensor_tensor(rr1[0:64, :], qpe[h % 2][0:64, :], ropC[0:64, :],
                                     Op.mult)
                gpsimd.tensor_tensor(rr2[0:64, :], rotb[0:64, :], ropS[0:64, :],
                                     Op.mult)
                gpsimd.tensor_tensor(qpr[h % 2][0:64, :], rr1[0:64, :], rr2[0:64, :],
                                     Op.add).then_inc(s_rope.h, 1)
                for kc in (1, 2, 3):
                    k = h * SC_H + kc
                    if k >= 12:
                        gpsimd.wait_ge(s_av.h, k - 11)
                        gpsimd.wait_ge(s_dn.h, k - 11)
                    gpsimd.memset(pt[k % 12][:, 0:kc * 128], 0.0).then_inc(s_mz.h, 1)

        @block.tensor
        def _(tensor):
            G = 4
            NG = SC_H // G

            def emit_av_group(gi, h):
                kg = h * SC_H + gi * G
                tensor.wait_ge(s_ex.h, kg + G)
                nmask = sum(1 for (qb, kc, _, _) in sched[:gi * G + G]
                            if kc >= 4 * qb)
                if nmask:
                    tensor.wait_ge(s_mk.h, h * 32 + nmask)
                mz = [kc for (qb, kc, _, _) in sched[gi * G: gi * G + G]
                      if qb == 0 and kc >= 1]
                if mz:
                    tensor.wait_ge(s_mz.h, h * 3 + max(mz))
                for ci_av in range(gi * G, gi * G + G):
                    qb, kc, qv0, last = sched[ci_av]
                    k = h * SC_H + ci_av
                    gq = h * NS + qb
                    av0 = 0 if qb == 0 else qv0
                    if kc == 0 and gq >= 2:
                        tensor.wait_ge(s_no.h, gq - 1)
                    tensor.matmul(pav[qb % 2][:, av0:TS],
                                  vb[h % 2][:, kc * 128:(kc + 1) * 128],
                                  pt[k % 12][:, av0:TS],
                                  start=(kc == 0), stop=last).then_inc(s_av.h, 1)
                    if last:
                        tensor.wait_ge(s_dc.h, gq + 1)
                        if gq >= 1:
                            tensor.wait_ge(s_rs.h, gq)
                        tensor.matmul(pden[0:8, :], P.ones8_r[:, 0:8], denr[:],
                                      start=True, stop=True).then_inc(s_pd.h, 1)
                        tensor.wait_ge(s_rc.h, gq + 1)
                        tensor.matmul(prep[:], P.ones_row_r[0:1, :], rcpr[0:1, :],
                                      start=True, stop=True).then_inc(s_rp.h, 1)

            for h in range(NH):
                tensor.wait_ge(s_ld.h, 48 + (h + 1) * 64)
                if h >= 1:
                    tensor.wait_ge(s_ex.h, h * SC_H)
                for kq in range(8):
                    kk = h * 8 + kq
                    if kk >= 1:
                        tensor.wait_ge(s_rte.h, kk)
                    tensor.matmul(protq[0:64, :], P.rot64[0:64, :],
                                  qpe[h % 2][:, kq * TS:(kq + 1) * TS],
                                  start=True, stop=True).then_inc(s_rot.h, 1)
                tensor.wait_ge(s_rope.h, h + 1)
                for gi in range(NG):
                    kg = h * SC_H + gi * G
                    if kg >= 1:
                        tensor.wait_ge(s_ex.h, kg if gi == 0 else kg - 1)
                    if gi == 1:
                        tensor.wait_ge(s_rte.h, 8 * (h + 1))
                    for ci in range(gi * G, gi * G + G):
                        qb, kc, qv0, last = sched[ci]
                        q0 = qb * TS
                        tensor.matmul(psc[ci % 5][:, qv0:TS],
                                      kn[h % 2][:, kc * 128:(kc + 1) * 128],
                                      qn[h % 2][:, q0 + qv0: q0 + TS],
                                      start=True, stop=False)
                        tensor.matmul(psc[ci % 5][:, qv0:TS],
                                      kpe[:, kc * 128:(kc + 1) * 128],
                                      qpr[h % 2][:, q0 + qv0: q0 + TS],
                                      start=False, stop=True).then_inc(s_sc.h, 1)
                    if gi >= 2:
                        emit_av_group(gi - 2, h)
                emit_av_group(NG - 2, h)
                emit_av_group(NG - 1, h)

        @block.scalar
        def _(scalar):
            for h in range(NH):
                for ci, (qb, kc, qv0, last) in enumerate(sched):
                    k = h * SC_H + ci
                    gq = h * NS + qb
                    scalar.wait_ge(s_sc.h, k + 1)
                    if k >= 12:
                        scalar.wait_ge(s_av.h, k - 11)
                        scalar.wait_ge(s_dn.h, k - 11)
                    scalar.activation(pt[k % 12][:, qv0:TS], psc[ci % 5][:, qv0:TS],
                                      AF.Exp, bias=SM_SHIFT, scale=INV_SQ) \
                        .then_inc(s_ex.h, 1)

        @block.vector
        def _(vector):
            for h in range(NH):
                for kq in range(8):
                    kk = h * 8 + kq
                    vector.wait_ge(s_rot.h, kk + 1)
                    if kq == 0 and h >= 1:
                        vector.wait_ge(s_rope.h, h)
                    vector.tensor_copy(rotb[:, kq * TS:(kq + 1) * TS], protq[0:64, :]) \
                        .then_inc(s_rte.h, 1)
                for ci, (qb, kc, qv0, last) in enumerate(sched):
                    k = h * SC_H + ci
                    gq = h * NS + qb
                    vector.wait_ge(s_ex.h, k + 1)
                    if kc >= 4 * qb:
                        jj = kc - 4 * qb
                        vector.tensor_tensor(pt[k % 12][:, jj * 128:(jj + 1) * 128],
                                             pt[k % 12][:, jj * 128:(jj + 1) * 128],
                                             P.tril_b[:], Op.mult).then_inc(s_mk.h, 1)
                    if kc == 0:
                        vector.tensor_copy(den[qb % 2][:], pt[k % 12][:, 0:TS]) \
                            .then_inc(s_dn.h, 1)
                    else:
                        vector.tensor_tensor(den[qb % 2][:, qv0:TS],
                                             den[qb % 2][:, qv0:TS],
                                             pt[k % 12][:, qv0:TS], Op.add) \
                            .then_inc(s_dn.h, 1)
                    if last:
                        if gq >= 1:
                            vector.wait_ge(s_pd.h, gq)
                        vector.tensor_copy(denr[:], den[qb % 2][:]).then_inc(s_dc.h, 1)
                        vector.wait_ge(s_pd.h, gq + 1)
                        if gq >= 1:
                            vector.wait_ge(s_rp.h, gq)
                        vector.reciprocal(rcp[0:8, :], pden[0:8, :])
                        vector.tensor_copy(rcpr[0:8, :], rcp[0:8, :]).then_inc(s_rc.h, 1)
                        vector.wait_ge(s_rp.h, gq + 1)
                        vector.tensor_copy(rS[qb % 2][:], prep[:]).then_inc(s_rs.h, 1)
                        vector.wait_ge(s_av.h, k + 1)
                        if h >= 2 and qb == 0:
                            vector.wait_ge(s_st.h, (h - 1) * 16)
                        vector.tensor_tensor(ato[h % 2][:, qb * TS:(qb + 1) * TS],
                                             pav[qb % 2][:], rS[qb % 2][:], Op.mult) \
                            .then_inc(s_no.h, 1)


def emit_oproj(P, dins, dsc, tag="ao"):
    block, nc = P.block, P.nc
    with ExitStack() as st:
        E = st.enter_context
        owb = E(nc.sbuf_tensor("owb", [128, NC * 2048], BF16))
        emit_cast_weights(P, dins["owT"], owb, 2048, 2048, tag)
        P.barrier(tag + "w")
        s_ld, s_mo, s_ev, s_st = (P.sem() for _ in range(4))
        atb = [E(nc.sbuf_tensor(tag + "_at%d" % j, [128, NH * TS], BF16)) for j in range(2)]
        rbuf = [E(nc.sbuf_tensor(tag + "_r%d" % j, [128, NC * TS], F32)) for j in range(2)]
        ot = [E(nc.sbuf_tensor(tag + "_ot%d" % j, [128, TS], F32)) for j in range(2)]
        po = [E(nc.psum_tensor(tag + "_po%d" % j, [128, TS], F32)) for j in range(2)]

        @block.sync
        def _(sync):
            for s in range(NS):
                if s >= 2:
                    sync.wait_ge(s_mo.h, (s - 1) * NC)
                    sync.wait_ge(s_ev.h, (s - 1) * NC)
                for h in range(NH):
                    sync.dma_start(atb[s % 2][:, h * TS:(h + 1) * TS],
                                   dsc["atoT_sc"][:, h * T + s * TS: h * T + (s + 1) * TS]) \
                        .then_inc(s_ld.h, 16)
                src2 = r3(dsc["resid2"])[:, :, s * TS:(s + 1) * TS]
                sync.dma_start(r3(rbuf[s % 2].ap(), c=NC), src2).then_inc(s_ld.h, 16)
                for oc in range(NC):
                    ko = s * NC + oc
                    sync.wait_ge(s_ev.h, ko + 1)
                    dst = dsc["resid3"][:, oc * T + s * TS: oc * T + (s + 1) * TS]
                    sync.dma_start(dst, ot[ko % 2][:]).then_inc(s_st.h, 16)
            sync.wait_ge(s_st.h, NS * NC * 16)

        @block.tensor
        def _(tensor):
            for s in range(NS):
                tensor.wait_ge(s_ld.h, (s + 1) * 17 * 16)
                for oc in range(NC):
                    ko = s * NC + oc
                    if ko >= 2:
                        tensor.wait_ge(s_ev.h, ko - 1)
                    for h in range(NH):
                        ins = tensor.matmul(po[ko % 2][:],
                                            owb[:, h * 2048 + oc * 128: h * 2048 + (oc + 1) * 128],
                                            atb[s % 2][:, h * TS:(h + 1) * TS],
                                            start=(h == 0), stop=(h == NH - 1))
                    ins.then_inc(s_mo.h, 1)

        @block.vector
        def _(vector):
            for s in range(NS):
                for oc in range(NC):
                    ko = s * NC + oc
                    vector.wait_ge(s_mo.h, ko + 1)
                    if ko >= 2:
                        vector.wait_ge(s_st.h, (ko - 1) * 16)
                    vector.tensor_tensor(ot[ko % 2][:], po[ko % 2][:],
                                         rbuf[s % 2][:, oc * TS:(oc + 1) * TS], Op.add) \
                        .then_inc(s_ev.h, 1)


def build_program(phases=ALL_PHASES, debug=False, race_detect=False):
    nc = bass.Bass("TRN2", target_bir_lowering=False,
                   detect_race_conditions=race_detect)
    dins = {}
    for k, shp in GLOBAL_INPUTS.items():
        dins[k] = nc.dram_tensor(k, list(shp), F32, kind="ExternalInput").ap()
    for k, shp in GLOBAL_INPUTS_BF.items():
        dins[k] = nc.dram_tensor(k, list(shp), BF16, kind="ExternalInput").ap()
    dins["x0"] = nc.dram_tensor("x0", [T, D], F32, kind="ExternalInput").ap()
    out = nc.dram_tensor("out", [T, D], F32, kind="ExternalOutput").ap()
    skind = "ExternalOutput" if debug else "Internal"
    dsc = {}
    for k, shp in SCRATCH.items():
        dsc[k] = nc.dram_tensor(k, list(shp), F32, kind=skind).ap()
    for k, shp in SCRATCH_R.items():
        dsc[k] = nc.dram_tensor(k, list(shp), F32R, kind="Internal").ap()
    if FFN_DBG:
        dsc["dbg_lgt"] = nc.dram_tensor("dbg_lgt", [128, NS * 32], F32, kind="ExternalOutput").ap()
        dsc["dbg_wv"] = nc.dram_tensor("dbg_wv", [128, NS * 32], F32, kind="ExternalOutput").ap()
        dsc["dbg_wT"] = nc.dram_tensor("dbg_wT", [8, NS * TS], F32, kind="ExternalOutput").ap()
        dsc["dbg_zt"] = nc.dram_tensor("dbg_zt", [128, NS * TS], BF16, kind="ExternalOutput").ap()
        dsc["dbg_xat"] = nc.dram_tensor("dbg_xat", [128, NS * TS], BF16, kind="ExternalOutput").ap()
    for k, shp in SCRATCH_BF.items():
        dsc[k] = nc.dram_tensor(k, list(shp), BF16, kind=skind).ap()

    with ExitStack() as stack:
        block = stack.enter_context(nc.Block())
        P = Prog(nc, stack, block)
        P.dsc = dsc
        P.ident_f = stack.enter_context(nc.sbuf_tensor("ident_f", [128, 128], F32))
        P.ident_b = stack.enter_context(nc.sbuf_tensor("ident_b", [128, 128], BF16))
        P.ones_row = stack.enter_context(nc.sbuf_tensor("ones_rowS", [1, 128], F32))
        P.ones_col_b = stack.enter_context(nc.sbuf_tensor("ones_col_bS", [128, 1], BF16))
        P.ones_col_f = stack.enter_context(nc.sbuf_tensor("ones_col_fS", [128, 1], F32))
        P.ones_col_r = stack.enter_context(nc.sbuf_tensor("ones_col_rS", [128, 1], F32R))
        P.ones_row_r = stack.enter_context(nc.sbuf_tensor("ones_row_rS", [1, 128], F32R))
        P.ones8_f = stack.enter_context(nc.sbuf_tensor("ones8_fS", [128, 8], F32))
        P.ones8_r = stack.enter_context(nc.sbuf_tensor("ones8_rS", [128, 8], F32R))
        P.sel_replr = stack.enter_context(nc.sbuf_tensor("sel_replrS", [128, 2048], F32R))
        P.sel16 = stack.enter_context(nc.sbuf_tensor("sel16S", [16, 128], F32))
        P.sel_er = stack.enter_context(nc.sbuf_tensor("sel_erS", [8, 128], F32))
        P.tsel_b = stack.enter_context(nc.sbuf_tensor("tsel_bS", [128, 2048], BF16))
        P.tril_b = stack.enter_context(nc.sbuf_tensor("tril_bS", [128, 128], BF16))
        P.a_cols = stack.enter_context(nc.sbuf_tensor("a_colsS", [128, 256], F32))
        P.d_cols = stack.enter_context(nc.sbuf_tensor("d_colsS", [128, 16], F32))
        P.qa_lnw = stack.enter_context(nc.sbuf_tensor("qa_lnwS", [128, 12], F32))
        P.kva_lnw = stack.enter_context(nc.sbuf_tensor("kva_lnwS", [128, 4], F32))
        P.fb_cols = stack.enter_context(nc.sbuf_tensor("fb_colsS", [128, 16], F32))
        P.gate_b = stack.enter_context(nc.sbuf_tensor("gate_bS", [8, 1], F32))
        P.rot64 = stack.enter_context(nc.sbuf_tensor("rot64S", [64, 64], BF16))
        P.fconsts = stack.enter_context(nc.sbuf_tensor("fconstsS", [128, 8], F32))
        for i, v in enumerate([E_LN, E_RMS, SM_SHIFT, INV_SQ, -0.5, -1.0, 1.0, 0.0]):
            nc.const_aps.register(F32, float(v), P.fconsts[:, i:i + 1])

        sd = P.sem("dma0")
        consts = [("ident_f", P.ident_f, "ident_f32"), ("ident_b", P.ident_b, "ident_bf"),
                  ("ones_row", P.ones_row, "ones_row"), ("ones_col_b", P.ones_col_b, "ones_col_bf"),
                  ("ones_col_f", P.ones_col_f, "ones_col_f"),
                  ("ones8_f", P.ones8_f, "ones8_f"),
                  ("sel16", P.sel16, "sel16"),
                  ("sel_er", P.sel_er, "sel_er"), ("tsel_b", P.tsel_b, "tsel_bf"),
                  ("tril_b", P.tril_b, "tril_bf"), ("a_cols", P.a_cols, "a_log_cols"),
                  ("d_cols", P.d_cols, "d_cols"), ("qa_lnw", P.qa_lnw, "qa_lnw_cols"),
                  ("kva_lnw", P.kva_lnw, "kva_lnw_cols"), ("fb_cols", P.fb_cols, "fb_cols"),
                  ("gate_b", P.gate_b, "gate_b_col"), ("rot64", P.rot64, "rot64_bf"),
                  ("fconsts", P.fconsts, "fconsts")]

        @block.sync
        def _(sync):
            for _, sb, name in consts:
                sync.dma_start(sb[:], dins[name]).then_inc(sd.h, 16)

        nload = 16 * len(consts)

        @block.scalar
        def _(scalar):
            scalar.wait_ge(sd.h, nload)
            scalar.activation(P.a_cols[:], P.a_cols[:], AF.Exp)

        @block.vector
        def _(vector):
            vector.wait_ge(sd.h, nload)

        P.barrier("c0")

        with nc.sbuf_tensor("sel_replT", [128, 2048], F32) as sel_tmp:
            @block.sync
            def _(sync):
                sync.dma_start(sel_tmp[:], dins["sel_repl"]).then_inc(sd.h, 16)

            @block.vector
            def _(vector):
                vector.wait_ge(sd.h, nload + 16)
                vector.tensor_scalar(P.a_cols[:], P.a_cols[:], -1.0, None, Op.mult)
                vector.tensor_copy(P.sel_replr[:], sel_tmp[:])
                vector.tensor_copy(P.ones_col_r[:], P.ones_col_f[:])
                vector.tensor_copy(P.ones_row_r[0:1, :], P.ones_row[0:1, :])
                vector.tensor_copy(P.ones8_r[:], P.ones8_f[:])

            P.recycle()

        if "P0" in phases:
            emit_transpose_in(P, dins["x0"], dsc["resid0"])
            P.recycle()
        if "LN1" in phases:
            emit_ln(P, dsc["resid0"], dsc["xn1"], "ln1")
            P.recycle()
        with ExitStack() as mscope2:
            if "M2" in phases:
                P.bpat = mscope2.enter_context(nc.sbuf_tensor("bpatS", [128, T], F32))
                P.cpat = mscope2.enter_context(nc.sbuf_tensor("cpatS", [128, T], F32))
            with ExitStack() as mscope1:
                if "M1" in phases:
                    P.btT = mscope1.enter_context(nc.sbuf_tensor("btT", [16, T], F32))
                    P.ctT = mscope1.enter_context(nc.sbuf_tensor("ctT", [16, T], F32))
                    emit_xproj(P, dins, dsc)
                    P.recycle()
                if "M2" in phases:
                    emit_bcpat(P)
                    P.recycle()
            if "M3" in phases:
                emit_scan(P, dsc)
                P.recycle()
        if "LN2" in phases:
            emit_ln(P, dsc["resid1"], dsc["xn2"], "ln2")
            P.recycle()
        if "FFN1" in phases:
            emit_ffn(P, dins, dsc["xn2"], dsc["resid1"], dsc["resid2"], "f1")
            P.recycle()
        if "LN3" in phases:
            emit_ln(P, dsc["resid2"], dsc["xn3"], "ln3")
            P.recycle()
        if "AQ" in phases:
            emit_attn_q(P, dins, dsc)
            P.recycle()
        if "AKV" in phases:
            emit_attn_kv(P, dins, dsc)
            P.recycle()
        if "AB" in phases:
            emit_attn_core(P, dins, dsc)
            P.recycle()
        if "AO" in phases:
            emit_oproj(P, dins, dsc)
            P.recycle()
        if "LN4" in phases:
            emit_ln(P, dsc["resid3"], dsc["xn4"], "ln4")
            P.recycle()
        if "FFN2" in phases:
            emit_ffn(P, dins, dsc["xn4"], dsc["resid3"], dsc["resid4"], "f2")
            P.recycle()
        if "OUT" in phases:
            src = dsc["resid4"] if "FFN2" in phases else dsc["resid0"]
            emit_transpose_out(P, src, out)
            P.recycle()

    return nc


def kernel(**inputs):
    g = host_prepare(inputs)
    x = np.asarray(inputs["L0_attn_a_x"], np.float32)
    nc = build_program()
    in_maps = []
    for b in range(8):
        m = dict(g)
        m["x0"] = np.ascontiguousarray(x[b])
        in_maps.append(m)
    res = run_bass_kernel_spmd(nc, in_maps, core_ids=list(range(8)))
    outs = [np.asarray(res.results[b]["out"], np.float32) for b in range(8)]
    return np.stack(outs, 0)


# version 11
# speedup vs baseline: 1.0513x; 1.0019x over previous
import numpy as np
import ml_dtypes
from contextlib import ExitStack

import concourse.bass as bass
import concourse.mybir as mybir
from concourse.bass_utils import run_bass_kernel_spmd
from concourse.alu_op_type import AluOpType as Op

F32 = mybir.dt.float32
F32R = mybir.dt.float32r
BF16 = mybir.dt.bfloat16
AF = mybir.ActivationFunctionType
AX = mybir.AxisListType

D = 2048
T = 4096
NC = 16
NS = 8
TS = 512
NH = 16
QLR = 1536
KVLR = 512
DN = 128
DR = 64
DV = 128
QHD = DN + DR
E_LN = 1e-5
E_RMS = 1e-6
INV_SQ = float(1.0 / np.sqrt(float(QHD)))
SM_SHIFT = -30.0

ALL_PHASES = ("P0", "LN1", "M1", "M2", "M3", "LN2", "FFN1", "LN3",
              "AQ", "AKV", "AB", "AO", "LN4", "FFN2", "OUT")


def _bf(x):
    return np.ascontiguousarray(np.asarray(x).astype(ml_dtypes.bfloat16))


class Sem:
    def __init__(self, h):
        self.h = h


class Prog:

    ENGINES = ("sync", "vector", "scalar", "tensor", "gpsimd")

    def __init__(self, nc, stack, block, npool=64):
        self.nc = nc
        self.stack = stack
        self.block = block
        self._pool = [stack.enter_context(nc.semaphore("pool%d" % i))
                      for i in range(npool)]
        self._free = list(self._pool)
        self._used = []
        self._bar = stack.enter_context(nc.semaphore("barrier"))
        self._bar_n = 0
        self._vdum = stack.enter_context(nc.sbuf_tensor("bar_vdum", [1, 2], F32))
        self._sdum = stack.enter_context(nc.sbuf_tensor("bar_sdum", [1, 2], F32))
        self._gdum = stack.enter_context(nc.sbuf_tensor("bar_gdum", [1, 2], F32))

        @block.vector
        def _(vector):
            vector.memset(self._vdum[:], 0.0)

        @block.scalar
        def _(scalar):
            scalar.memzero(self._sdum[:])

        @block.gpsimd
        def _(gpsimd):
            gpsimd.memset(self._gdum[:], 0.0)

    def sem(self, name=None):
        h = self._free.pop()
        self._used.append(h)
        return Sem(h)

    def barrier(self, tag=None):
        n = len(self.ENGINES)
        tgt = self._bar_n + n

        @self.block.sync
        def _(sync):
            sync.nop().then_inc(self._bar, 1)
            sync.wait_ge(self._bar, tgt)

        @self.block.tensor
        def _(tensor):
            tensor.nop().then_inc(self._bar, 1)
            tensor.wait_ge(self._bar, tgt)

        @self.block.vector
        def _(vector):
            vector.memset(self._vdum[0:1, 0:1], 0.0).then_inc(self._bar, 1)
            vector.wait_ge(self._bar, tgt)

        @self.block.scalar
        def _(scalar):
            scalar.activation(self._sdum[0:1, 0:1], self._sdum[0:1, 1:2],
                              AF.Copy).then_inc(self._bar, 1)
            scalar.wait_ge(self._bar, tgt)

        @self.block.gpsimd
        def _(gpsimd):
            gpsimd.memset(self._gdum[0:1, 0:1], 0.0).then_inc(self._bar, 1)
            gpsimd.wait_ge(self._bar, tgt)

        self._bar_n = tgt

    def recycle(self):
        self.barrier()
        used = self._used
        if used:
            @self.block.sync
            def _(sync):
                for h in used:
                    sync.sem_clear(h)
            self.barrier()
        self._free.extend(self._used)
        self._used = []


def r3(ap, c=NC):
    return ap.rearrange("p (c t) -> p c t", c=c)


def host_constants():
    c = {}
    c["ident_f32"] = np.eye(128, dtype=np.float32)
    c["ident_bf"] = _bf(np.eye(128))
    c["ones_row"] = np.ones((1, 128), dtype=np.float32)
    c["ones_col_f"] = np.ones((128, 1), dtype=np.float32)
    c["ones8_f"] = np.ones((128, 8), dtype=np.float32)
    c["ones_col_bf"] = _bf(np.ones((128, 1)))
    sel = np.zeros((128, 16 * 128), dtype=np.float32)
    for i in range(16):
        for m in range(128):
            sel[8 * i + m // 16, 128 * i + m] = 1.0
    c["sel_repl"] = sel
    s16 = np.zeros((16, 128), dtype=np.float32)
    for m in range(128):
        s16[m % 16, m] = 1.0
    c["sel16"] = s16
    ts = np.zeros((128, 16 * 128), dtype=np.float32)
    for i in range(16):
        for p in range(128):
            ts[p, 128 * i + 8 * i + p // 16] = 1.0
    c["tsel_bf"] = _bf(ts)
    ser = np.zeros((8, 128), dtype=np.float32)
    for m in range(128):
        ser[m // 16, m] = 1.0
    c["sel_er"] = ser
    tril = (np.arange(128)[:, None] <= np.arange(128)[None, :]).astype(np.float32)
    c["tril_bf"] = _bf(tril)
    R = np.zeros((64, 64), dtype=np.float32)
    for i in range(32):
        R[i + 32, i] = -1.0
        R[i, i + 32] = 1.0
    c["rot64_bf"] = _bf(R)
    c["fconsts"] = np.tile(np.array([[E_LN, E_RMS, SM_SHIFT, INV_SQ, -0.5, -1.0, 1.0, 0.0]],
                                    dtype=np.float32), (128, 1))
    inv = 1.0 / (10000.0 ** (np.arange(0, DR, 2, dtype=np.float64) / DR))
    tpos = np.arange(T, dtype=np.float64)
    f = np.outer(tpos, inv)
    emb = np.concatenate([f, f], axis=-1)
    c["ropeC"] = np.ascontiguousarray(np.cos(emb).T.astype(np.float32))
    c["ropeS"] = np.ascontiguousarray(np.sin(emb).T.astype(np.float32))
    return c


def host_prepare(inputs):
    g = host_constants()
    g["wxT"] = np.ascontiguousarray(np.asarray(inputs["L0_attn_a_W_xproj"], np.float32).T)
    al = np.asarray(inputs["L0_attn_a_A_log"], np.float32).reshape(-1)
    g["a_log_cols"] = np.ascontiguousarray(al.reshape(256, 128).T)
    g["d_cols"] = np.ascontiguousarray(
        np.asarray(inputs["L0_attn_a_D"], np.float32).reshape(16, 128).T)
    g["qaT"] = np.ascontiguousarray(np.asarray(inputs["L0_attn_b_q_a_w"], np.float32).T)
    g["qa_lnw_cols"] = np.ascontiguousarray(
        np.asarray(inputs["L0_attn_b_q_a_ln_w"], np.float32).reshape(12, 128).T)
    qb = np.asarray(inputs["L0_attn_b_q_b_w"], np.float32).reshape(NH, QHD, QLR)
    nope = qb[:, :DN, :].reshape(NH * DN, QLR)
    pe = qb[:, DN:, :].reshape(NH * DR, QLR)
    g["qbT"] = np.ascontiguousarray(np.concatenate([nope, pe], 0).T)
    g["kvaT"] = np.ascontiguousarray(np.asarray(inputs["L0_attn_b_kv_a_w"], np.float32).T)
    g["kva_lnw_cols"] = np.ascontiguousarray(
        np.asarray(inputs["L0_attn_b_kv_a_ln_w"], np.float32).reshape(4, 128).T)
    kvb = np.asarray(inputs["L0_attn_b_kv_b_w"], np.float32).reshape(NH, DN + DV, KVLR)
    knope = kvb[:, :DN, :].reshape(NH * DN, KVLR)
    vrows = kvb[:, DN:, :].reshape(NH * DV, KVLR)
    g["kvbT"] = np.ascontiguousarray(np.concatenate([knope, vrows], 0).T)
    g["owT"] = np.ascontiguousarray(np.asarray(inputs["L0_attn_b_o_w"], np.float32).T)
    g["fWT"] = np.ascontiguousarray(np.asarray(inputs["L0_ffn_W"], np.float32).T)
    g["fb_cols"] = np.ascontiguousarray(
        np.asarray(inputs["L0_ffn_b"], np.float32).reshape(16, 128).T)
    lA = np.asarray(inputs["L0_ffn_lora_A"], np.float32).reshape(8 * 16, 2048)
    gatew = np.asarray(inputs["L0_ffn_gate_w"], np.float32)
    g["flAT"] = np.ascontiguousarray(np.concatenate([lA, gatew], 0).T)
    g["gate_b_col"] = np.ascontiguousarray(
        np.asarray(inputs["L0_ffn_gate_b"], np.float32).reshape(8, 1))
    g["flBT"] = np.ascontiguousarray(
        np.asarray(inputs["L0_ffn_lora_B"], np.float32).transpose(0, 2, 1).reshape(128, 2048))
    return g


GLOBAL_INPUTS = {
    "wxT": (2048, 2080), "a_log_cols": (128, 256), "d_cols": (128, 16),
    "qaT": (2048, 1536), "qa_lnw_cols": (128, 12), "qbT": (1536, 3072),
    "kvaT": (2048, 576), "kva_lnw_cols": (128, 4), "kvbT": (512, 4096),
    "owT": (2048, 2048),
    "fWT": (2048, 2048), "fb_cols": (128, 16), "flAT": (2048, 136),
    "gate_b_col": (8, 1), "flBT": (128, 2048),
    "ident_f32": (128, 128), "ones_row": (1, 128), "ones_col_f": (128, 1),
    "ones8_f": (128, 8),
    "sel_repl": (128, 2048), "sel16": (16, 128), "sel_er": (8, 128),
    "ropeC": (64, T), "ropeS": (64, T), "fconsts": (128, 8),
}
GLOBAL_INPUTS_BF = {
    "ident_bf": (128, 128), "tsel_bf": (128, 2048), "tril_bf": (128, 128),
    "ones_col_bf": (128, 1), "rot64_bf": (64, 64),
}
SCRATCH = {
    "resid0": (128, NC * T), "resid1": (128, NC * T), "resid2": (128, NC * T),
    "resid3": (128, NC * T), "resid4": (128, NC * T),
}
SCRATCH_R = {
    "dtT_sc": (128, NC * T), "wmT_sc": (128, NC * T),
}
SCRATCH_BF = {
    "xn1": (128, NC * T), "xn2": (128, NC * T), "xn3": (128, NC * T),
    "xn4": (128, NC * T),
    "qn_sc": (128, NH * T), "qpe_sc": (64, NH * T),
    "kn_sc": (128, NH * T), "kpe_sc": (64, T),
    "v_sc": (128, NH * T), "atoT_sc": (128, NH * T),
}


def emit_transpose_in(P, x0, resid0):
    block, nc = P.block, P.nc
    with ExitStack() as st:
        E = st.enter_context
        s_ld, s_tr, s_ev, s_st = (P.sem() for _ in range(4))
        xbuf = [E(nc.sbuf_tensor("ti_x%d" % j, [128, D], F32)) for j in range(2)]
        obuf = [E(nc.sbuf_tensor("ti_o%d" % j, [128, NC * 128], F32)) for j in range(2)]
        pt = [E(nc.psum_tensor("ti_p%d" % j, [128, 128], F32)) for j in range(2)]
        NTB = T // 128

        @block.sync
        def _(sync):
            for tb in range(NTB):
                if tb >= 2:
                    sync.wait_ge(s_tr.h, (tb - 1) * NC)
                sync.dma_start(xbuf[tb % 2][:], x0[tb * 128:(tb + 1) * 128, :]) \
                    .then_inc(s_ld.h, 16)
                if tb >= 1:
                    sync.wait_ge(s_ev.h, tb * NC)
                    dst = r3(resid0)[:, :, (tb - 1) * 128: tb * 128]
                    sync.dma_start(dst, r3(obuf[(tb - 1) % 2].ap(), c=NC)) \
                        .then_inc(s_st.h, 16)
            sync.wait_ge(s_ev.h, NTB * NC)
            dst = r3(resid0)[:, :, (NTB - 1) * 128: NTB * 128]
            sync.dma_start(dst, r3(obuf[(NTB - 1) % 2].ap(), c=NC)).then_inc(s_st.h, 16)
            sync.wait_ge(s_st.h, NTB * 16)

        @block.tensor
        def _(tensor):
            for tb in range(NTB):
                tensor.wait_ge(s_ld.h, (tb + 1) * 16)
                for dc in range(NC):
                    k = tb * NC + dc
                    if k >= 2:
                        tensor.wait_ge(s_ev.h, k - 1)
                    tensor.transpose(pt[k % 2][:],
                                     xbuf[tb % 2][:, dc * 128:(dc + 1) * 128],
                                     P.ident_f[:]).then_inc(s_tr.h, 1)

        @block.scalar
        def _(scalar):
            for tb in range(NTB):
                if tb >= 2:
                    scalar.wait_ge(s_st.h, (tb - 1) * 16)
                for dc in range(NC):
                    k = tb * NC + dc
                    scalar.wait_ge(s_tr.h, k + 1)
                    scalar.activation(obuf[tb % 2][:, dc * 128:(dc + 1) * 128],
                                      pt[k % 2][:], AF.Copy).then_inc(s_ev.h, 1)


def emit_transpose_out(P, resid, out):
    block, nc = P.block, P.nc
    with ExitStack() as st:
        E = st.enter_context
        s_ld, s_tr, s_ev, s_st = (P.sem() for _ in range(4))
        xbuf = [E(nc.sbuf_tensor("to_x%d" % j, [128, NC * 128], F32)) for j in range(2)]
        obuf = [E(nc.sbuf_tensor("to_o%d" % j, [128, D], F32)) for j in range(2)]
        pt = [E(nc.psum_tensor("to_p%d" % j, [128, 128], F32)) for j in range(2)]
        NTB = T // 128

        @block.sync
        def _(sync):
            for tb in range(NTB):
                if tb >= 2:
                    sync.wait_ge(s_tr.h, (tb - 1) * NC)
                src = r3(resid)[:, :, tb * 128:(tb + 1) * 128]
                sync.dma_start(r3(xbuf[tb % 2].ap(), c=NC), src).then_inc(s_ld.h, 16)
                if tb >= 1:
                    sync.wait_ge(s_ev.h, tb * NC)
                    sync.dma_start(out[(tb - 1) * 128: tb * 128, :],
                                   obuf[(tb - 1) % 2][:]).then_inc(s_st.h, 16)
            sync.wait_ge(s_ev.h, NTB * NC)
            sync.dma_start(out[(NTB - 1) * 128: NTB * 128, :],
                           obuf[(NTB - 1) % 2][:]).then_inc(s_st.h, 16)
            sync.wait_ge(s_st.h, NTB * 16)

        @block.tensor
        def _(tensor):
            for tb in range(NTB):
                tensor.wait_ge(s_ld.h, (tb + 1) * 16)
                for dc in range(NC):
                    k = tb * NC + dc
                    if k >= 2:
                        tensor.wait_ge(s_ev.h, k - 1)
                    tensor.transpose(pt[k % 2][:],
                                     xbuf[tb % 2][:, dc * 128:(dc + 1) * 128],
                                     P.ident_f[:]).then_inc(s_tr.h, 1)

        @block.scalar
        def _(scalar):
            for tb in range(NTB):
                if tb >= 2:
                    scalar.wait_ge(s_st.h, (tb - 1) * 16)
                for dc in range(NC):
                    k = tb * NC + dc
                    scalar.wait_ge(s_tr.h, k + 1)
                    scalar.activation(obuf[tb % 2][:, dc * 128:(dc + 1) * 128],
                                      pt[k % 2][:], AF.Copy).then_inc(s_ev.h, 1)


def emit_ln(P, resid, xn, tag):
    block, nc = P.block, P.nc
    with ExitStack() as st:
        E = st.enter_context
        s_ld, s_cst, s_sq, s_stat, s_sm1, s_rstd, s_sm2, s_rep, s_app, s_st = \
            (P.sem() for _ in range(10))
        xbuf = [E(nc.sbuf_tensor(tag + "_x%d" % j, [128, NC * TS], F32)) for j in range(2)]
        xb16 = E(nc.sbuf_tensor(tag + "_xb", [128, NC * TS], BF16))
        xsq = E(nc.sbuf_tensor(tag + "_sq", [128, NC * TS], BF16))
        onb = E(nc.sbuf_tensor(tag + "_onb", [128, NC * TS], BF16))
        tmp = E(nc.sbuf_tensor(tag + "_tmp", [128, TS], F32))
        small = E(nc.sbuf_tensor(tag + "_small", [1, 5 * TS], F32))
        pm = E(nc.psum_tensor(tag + "_pm", [128, TS], F32))
        pq = E(nc.psum_tensor(tag + "_pq", [128, TS], F32))
        prep = E(nc.psum_tensor(tag + "_prep", [128, 2 * TS], F32))

        def C(dc):
            return slice(dc * TS, (dc + 1) * TS)

        M = small[0:1, 0:TS]
        MSQ = small[0:1, TS:2 * TS]
        VAR = small[0:1, 2 * TS:3 * TS]
        RSTD = small[0:1, 3 * TS:4 * TS]
        NMR = small[0:1, 4 * TS:5 * TS]

        @block.sync
        def _(sync):
            for s in range(NS):
                if s >= 2:
                    sync.wait_ge(s_app.h, (s - 1) * 2 * NC)
                src = r3(resid)[:, :, s * TS:(s + 1) * TS]
                sync.dma_start(r3(xbuf[s % 2].ap(), c=NC), src).then_inc(s_ld.h, 16)
                sync.wait_ge(s_app.h, (s + 1) * 2 * NC)
                dst = r3(xn)[:, :, s * TS:(s + 1) * TS]
                sync.dma_start(dst, r3(onb.ap(), c=NC)).then_inc(s_st.h, 16)
            sync.wait_ge(s_st.h, NS * 16)

        @block.scalar
        def _(scalar):
            for s in range(NS):
                if s >= 1:
                    scalar.wait_ge(s_stat.h, s * 2 * NC)
                for dc in range(NC):
                    scalar.wait_ge(s_cst.h, s * NC + dc + 1)
                    scalar.activation(xsq[:, C(dc)], xb16[:, C(dc)],
                                      AF.Square).then_inc(s_sq.h, 1)
                scalar.wait_ge(s_sm1.h, s + 1)
                if s >= 1:
                    scalar.wait_ge(s_rep.h, 2 * s)
                scalar.activation(RSTD, VAR, AF.Ln, bias=E_LN)
                scalar.activation(RSTD, RSTD, AF.Exp, scale=-0.5).then_inc(s_rstd.h, 1)

        @block.tensor
        def _(tensor):
            for s in range(NS):
                tensor.wait_ge(s_ld.h, (s + 1) * 16)
                if s >= 1:
                    tensor.wait_ge(s_sm1.h, s)
                for dc in range(NC):
                    tensor.wait_ge(s_cst.h, s * NC + dc + 1)
                    tensor.wait_ge(s_sq.h, s * NC + dc + 1)
                    tensor.matmul(pm[0:1, :], P.ones_col_b[:, 0:1],
                                  xb16[:, C(dc)],
                                  start=(dc == 0), stop=(dc == NC - 1))
                    tensor.matmul(pq[0:1, :], P.ones_col_b[:, 0:1],
                                  xsq[:, C(dc)],
                                  start=(dc == 0), stop=(dc == NC - 1)) \
                        .then_inc(s_stat.h, 2)
                tensor.wait_ge(s_sm2.h, s + 1)
                if s >= 1:
                    tensor.wait_ge(s_app.h, s * 2 * NC)
                tensor.matmul(prep[:, 0:TS], P.ones_row[0:1, :], RSTD,
                              start=True, stop=True)
                tensor.matmul(prep[:, TS:2 * TS], P.ones_row[0:1, :], NMR,
                              start=True, stop=True).then_inc(s_rep.h, 2)

        @block.vector
        def _(vector):
            for s in range(NS):
                vector.wait_ge(s_ld.h, (s + 1) * 16)
                if s >= 1:
                    vector.wait_ge(s_stat.h, s * 2 * NC)
                for dc in range(NC):
                    vector.tensor_copy(xb16[:, C(dc)], xbuf[s % 2][:, C(dc)]) \
                        .then_inc(s_cst.h, 1)
                vector.wait_ge(s_stat.h, (s + 1) * 2 * NC)
                vector.tensor_scalar(M, pm[0:1, :], 1.0 / D, None, Op.mult)
                vector.tensor_scalar(MSQ, pq[0:1, :], 1.0 / D, None, Op.mult)
                vector.tensor_tensor(VAR, M, M, Op.mult)
                vector.tensor_tensor(VAR, MSQ, VAR, Op.subtract).then_inc(s_sm1.h, 1)
                vector.wait_ge(s_rstd.h, s + 1)
                if s >= 1:
                    vector.wait_ge(s_rep.h, 2 * s)
                vector.tensor_tensor(NMR, M, RSTD, Op.mult)
                vector.tensor_scalar(NMR, NMR, -1.0, None, Op.mult).then_inc(s_sm2.h, 1)
                vector.wait_ge(s_rep.h, (s + 1) * 2)
                if s >= 1:
                    vector.wait_ge(s_st.h, s * 16)
                for dc in range(NC):
                    vector.tensor_tensor(tmp[:], xbuf[s % 2][:, C(dc)],
                                         prep[:, 0:TS], Op.mult).then_inc(s_app.h, 1)
                    vector.tensor_tensor(onb[:, C(dc)], tmp[:],
                                         prep[:, TS:2 * TS], Op.add).then_inc(s_app.h, 1)


def emit_cast_weights(P, src_dram, dst_sbuf, rows, cols, tag):
    block, nc = P.block, P.nc
    nchunk = rows // 128
    with ExitStack() as st:
        E = st.enter_context
        s_ld, s_cv = P.sem(), P.sem()
        stage = [E(nc.sbuf_tensor(tag + "_wst%d" % j, [128, cols], F32)) for j in range(2)]

        @block.sync
        def _(sync):
            for ch in range(nchunk):
                if ch >= 2:
                    sync.wait_ge(s_cv.h, ch - 1)
                sync.dma_start(stage[ch % 2][:], src_dram[ch * 128:(ch + 1) * 128, :]) \
                    .then_inc(s_ld.h, 16)
            sync.wait_ge(s_cv.h, nchunk)

        @block.scalar
        def _(scalar):
            for ch in range(nchunk):
                scalar.wait_ge(s_ld.h, (ch + 1) * 16)
                scalar.activation(dst_sbuf[:, ch * cols:(ch + 1) * cols],
                                  stage[ch % 2][:], AF.Copy).then_inc(s_cv.h, 1)


def emit_xproj(P, dins, dsc, tag="m1"):
    block, nc = P.block, P.nc
    with ExitStack() as st:
        E = st.enter_context
        wxb = E(nc.sbuf_tensor("wxb", [128, NC * 2080], BF16))
        emit_cast_weights(P, dins["wxT"], wxb, 2048, 2080, tag)
        P.barrier(tag + "w")

        s_ld, s_mm, s_sp, s_dw, s_st = (P.sem() for _ in range(5))
        xnb = [E(nc.sbuf_tensor(tag + "_xn%d" % j, [128, NC * TS], BF16)) for j in range(2)]
        pxp = [E(nc.psum_tensor(tag + "_p%d" % j, [128, TS], F32)) for j in range(2)]
        sp1 = [E(nc.sbuf_tensor(tag + "_s1%d" % j, [128, TS], F32)) for j in range(2)]
        sp2 = [E(nc.sbuf_tensor(tag + "_s2%d" % j, [128, TS], F32)) for j in range(2)]
        dtb = [E(nc.sbuf_tensor(tag + "_dt%d" % j, [128, TS], F32)) for j in range(2)]
        dtr = [E(nc.sbuf_tensor(tag + "_dr%d" % j, [128, TS], F32R)) for j in range(2)]
        wtb = [E(nc.sbuf_tensor(tag + "_wt%d" % j, [128, TS], F32R)) for j in range(2)]
        NM = 18

        def wcols(m):
            if m < 16:
                return (m * 128, (m + 1) * 128)
            if m == 16:
                return (2048, 2064)
            return (2064, 2080)

        @block.sync
        def _(sync):
            for s in range(NS):
                if s >= 2:
                    sync.wait_ge(s_mm.h, (s - 1) * NM)
                    sync.wait_ge(s_dw.h, (s - 1) * 48)
                src = r3(dsc["xn1"])[:, :, s * TS:(s + 1) * TS]
                sync.dma_start(r3(xnb[s % 2].ap(), c=NC), src).then_inc(s_ld.h, 16)
                for m in range(16):
                    km = s * 16 + m
                    sync.wait_ge(s_dw.h, 3 * km + 3)
                    sync.dma_start(dsc["dtT_sc"][:, m * T + s * TS: m * T + (s + 1) * TS],
                                   dtr[km % 2][:]).then_inc(s_st.h, 16)
                    sync.dma_start(dsc["wmT_sc"][:, m * T + s * TS: m * T + (s + 1) * TS],
                                   wtb[km % 2][:]).then_inc(s_st.h, 16)
            sync.wait_ge(s_st.h, NS * 32 * 16)

        @block.tensor
        def _(tensor):
            for s in range(NS):
                tensor.wait_ge(s_ld.h, (s + 1) * 16)
                for m in range(NM):
                    k = s * NM + m
                    if k >= 2:
                        tensor.wait_ge(s_sp.h, k - 1)
                    c0, c1 = wcols(m)
                    rows = 128 if m < 16 else 16
                    for dc in range(NC):
                        ins = tensor.matmul(pxp[k % 2][0:rows, :],
                                            wxb[:, dc * 2080 + c0: dc * 2080 + c1],
                                            xnb[s % 2][:, dc * TS:(dc + 1) * TS],
                                            start=(dc == 0), stop=(dc == NC - 1))
                    ins.then_inc(s_mm.h, 1)

        @block.scalar
        def _(scalar):
            for s in range(NS):
                for m in range(NM):
                    k = s * NM + m
                    scalar.wait_ge(s_mm.h, k + 1)
                    if m < 16:
                        km = s * 16 + m
                        if km >= 2:
                            scalar.wait_ge(s_dw.h, 3 * km - 3)
                        scalar.activation(sp1[km % 2][:], pxp[k % 2][:], AF.Abs)
                        scalar.activation(sp1[km % 2][:], sp1[km % 2][:], AF.Exp, scale=-1.0)
                        scalar.activation(sp1[km % 2][:], sp1[km % 2][:], AF.Ln, bias=1.0)
                        scalar.activation(sp2[km % 2][:], pxp[k % 2][:], AF.Relu) \
                            .then_inc(s_sp.h, 1)
                    elif m == 16:
                        scalar.activation(P.btT[:, s * TS:(s + 1) * TS],
                                          pxp[k % 2][0:16, :], AF.Copy).then_inc(s_sp.h, 1)
                    else:
                        scalar.activation(P.ctT[:, s * TS:(s + 1) * TS],
                                          pxp[k % 2][0:16, :], AF.Copy).then_inc(s_sp.h, 1)

        @block.vector
        def _(vector):
            for s in range(NS):
                vector.wait_ge(s_ld.h, (s + 1) * 16)
                for m in range(16):
                    k = s * NM + m
                    km = s * 16 + m
                    vector.wait_ge(s_sp.h, k + 1)
                    if km >= 2:
                        vector.wait_ge(s_st.h, (km - 1) * 32)
                    vector.tensor_tensor(dtb[km % 2][:], sp1[km % 2][:],
                                         sp2[km % 2][:], Op.add).then_inc(s_dw.h, 1)
                    vector.tensor_tensor(wtb[km % 2][:], dtb[km % 2][:],
                                         xnb[s % 2][:, m * TS:(m + 1) * TS], Op.mult) \
                        .then_inc(s_dw.h, 1)
                    vector.tensor_copy(dtr[km % 2][:], dtb[km % 2][:]) \
                        .then_inc(s_dw.h, 1)


def emit_bcpat(P, tag="m2"):
    block, nc = P.block, P.nc
    with ExitStack() as st:
        E = st.enter_context
        s_mm, s_ev = P.sem(), P.sem()
        pp = [E(nc.psum_tensor(tag + "_p%d" % j, [128, TS], F32)) for j in range(2)]

        @block.tensor
        def _(tensor):
            for s in range(NS):
                for j in range(2):
                    src = P.btT if j == 0 else P.ctT
                    k = s * 2 + j
                    if k >= 2:
                        tensor.wait_ge(s_ev.h, k - 1)
                    tensor.matmul(pp[k % 2][:], P.sel16[0:16, :],
                                  src[:, s * TS:(s + 1) * TS],
                                  start=True, stop=True).then_inc(s_mm.h, 1)

        @block.vector
        def _(vector):
            for s in range(NS):
                for j in range(2):
                    dst = P.bpat if j == 0 else P.cpat
                    k = s * 2 + j
                    vector.wait_ge(s_mm.h, k + 1)
                    vector.tensor_copy(dst[:, s * TS:(s + 1) * TS], pp[k % 2][:]) \
                        .then_inc(s_ev.h, 1)


def emit_scan(P, dsc, tag="m3"):
    block, nc = P.block, P.nc
    HT = T // 2
    with ExitStack() as st:
        E = st.enter_context
        s_ld = P.sem("m3ld")
        s_md = P.sem("m3md")
        s_mw = P.sem("m3mw")
        s_a = P.sem("m3a")
        s_u = P.sem("m3u")
        s_sc = P.sem("m3sc")
        s_po = P.sem("m3po")
        s_sum = P.sem("m3sum")
        s_yf = P.sem("m3yf")
        s_st = P.sem("m3st")

        dtc = E(nc.sbuf_tensor("m3_dt", [128, T], F32R))
        wc = E(nc.sbuf_tensor("m3_w", [128, T], F32R))
        xnc = E(nc.sbuf_tensor("m3_xn", [128, T], BF16))
        rc = [E(nc.sbuf_tensor("m3_rc%d" % j, [128, T], F32)) for j in range(2)]
        abuf = [E(nc.sbuf_tensor("m3_a%d" % j, [128, HT], F32)) for j in range(3)]
        ubuf = [E(nc.sbuf_tensor("m3_u%d" % j, [128, HT], F32)) for j in range(2)]
        hbuf = [E(nc.sbuf_tensor("m3_h%d" % j, [128, HT], F32)) for j in range(3)]
        hp = [E(nc.sbuf_tensor("m3_hp%d" % j, [128, HT], BF16)) for j in range(3)]
        carry = E(nc.sbuf_tensor("m3_carry", [128, 16], F32))
        yt = E(nc.sbuf_tensor("m3_yt", [128, TS], F32))
        yo = [E(nc.sbuf_tensor("m3_yo%d" % j, [128, TS], F32)) for j in range(2)]
        pd = E(nc.psum_tensor("m3_pd", [128, 2 * TS], F32))
        pw = E(nc.psum_tensor("m3_pw", [128, 2 * TS], F32))
        py = [E(nc.psum_tensor("m3_py%d" % j, [128, TS], F32)) for j in range(4)]

        @block.sync
        def _(sync):
            for c in range(16):
                if c >= 1:
                    sync.wait_ge(s_yf.h, c * 8)
                sync.dma_start(xnc[:], dsc["xn1"][:, c * T:(c + 1) * T]) \
                    .then_inc(s_ld.h, 16)
                sync.dma_start(rc[c % 2][:], dsc["resid0"][:, c * T:(c + 1) * T]) \
                    .then_inc(s_ld.h, 16)
                if c >= 1:
                    sync.wait_ge(s_md.h, 128 * c)
                    sync.wait_ge(s_mw.h, 128 * c)
                sync.dma_start(dtc[:], dsc["dtT_sc"][:, c * T:(c + 1) * T]) \
                    .then_inc(s_ld.h, 16)
                sync.dma_start(wc[:], dsc["wmT_sc"][:, c * T:(c + 1) * T]) \
                    .then_inc(s_ld.h, 16)
                for hf in range(2):
                    g = c * 2 + hf
                    for sl in range(4):
                        sync.wait_ge(s_yf.h, g * 4 + sl + 1)
                        t0 = c * T + hf * HT + sl * TS
                        sync.dma_start(dsc["resid1"][:, t0:t0 + TS],
                                       yo[(g * 4 + sl) % 2][:]).then_inc(s_st.h, 16)
            sync.wait_ge(s_st.h, 16 * 128)

        def sum_mms(tensor, g, i, half2):
            if half2 < 0:
                return
            j = g * 16 + i
            tensor.wait_ge(s_po.h, 2 * j + half2 + 1)
            for sl in (2 * half2, 2 * half2 + 1):
                if i == 0 and g >= 1:
                    tensor.wait_ge(s_yf.h, (g - 1) * 4 + sl + 1)
                tensor.matmul(py[sl][:], P.tsel_b[:, i * 128:(i + 1) * 128],
                              hp[i % 3][:, sl * TS:(sl + 1) * TS],
                              start=(i == 0), stop=(i == 15)).then_inc(s_sum.h, 1)

        @block.tensor
        def _(tensor):
            for c in range(16):
                tensor.wait_ge(s_ld.h, (c + 1) * 64)
                for hf in range(2):
                    g = c * 2 + hf
                    for i in range(16):
                        j = g * 16 + i
                        selL = P.sel_replr[:, i * 128:(i + 1) * 128]
                        for half2 in range(2):
                            for sl in (2 * half2, 2 * half2 + 1):
                                t0 = hf * HT + sl * TS
                                if j >= 1 or sl >= 2:
                                    tensor.wait_ge(s_a.h, 2 * j + sl // 2)
                                tensor.matmul(pd[:, (sl % 2) * TS:(sl % 2 + 1) * TS],
                                              selL,
                                              dtc[:, t0:t0 + TS],
                                              start=True, stop=True).then_inc(s_md.h, 1)
                            for sl in (2 * half2, 2 * half2 + 1):
                                t0 = hf * HT + sl * TS
                                if j >= 1 or sl >= 2:
                                    tensor.wait_ge(s_u.h, 2 * j + sl // 2)
                                tensor.matmul(pw[:, (sl % 2) * TS:(sl % 2 + 1) * TS],
                                              selL,
                                              wc[:, t0:t0 + TS],
                                              start=True, stop=True).then_inc(s_mw.h, 1)
                            if i >= 2:
                                sum_mms(tensor, g, i - 2, half2)
                    for it in (14, 15):
                        sum_mms(tensor, g, it, 0)
                        sum_mms(tensor, g, it, 1)

        @block.scalar
        def _(scalar):
            for c in range(16):
                for hf in range(2):
                    g = c * 2 + hf
                    for i in range(16):
                        j = g * 16 + i
                        acol = P.a_cols[:, 16 * c + i:16 * c + i + 1]
                        scalar.wait_ge(s_md.h, j * 4 + 2)
                        if j >= 3:
                            scalar.wait_ge(s_sc.h, j - 2)
                        scalar.activation(abuf[i % 3][:, 0:2 * TS], pd[:, 0:2 * TS],
                                          AF.Exp, scale=acol).then_inc(s_a.h, 1)
                        scalar.wait_ge(s_md.h, j * 4 + 4)
                        scalar.activation(abuf[i % 3][:, 2 * TS:4 * TS], pd[:, 0:2 * TS],
                                          AF.Exp, scale=acol).then_inc(s_a.h, 1)

        @block.vector
        def _(vector):
            for c in range(16):
                for hf in range(2):
                    g = c * 2 + hf
                    for i in range(16):
                        j = g * 16 + i
                        vector.wait_ge(s_mw.h, j * 4 + 2)
                        vector.tensor_tensor(ubuf[i % 2][:, 0:2 * TS], pw[:, 0:2 * TS],
                                             P.bpat[:, hf * HT:hf * HT + 2 * TS],
                                             Op.mult).then_inc(s_u.h, 1)
                        vector.wait_ge(s_mw.h, j * 4 + 4)
                        vector.tensor_tensor(ubuf[i % 2][:, 2 * TS:4 * TS],
                                             pw[:, 0:2 * TS],
                                             P.bpat[:, hf * HT + 2 * TS:(hf + 1) * HT],
                                             Op.mult).then_inc(s_u.h, 1)
                        vector.wait_ge(s_a.h, 2 * j + 2)
                        if j >= 3:
                            vector.wait_ge(s_po.h, 2 * j - 4)
                        init = 0.0 if hf == 0 else carry[:, i:i + 1]
                        vector.tensor_tensor_scan(hbuf[i % 3][:], abuf[i % 3][:],
                                                  ubuf[i % 2][:], init,
                                                  Op.mult, Op.add).then_inc(s_sc.h, 1)
                        if hf == 0:
                            vector.wait_ge(s_sc.h, j + 1)
                            vector.tensor_copy(carry[:, i:i + 1],
                                               hbuf[i % 3][:, HT - 1:HT])
                    for sl in range(4):
                        k = g * 4 + sl
                        vector.wait_ge(s_sum.h, g * 64 + 60 + sl + 1)
                        t0 = hf * HT + sl * TS
                        vector.scalar_tensor_tensor(
                            yt[:], xnc[:, t0:t0 + TS], P.d_cols[:, c:c + 1],
                            py[sl][:], Op.mult, Op.add)
                        if k >= 2:
                            vector.wait_ge(s_st.h, (k - 1) * 16)
                        vector.tensor_tensor(yo[k % 2][:], yt[:],
                                             rc[c % 2][:, t0:t0 + TS], Op.add) \
                            .then_inc(s_yf.h, 1)

        @block.gpsimd
        def _(gpsimd):
            for c in range(16):
                for hf in range(2):
                    g = c * 2 + hf
                    for i in range(16):
                        j = g * 16 + i
                        gpsimd.wait_ge(s_sc.h, j + 1)
                        if j >= 3:
                            gpsimd.wait_ge(s_sum.h, (j - 2) * 4)
                        HH = HT // 2
                        gpsimd.tensor_tensor(hp[i % 3][:, 0:HH], hbuf[i % 3][:, 0:HH],
                                             P.cpat[:, hf * HT: hf * HT + HH],
                                             Op.mult).then_inc(s_po.h, 1)
                        gpsimd.tensor_tensor(hp[i % 3][:, HH:HT], hbuf[i % 3][:, HH:HT],
                                             P.cpat[:, hf * HT + HH:(hf + 1) * HT],
                                             Op.mult).then_inc(s_po.h, 1)


FFN_LORA = True
FFN_DBG = False


def emit_ffn(P, dins, xn, resid_in, resid_out, tag):
    block, nc = P.block, P.nc
    with ExitStack() as st:
        E = st.enter_context
        fwb = E(nc.sbuf_tensor(tag + "_fwb", [128, NC * 2048], BF16))
        flab = E(nc.sbuf_tensor(tag + "_flab", [128, NC * 136], BF16))
        flbb = E(nc.sbuf_tensor(tag + "_flbb", [128, 2048], BF16))
        emit_cast_weights(P, dins["fWT"], fwb, 2048, 2048, tag + "a")
        emit_cast_weights(P, dins["flAT"], flab, 2048, 136, tag + "b")
        emit_cast_weights(P, dins["flBT"], flbb, 128, 2048, tag + "c")
        P.barrier(tag + "w")

        s_ld, s_mg, s_lg, s_gb, s_t1, s_gd, s_t2, s_wt, s_wp, s_xa, s_z, \
            s_mo, s_ev, s_st, s_gs = (P.sem() for _ in range(15))

        xnb = [E(nc.sbuf_tensor(tag + "_xn%d" % j, [128, NC * TS], BF16)) for j in range(2)]
        rbuf = [E(nc.sbuf_tensor(tag + "_r%d" % j, [128, NC * TS], F32)) for j in range(2)]
        lg = E(nc.sbuf_tensor(tag + "_lg", [8, TS], F32))
        lgt = E(nc.sbuf_tensor(tag + "_lgt", [128, 32], F32))
        lmm = E(nc.sbuf_tensor(tag + "_lmm", [128, 32], F32))
        mk1 = E(nc.sbuf_tensor(tag + "_mk1", [128, 32], F32))
        m1c = E(nc.sbuf_tensor(tag + "_m1c", [128, 4], F32))
        m2c = E(nc.sbuf_tensor(tag + "_m2c", [128, 4], F32))
        e2c = E(nc.sbuf_tensor(tag + "_e2c", [128, 4], F32))
        w1c = E(nc.sbuf_tensor(tag + "_w1c", [128, 4], F32))
        w2c = E(nc.sbuf_tensor(tag + "_w2c", [128, 4], F32))
        wv = E(nc.sbuf_tensor(tag + "_wv", [128, 32], F32))
        wT = E(nc.sbuf_tensor(tag + "_wT", [8, TS], F32))
        xat = E(nc.sbuf_tensor(tag + "_xat", [128, TS], BF16))
        zt = E(nc.sbuf_tensor(tag + "_zt", [128, TS], BF16))
        ot = [E(nc.sbuf_tensor(tag + "_ot%d" % j, [128, TS], F32)) for j in range(2)]
        pxa = E(nc.psum_tensor(tag + "_pxa", [128, TS], F32))
        pg = E(nc.psum_tensor(tag + "_pg", [8, TS], F32))
        ptg = E(nc.psum_tensor(tag + "_ptg", [128, 32], F32))
        ptw = E(nc.psum_tensor(tag + "_ptw", [8, TS], F32))
        pwp = E(nc.psum_tensor(tag + "_pwp", [128, TS], F32))
        pbase = [E(nc.psum_tensor(tag + "_pb%d" % j, [128, TS], F32)) for j in range(2)]

        @block.sync
        def _(sync):
            for s in range(NS):
                if s >= 2:
                    sync.wait_ge(s_mo.h, (s - 1) * NC)
                    sync.wait_ge(s_ev.h, (s - 1) * NC)
                src = r3(xn)[:, :, s * TS:(s + 1) * TS]
                sync.dma_start(r3(xnb[s % 2].ap(), c=NC), src).then_inc(s_ld.h, 16)
                src2 = r3(resid_in)[:, :, s * TS:(s + 1) * TS]
                sync.dma_start(r3(rbuf[s % 2].ap(), c=NC), src2).then_inc(s_ld.h, 16)
                if FFN_DBG and tag == "f1":
                    sync.wait_ge(s_z.h, s + 1)
                    sync.dma_start(P.dsc["dbg_lgt"][:, s * 32:(s + 1) * 32], lgt[:]).then_inc(s_st.h, 16)
                    sync.dma_start(P.dsc["dbg_wv"][:, s * 32:(s + 1) * 32], wv[:]).then_inc(s_st.h, 16)
                    sync.dma_start(P.dsc["dbg_wT"][0:8, s * TS:(s + 1) * TS], wT[0:8, :]).then_inc(s_st.h, 16)
                    sync.dma_start(P.dsc["dbg_zt"][:, s * TS:(s + 1) * TS], zt[:]).then_inc(s_st.h, 16)
                    sync.dma_start(P.dsc["dbg_xat"][:, s * TS:(s + 1) * TS], xat[:]).then_inc(s_st.h, 16)
                for oc in range(NC):
                    ko = s * NC + oc
                    sync.wait_ge(s_ev.h, ko + 1)
                    dst = resid_out[:, oc * T + s * TS: oc * T + (s + 1) * TS]
                    sync.dma_start(dst, ot[ko % 2][:]).then_inc(s_st.h, 16)
            sync.wait_ge(s_st.h, NS * NC * 16 + (NS * 80 if (FFN_DBG and tag == "f1") else 0))

        @block.tensor
        def _(tensor):
            for s in range(NS):
                tensor.wait_ge(s_ld.h, (s + 1) * 32)
                if s >= 1:
                    tensor.wait_ge(s_xa.h, s)
                    tensor.wait_ge(s_lg.h, 2 * s - 1)
                for dc in range(NC):
                    tensor.matmul(pxa[:], flab[:, dc * 136:dc * 136 + 128],
                                  xnb[s % 2][:, dc * TS:(dc + 1) * TS],
                                  start=(dc == 0), stop=(dc == NC - 1))
                for dc in range(NC):
                    ins = tensor.matmul(pg[0:8, :], flab[:, dc * 136 + 128:(dc + 1) * 136],
                                        xnb[s % 2][:, dc * TS:(dc + 1) * TS],
                                        start=(dc == 0), stop=(dc == NC - 1))
                ins.then_inc(s_mg.h, 1)
                tensor.wait_ge(s_gb.h, s + 1)
                if s >= 1:
                    tensor.wait_ge(s_gs.h, 36 * (s - 1) + 1)
                for tb in range(4):
                    tensor.transpose(ptg[:, tb * 8:(tb + 1) * 8],
                                     lg[0:8, tb * 128:(tb + 1) * 128],
                                     P.ident_f[0:8, 0:8]).then_inc(s_t1.h, 1)
                tensor.wait_ge(s_gs.h, 36 * s + 36)
                if s >= 1:
                    tensor.wait_ge(s_wt.h, s)
                for tb in range(4):
                    tensor.transpose(ptw[0:8, tb * 128:(tb + 1) * 128],
                                     wv[:, tb * 8:(tb + 1) * 8],
                                     P.ident_f[:]).then_inc(s_t2.h, 1)
                tensor.wait_ge(s_wt.h, s + 1)
                if s >= 1:
                    tensor.wait_ge(s_z.h, s)
                tensor.matmul(pwp[:], P.sel_er[0:8, :], wT[0:8, :],
                              start=True, stop=True).then_inc(s_wp.h, 1)
                for oc in range(NC):
                    ko = s * NC + oc
                    if ko >= 2:
                        tensor.wait_ge(s_ev.h, ko - 1)
                    for dc in range(NC):
                        ins = tensor.matmul(pbase[ko % 2][:],
                                      fwb[:, dc * 2048 + oc * 128: dc * 2048 + (oc + 1) * 128],
                                      xnb[s % 2][:, dc * TS:(dc + 1) * TS],
                                      start=(dc == 0), stop=(not FFN_LORA and dc == NC - 1))
                    if FFN_LORA:
                        if oc == 0:
                            tensor.wait_ge(s_z.h, s + 1)
                        tensor.matmul(pbase[ko % 2][:], flbb[:, oc * 128:(oc + 1) * 128],
                                      zt[:], start=False, stop=True).then_inc(s_mo.h, 1)
                    else:
                        ins.then_inc(s_mo.h, 1)

        @block.scalar
        def _(scalar):
            for s in range(NS):
                scalar.wait_ge(s_mg.h, s + 1)
                if s >= 1:
                    scalar.wait_ge(s_t1.h, 4 * s)
                scalar.activation(lg[0:8, :], pg[0:8, :], AF.Copy).then_inc(s_lg.h, 1)
                scalar.wait_ge(s_gs.h, 36 * s + 21)
                scalar.activation(e2c[:], m2c[:], AF.Exp).then_inc(s_lg.h, 1)
                scalar.wait_ge(s_t2.h, (s + 1) * 4)
                if s >= 1:
                    scalar.wait_ge(s_wp.h, s)
                scalar.activation(wT[0:8, :], ptw[0:8, :], AF.Copy).then_inc(s_wt.h, 1)
                scalar.wait_ge(s_mg.h, s + 1)
                if s >= 1:
                    scalar.wait_ge(s_z.h, s)
                scalar.activation(xat[:], pxa[:], AF.Copy).then_inc(s_xa.h, 1)

        @block.vector
        def _(vector):
            for s in range(NS):
                vector.wait_ge(s_lg.h, 2 * s + 1)
                vector.tensor_scalar(lg[0:8, :], lg[0:8, :], P.gate_b[0:8, 0:1],
                                     None, Op.add).then_inc(s_gb.h, 1)
                gctr = [36 * s]

                def gv(ins):
                    gctr[0] += 1
                    ins.then_inc(s_gs.h, 1)
                    vector.wait_ge(s_gs.h, gctr[0])

                vector.wait_ge(s_t1.h, (s + 1) * 4)
                gv(vector.tensor_copy(lgt[:], ptg[:]))
                for tb in range(4):
                    sl8 = slice(tb * 8, (tb + 1) * 8)
                    gv(vector.reduce_max(m1c[:, tb:tb + 1], lgt[:, sl8], axis=AX.X))
                for tb in range(4):
                    sl8 = slice(tb * 8, (tb + 1) * 8)
                    gv(vector.tensor_scalar(lgt[:, sl8], lgt[:, sl8], m1c[:, tb:tb + 1],
                                            None, Op.subtract))
                    gv(vector.tensor_scalar(mk1[:, sl8], lgt[:, sl8], 0.0, None, Op.is_ge))
                    gv(vector.scalar_tensor_tensor(lmm[:, sl8], mk1[:, sl8], -1e30,
                                                   lgt[:, sl8], Op.mult, Op.add))
                    gv(vector.reduce_max(m2c[:, tb:tb + 1], lmm[:, sl8], axis=AX.X))
                vector.wait_ge(s_lg.h, 2 * s + 2)
                if s >= 1:
                    vector.wait_ge(s_t2.h, 4 * s)
                gv(vector.tensor_scalar(e2c[:], e2c[:], 1.0, None, Op.add))
                gv(vector.reciprocal(w1c[:], e2c[:]))
                gv(vector.tensor_scalar(w2c[:], w1c[:], -1.0, 1.0, Op.mult, Op.add))
                for tb in range(4):
                    sl8 = slice(tb * 8, (tb + 1) * 8)
                    gv(vector.tensor_scalar(wv[:, sl8], lmm[:, sl8], m2c[:, tb:tb + 1],
                                            0.0, Op.subtract, Op.is_ge))
                    gv(vector.tensor_scalar(wv[:, sl8], wv[:, sl8], w2c[:, tb:tb + 1],
                                            None, Op.mult))
                    ins = vector.scalar_tensor_tensor(wv[:, sl8], mk1[:, sl8],
                                                      w1c[:, tb:tb + 1], wv[:, sl8],
                                                      Op.mult, Op.add)
                    if tb < 3:
                        gv(ins)
                    else:
                        gctr[0] += 1
                        ins.then_inc(s_gs.h, 1)
                vector.wait_ge(s_wp.h, s + 1)
                vector.wait_ge(s_xa.h, s + 1)
                if s >= 1:
                    vector.wait_ge(s_mo.h, s * NC)
                vector.tensor_tensor(zt[:], xat[:], pwp[:], Op.mult).then_inc(s_z.h, 1)
                for oc in range(NC):
                    ko = s * NC + oc
                    vector.wait_ge(s_mo.h, ko + 1)
                    if ko >= 2:
                        vector.wait_ge(s_st.h, (ko - 1) * 16)
                    vector.scalar_tensor_tensor(
                        ot[ko % 2][:], pbase[ko % 2][:], P.fb_cols[:, oc:oc + 1],
                        rbuf[s % 2][:, oc * TS:(oc + 1) * TS], Op.add, Op.add) \
                        .then_inc(s_ev.h, 1)


def emit_attn_q(P, dins, dsc, tag="aq"):
    block, nc = P.block, P.nc
    with ExitStack() as st:
        E = st.enter_context
        qab = E(nc.sbuf_tensor("qab", [128, NC * 1536], BF16))
        qbb = E(nc.sbuf_tensor("qbb", [128, 12 * 3072], BF16))
        emit_cast_weights(P, dins["qaT"], qab, 2048, 1536, tag + "a")
        emit_cast_weights(P, dins["qbT"], qbb, 1536, 3072, tag + "b")
        P.barrier(tag + "w")

        s_ld, s_mq, s_eq, s_ns, s_ms, s_rs, s_rp, s_qn, s_mh, s_eh, s_st = \
            (P.sem() for _ in range(11))
        xnb = [E(nc.sbuf_tensor(tag + "_xn%d" % j, [128, NC * TS], BF16)) for j in range(2)]
        qlf = E(nc.sbuf_tensor(tag + "_qlf", [128, 12 * TS], BF16))
        qsq = E(nc.sbuf_tensor(tag + "_qsq", [128, TS], BF16))
        qln = E(nc.sbuf_tensor(tag + "_qln", [128, 12 * TS], BF16))
        qot = [E(nc.sbuf_tensor(tag + "_qo%d" % j, [128, TS], BF16)) for j in range(2)]
        small = E(nc.sbuf_tensor(tag + "_small", [1, 2 * TS], F32))
        pql = [E(nc.psum_tensor(tag + "_pql%d" % j, [128, TS], F32)) for j in range(2)]
        pns = E(nc.psum_tensor(tag + "_pns", [128, TS], F32))
        prs = E(nc.psum_tensor(tag + "_prs", [128, TS], F32))
        pq = [E(nc.psum_tensor(tag + "_pq%d" % j, [128, TS], F32)) for j in range(2)]
        RSTD = small[0:1, 0:TS]
        MS = small[0:1, TS:2 * TS]
        NHM = NH + 8

        @block.sync
        def _(sync):
            for s in range(NS):
                if s >= 2:
                    sync.wait_ge(s_mh.h, (s - 1) * NHM)
                src = r3(dsc["xn3"])[:, :, s * TS:(s + 1) * TS]
                sync.dma_start(r3(xnb[s % 2].ap(), c=NC), src).then_inc(s_ld.h, 16)
                for m in range(NHM):
                    km = s * NHM + m
                    sync.wait_ge(s_eh.h, km + 1)
                    if m < NH:
                        dst = dsc["qn_sc"][:, m * T + s * TS: m * T + (s + 1) * TS]
                        sync.dma_start(dst, qot[km % 2][:]).then_inc(s_st.h, 16)
                    else:
                        mm = m - NH
                        base = (2 * mm) * T + s * TS
                        dst = bass.AP(dsc["qpe_sc"].tensor, base,
                                      [[T, 2], [NH * T, 64], [1, TS]])
                        sync.dma_start(dst, qot[km % 2][:]).then_inc(s_st.h, 16)
            sync.wait_ge(s_st.h, NS * NHM * 16)

        @block.tensor
        def _(tensor):
            for s in range(NS):
                tensor.wait_ge(s_ld.h, (s + 1) * 16)
                for m in range(12):
                    k = s * 12 + m
                    if k >= 2:
                        tensor.wait_ge(s_eq.h, 2 * (k - 1))
                    for dc in range(NC):
                        ins = tensor.matmul(pql[k % 2][:],
                                            qab[:, dc * 1536 + m * 128: dc * 1536 + (m + 1) * 128],
                                            xnb[s % 2][:, dc * TS:(dc + 1) * TS],
                                            start=(dc == 0), stop=(dc == NC - 1))
                    ins.then_inc(s_mq.h, 1)
                    tensor.wait_ge(s_eq.h, 2 * k + 2)
                    if m == 0 and s >= 1:
                        tensor.wait_ge(s_ms.h, s)
                    tensor.matmul(pns[0:1, :], P.ones_col_b[:, 0:1],
                                  qsq[:],
                                  start=(m == 0), stop=(m == 11)).then_inc(s_ns.h, 1)
                tensor.wait_ge(s_rs.h, s + 1)
                if s >= 1:
                    tensor.wait_ge(s_qn.h, s * 12)
                tensor.matmul(prs[:], P.ones_row[0:1, :], RSTD,
                              start=True, stop=True).then_inc(s_rp.h, 1)
                for m in range(NHM):
                    km = s * NHM + m
                    if km >= 2:
                        tensor.wait_ge(s_eh.h, km - 1)
                    if m == 0:
                        tensor.wait_ge(s_qn.h, (s + 1) * 12)
                    if m < NH:
                        cols0 = m * 128
                    else:
                        cols0 = 2048 + (m - NH) * 128
                    for kl in range(12):
                        ins = tensor.matmul(pq[km % 2][:],
                                            qbb[:, kl * 3072 + cols0: kl * 3072 + cols0 + 128],
                                            qln[:, kl * TS:(kl + 1) * TS],
                                            start=(kl == 0), stop=(kl == 11))
                    ins.then_inc(s_mh.h, 1)

        @block.scalar
        def _(scalar):
            for s in range(NS):
                for m in range(12):
                    k = s * 12 + m
                    scalar.wait_ge(s_mq.h, k + 1)
                    if k >= 1:
                        scalar.wait_ge(s_ns.h, k)
                    if s >= 1 and m == 0:
                        scalar.wait_ge(s_qn.h, s * 12)
                    scalar.activation(qlf[:, m * TS:(m + 1) * TS], pql[k % 2][:],
                                      AF.Copy)
                    scalar.activation(qsq[:], pql[k % 2][:], AF.Square) \
                        .then_inc(s_eq.h, 2)
                scalar.wait_ge(s_ms.h, s + 1)
                if s >= 1:
                    scalar.wait_ge(s_rp.h, s)
                scalar.activation(RSTD, MS, AF.Ln, bias=E_RMS)
                scalar.activation(RSTD, RSTD, AF.Exp, scale=-0.5).then_inc(s_rs.h, 1)
                for m in range(NHM):
                    km = s * NHM + m
                    scalar.wait_ge(s_mh.h, km + 1)
                    if km >= 2:
                        scalar.wait_ge(s_st.h, (km - 1) * 16)
                    scalar.activation(qot[km % 2][:], pq[km % 2][:], AF.Copy) \
                        .then_inc(s_eh.h, 1)

        @block.vector
        def _(vector):
            for s in range(NS):
                vector.wait_ge(s_ns.h, (s + 1) * 12)
                vector.tensor_scalar(MS, pns[0:1, :], 1.0 / QLR, None, Op.mult) \
                    .then_inc(s_ms.h, 1)
                vector.wait_ge(s_rp.h, s + 1)
                if s >= 1:
                    vector.wait_ge(s_mh.h, s * NHM)
                for m in range(12):
                    vector.scalar_tensor_tensor(qln[:, m * TS:(m + 1) * TS],
                                                qlf[:, m * TS:(m + 1) * TS],
                                                P.qa_lnw[:, m:m + 1], prs[:],
                                                Op.mult, Op.mult).then_inc(s_qn.h, 1)


def emit_attn_kv(P, dins, dsc, tag="ak"):
    block, nc = P.block, P.nc
    with ExitStack() as st:
        E = st.enter_context
        kvab = E(nc.sbuf_tensor("kvab", [128, NC * 576], BF16))
        kvbb = E(nc.sbuf_tensor("kvbb", [128, 4 * 4096], BF16))
        emit_cast_weights(P, dins["kvaT"], kvab, 2048, 576, tag + "a")
        emit_cast_weights(P, dins["kvbT"], kvbb, 512, 4096, tag + "b")
        P.barrier(tag + "w")

        s_ld, s_mc, s_ec, s_ns, s_ms, s_rs, s_rp, s_cn, s_mh, s_eh, s_rope, s_st, \
            s_rot = (P.sem() for _ in range(13))
        xnb = [E(nc.sbuf_tensor(tag + "_xn%d" % j, [128, NC * TS], BF16)) for j in range(2)]
        ropbC = E(nc.sbuf_tensor(tag + "_ropC", [64, TS], F32))
        ropbS = E(nc.sbuf_tensor(tag + "_ropS", [64, TS], F32))
        cf = E(nc.sbuf_tensor(tag + "_cf", [128, 4 * TS], BF16))
        csq = E(nc.sbuf_tensor(tag + "_csq", [128, TS], BF16))
        cnb = E(nc.sbuf_tensor(tag + "_cnb", [128, 4 * TS], BF16))
        kpf = E(nc.sbuf_tensor(tag + "_kpf", [64, TS], BF16))
        rt1 = E(nc.sbuf_tensor(tag + "_rt1", [64, TS], F32))
        rt2 = E(nc.sbuf_tensor(tag + "_rt2", [64, TS], F32))
        prot = E(nc.psum_tensor(tag + "_prot", [64, TS], F32))
        kpo = E(nc.sbuf_tensor(tag + "_kpo", [64, TS], BF16))
        kot = [E(nc.sbuf_tensor(tag + "_ko%d" % j, [128, TS], BF16)) for j in range(2)]
        small = E(nc.sbuf_tensor(tag + "_small", [1, 2 * TS], F32))
        pc = [E(nc.psum_tensor(tag + "_pc%d" % j, [128, TS], F32)) for j in range(2)]
        pns = E(nc.psum_tensor(tag + "_pns", [128, TS], F32))
        prs = E(nc.psum_tensor(tag + "_prs", [128, TS], F32))
        pk = [E(nc.psum_tensor(tag + "_pk%d" % j, [128, TS], F32)) for j in range(2)]
        RSTD = small[0:1, 0:TS]
        MS = small[0:1, TS:2 * TS]
        NKM = NH + 16

        @block.sync
        def _(sync):
            for s in range(NS):
                if s >= 2:
                    sync.wait_ge(s_mh.h, (s - 1) * NKM)
                src = r3(dsc["xn3"])[:, :, s * TS:(s + 1) * TS]
                sync.dma_start(r3(xnb[s % 2].ap(), c=NC), src).then_inc(s_ld.h, 32)
                sync.dma_start(ropbC[:], dins["ropeC"][:, s * TS:(s + 1) * TS]) \
                    .then_inc(s_ld.h, 32)
                sync.dma_start(ropbS[:], dins["ropeS"][:, s * TS:(s + 1) * TS]) \
                    .then_inc(s_ld.h, 32)
                for m in range(NKM):
                    km = s * NKM + m
                    sync.wait_ge(s_eh.h, km + 1)
                    if m < NH:
                        dst = dsc["kn_sc"][:, m * T + s * TS: m * T + (s + 1) * TS]
                        sync.dma_start(dst, kot[km % 2][:]).then_inc(s_st.h, 16)
                    else:
                        mv = m - NH
                        tb, vc = mv // 4, mv % 4
                        base = (4 * vc) * T + 128 * (4 * s + tb)
                        dst = bass.AP(dsc["v_sc"].tensor, base,
                                      [[NH * T, 128], [T, 4], [1, 128]])
                        sync.dma_start(dst, kot[km % 2][:]).then_inc(s_st.h, 16)
                sync.wait_ge(s_rope.h, s + 1)
                sync.dma_start(dsc["kpe_sc"][:, s * TS:(s + 1) * TS], kpo[0:64, :]) \
                    .then_inc(s_st.h, 16)
            sync.wait_ge(s_st.h, NS * (NKM + 1) * 16)

        @block.tensor
        def _(tensor):
            for s in range(NS):
                tensor.wait_ge(s_ld.h, (s + 1) * 96)
                for m in range(5):
                    k = s * 5 + m
                    if k >= 2:
                        tensor.wait_ge(s_ec.h, 2 * (k - 1))
                    rows = 128 if m < 4 else 64
                    for dc in range(NC):
                        ins = tensor.matmul(pc[k % 2][0:rows, :],
                                            kvab[:, dc * 576 + m * 128:
                                                 dc * 576 + m * 128 + rows],
                                            xnb[s % 2][:, dc * TS:(dc + 1) * TS],
                                            start=(dc == 0), stop=(dc == NC - 1))
                    ins.then_inc(s_mc.h, 1)
                    if m < 4:
                        tensor.wait_ge(s_ec.h, 2 * k + 2)
                        if m == 0 and s >= 1:
                            tensor.wait_ge(s_ms.h, s)
                        tensor.matmul(pns[0:1, :], P.ones_col_b[:, 0:1],
                                      csq[:],
                                      start=(m == 0), stop=(m == 3)).then_inc(s_ns.h, 1)
                tensor.wait_ge(s_ec.h, s * 10 + 10)
                if s >= 1:
                    tensor.wait_ge(s_rope.h, s)
                tensor.matmul(prot[0:64, :], P.rot64[0:64, :], kpf[0:64, :],
                              start=True, stop=True).then_inc(s_rot.h, 1)
                tensor.wait_ge(s_rs.h, s + 1)
                if s >= 1:
                    tensor.wait_ge(s_cn.h, s * 4)
                tensor.matmul(prs[:], P.ones_row[0:1, :], RSTD,
                              start=True, stop=True).then_inc(s_rp.h, 1)
                for m in range(NKM):
                    km = s * NKM + m
                    if km >= 2:
                        tensor.wait_ge(s_eh.h, km - 1)
                    if m == 0:
                        tensor.wait_ge(s_cn.h, (s + 1) * 4)
                    if m < NH:
                        for kl in range(4):
                            ins = tensor.matmul(pk[km % 2][:],
                                          kvbb[:, kl * 4096 + m * 128:
                                               kl * 4096 + (m + 1) * 128],
                                          cnb[:, kl * TS:(kl + 1) * TS],
                                          start=(kl == 0), stop=(kl == 3))
                    else:
                        mv = m - NH
                        tb, vc = mv // 4, mv % 4
                        for kl in range(4):
                            ins = tensor.matmul(pk[km % 2][:],
                                          cnb[:, kl * TS + tb * 128: kl * TS + (tb + 1) * 128],
                                          kvbb[:, kl * 4096 + 2048 + vc * TS:
                                               kl * 4096 + 2048 + (vc + 1) * TS],
                                          start=(kl == 0), stop=(kl == 3))
                    ins.then_inc(s_mh.h, 1)

        @block.scalar
        def _(scalar):
            for s in range(NS):
                for m in range(5):
                    k = s * 5 + m
                    scalar.wait_ge(s_mc.h, k + 1)
                    if m < 4:
                        if s * 4 + m >= 1:
                            scalar.wait_ge(s_ns.h, s * 4 + m)
                        if s >= 1 and m == 0:
                            scalar.wait_ge(s_cn.h, s * 4)
                        scalar.activation(cf[:, m * TS:(m + 1) * TS], pc[k % 2][:],
                                          AF.Copy)
                        scalar.activation(csq[:], pc[k % 2][:], AF.Square) \
                            .then_inc(s_ec.h, 2)
                    else:
                        scalar.activation(kpf[0:64, :], pc[k % 2][0:64, :], AF.Copy) \
                            .then_inc(s_ec.h, 2)
                scalar.wait_ge(s_ms.h, s + 1)
                if s >= 1:
                    scalar.wait_ge(s_rp.h, s)
                scalar.activation(RSTD, MS, AF.Ln, bias=E_RMS)
                scalar.activation(RSTD, RSTD, AF.Exp, scale=-0.5).then_inc(s_rs.h, 1)
                for m in range(NKM):
                    km = s * NKM + m
                    scalar.wait_ge(s_mh.h, km + 1)
                    if km >= 2:
                        scalar.wait_ge(s_st.h, (km - 1) * 16)
                    scalar.activation(kot[km % 2][:], pk[km % 2][:], AF.Copy) \
                        .then_inc(s_eh.h, 1)

        @block.vector
        def _(vector):
            for s in range(NS):
                vector.wait_ge(s_ns.h, (s + 1) * 4)
                vector.tensor_scalar(MS, pns[0:1, :], 1.0 / KVLR, None, Op.mult) \
                    .then_inc(s_ms.h, 1)
                vector.wait_ge(s_rp.h, s + 1)
                if s >= 1:
                    vector.wait_ge(s_mh.h, s * NKM)
                for m in range(4):
                    vector.scalar_tensor_tensor(cnb[:, m * TS:(m + 1) * TS],
                                                cf[:, m * TS:(m + 1) * TS],
                                                P.kva_lnw[:, m:m + 1], prs[:],
                                                Op.mult, Op.mult).then_inc(s_cn.h, 1)
                vector.wait_ge(s_rot.h, s + 1)
                if s >= 1:
                    vector.wait_ge(s_st.h, s * (NKM + 1) * 16)
                vector.tensor_tensor(rt1[0:64, :], kpf[0:64, :], ropbC[0:64, :], Op.mult)
                vector.tensor_tensor(rt2[0:64, :], prot[0:64, :], ropbS[0:64, :], Op.mult)
                vector.tensor_tensor(kpo[0:64, :], rt1[0:64, :], rt2[0:64, :], Op.add) \
                    .then_inc(s_rope.h, 1)


def _ab_schedule():
    sched = []
    for qb in range(NS):
        if qb == 0:
            order = [0, 1, 2, 3]
        else:
            order = [0] + [4 * qb + jj for jj in range(4)] + list(range(1, 4 * qb))
        for i, kc in enumerate(order):
            qv0 = max(0, kc - 4 * qb) * 128
            sched.append((qb, kc, qv0, i == len(order) - 1))
    return sched, len(sched)


def emit_attn_core(P, dins, dsc, tag="ab"):
    block, nc = P.block, P.nc
    sched, SC_H = _ab_schedule()
    with ExitStack() as st:
        E = st.enter_context
        s_ld = P.sem("abld")
        s_rope = P.sem("abrope")
        s_rot = P.sem("abrot")
        s_rte = P.sem("abrte")
        s_sc = P.sem("absc")
        s_ex = P.sem("abex")
        s_mk = P.sem("abmk")
        s_mz = P.sem("abmz")
        s_av = P.sem("abav")
        s_dn = P.sem("abdn")
        s_dc = P.sem("abdc")
        s_pd = P.sem("abpd")
        s_rc = P.sem("abrc")
        s_rp = P.sem("abrp")
        s_rs = P.sem("abrs")
        s_no = P.sem("abno")
        s_st = P.sem("abst")

        qn = [E(nc.sbuf_tensor("ab_qn%d" % j, [128, T], BF16)) for j in range(2)]
        qpe = [E(nc.sbuf_tensor("ab_qpe%d" % j, [64, T], BF16)) for j in range(2)]
        qpr = [E(nc.sbuf_tensor("ab_qpr%d" % j, [64, T], BF16)) for j in range(2)]
        rotb = E(nc.sbuf_tensor("ab_rotb", [64, T], BF16))
        kn = [E(nc.sbuf_tensor("ab_kn%d" % j, [128, T], BF16)) for j in range(2)]
        vb = [E(nc.sbuf_tensor("ab_vb%d" % j, [128, T], BF16)) for j in range(2)]
        kpe = E(nc.sbuf_tensor("ab_kpe", [64, T], BF16))
        ropC = E(nc.sbuf_tensor("ab_ropC", [64, T], F32))
        ropS = E(nc.sbuf_tensor("ab_ropS", [64, T], F32))
        rr1 = E(nc.sbuf_tensor("ab_rr1", [64, T], BF16))
        rr2 = E(nc.sbuf_tensor("ab_rr2", [64, T], BF16))
        pt = [E(nc.sbuf_tensor("ab_pt%d" % j, [128, TS], BF16)) for j in range(12)]
        den = [E(nc.sbuf_tensor("ab_den%d" % j, [128, TS], F32)) for j in range(2)]
        denr = E(nc.sbuf_tensor("ab_denr", [128, TS], F32R))
        rcp = E(nc.sbuf_tensor("ab_rcp", [8, TS], F32))
        rcpr = E(nc.sbuf_tensor("ab_rcpr", [8, TS], F32R))
        rS = [E(nc.sbuf_tensor("ab_rs%d" % j, [128, TS], F32)) for j in range(2)]
        ato = [E(nc.sbuf_tensor("ab_ato%d" % j, [128, T], BF16)) for j in range(2)]
        psc = [E(nc.psum_tensor("ab_psc%d" % j, [128, TS], F32)) for j in range(5)]
        pav = [E(nc.psum_tensor("ab_pav%d" % j, [128, TS], F32)) for j in range(2)]
        pdn = E(nc.psum_tensor("ab_pdn", [128, TS], F32))
        pden = pdn
        prep = pdn
        protq = psc[4]

        @block.sync
        def _(sync):
            sync.dma_start(ropC[:], dins["ropeC"]).then_inc(s_ld.h, 16)
            sync.dma_start(ropS[:], dins["ropeS"]).then_inc(s_ld.h, 16)
            sync.dma_start(kpe[:], dsc["kpe_sc"]).then_inc(s_ld.h, 16)
            for h in range(NH):
                if h >= 2:
                    sync.wait_ge(s_sc.h, (h - 1) * SC_H)
                    sync.wait_ge(s_av.h, (h - 1) * SC_H)
                    sync.wait_ge(s_rope.h, h - 1)
                    sync.wait_ge(s_rot.h, (h - 1) * 8)
                sync.dma_start(qn[h % 2][:], dsc["qn_sc"][:, h * T:(h + 1) * T]) \
                    .then_inc(s_ld.h, 16)
                sync.dma_start(qpe[h % 2][:], dsc["qpe_sc"][:, h * T:(h + 1) * T]) \
                    .then_inc(s_ld.h, 16)
                sync.dma_start(kn[h % 2][:], dsc["kn_sc"][:, h * T:(h + 1) * T]) \
                    .then_inc(s_ld.h, 16)
                sync.dma_start(vb[h % 2][:], dsc["v_sc"][:, h * T:(h + 1) * T]) \
                    .then_inc(s_ld.h, 16)
                if h >= 1:
                    sync.wait_ge(s_no.h, h * NS)
                    sync.dma_start(dsc["atoT_sc"][:, (h - 1) * T: h * T],
                                   ato[(h - 1) % 2][:]).then_inc(s_st.h, 16)
            sync.wait_ge(s_no.h, NH * NS)
            sync.dma_start(dsc["atoT_sc"][:, (NH - 1) * T: NH * T],
                           ato[(NH - 1) % 2][:]).then_inc(s_st.h, 16)
            sync.wait_ge(s_st.h, NH * 16)

        @block.gpsimd
        def _(gpsimd):
            for h in range(NH):
                gpsimd.wait_ge(s_ld.h, 48 + (h + 1) * 64)
                gpsimd.wait_ge(s_rte.h, 8 * (h + 1))
                if h >= 2:
                    gpsimd.wait_ge(s_sc.h, (h - 1) * SC_H)
                gpsimd.tensor_tensor(rr1[0:64, :], qpe[h % 2][0:64, :], ropC[0:64, :],
                                     Op.mult)
                gpsimd.tensor_tensor(rr2[0:64, :], rotb[0:64, :], ropS[0:64, :],
                                     Op.mult)
                gpsimd.tensor_tensor(qpr[h % 2][0:64, :], rr1[0:64, :], rr2[0:64, :],
                                     Op.add).then_inc(s_rope.h, 1)
                for kc in (1, 2, 3):
                    k = h * SC_H + kc
                    if k >= 12:
                        gpsimd.wait_ge(s_av.h, k - 11)
                        gpsimd.wait_ge(s_dn.h, k - 11)
                    gpsimd.memset(pt[k % 12][:, 0:kc * 128], 0.0).then_inc(s_mz.h, 1)

        @block.tensor
        def _(tensor):
            G = 4
            NG = SC_H // G

            def emit_av_group(gi, h):
                kg = h * SC_H + gi * G
                tensor.wait_ge(s_ex.h, kg + G)
                nmask = sum(1 for (qb, kc, _, _) in sched[:gi * G + G]
                            if kc >= 4 * qb)
                if nmask:
                    tensor.wait_ge(s_mk.h, h * 32 + nmask)
                mz = [kc for (qb, kc, _, _) in sched[gi * G: gi * G + G]
                      if qb == 0 and kc >= 1]
                if mz:
                    tensor.wait_ge(s_mz.h, h * 3 + max(mz))
                for ci_av in range(gi * G, gi * G + G):
                    qb, kc, qv0, last = sched[ci_av]
                    k = h * SC_H + ci_av
                    gq = h * NS + qb
                    av0 = 0 if qb == 0 else qv0
                    if kc == 0 and gq >= 2:
                        tensor.wait_ge(s_no.h, gq - 1)
                    tensor.matmul(pav[qb % 2][:, av0:TS],
                                  vb[h % 2][:, kc * 128:(kc + 1) * 128],
                                  pt[k % 12][:, av0:TS],
                                  start=(kc == 0), stop=last).then_inc(s_av.h, 1)
                    if last:
                        tensor.wait_ge(s_dc.h, gq + 1)
                        if gq >= 1:
                            tensor.wait_ge(s_rs.h, gq)
                        tensor.matmul(pden[0:8, :], P.ones8_r[:, 0:8], denr[:],
                                      start=True, stop=True).then_inc(s_pd.h, 1)
                        tensor.wait_ge(s_rc.h, gq + 1)
                        tensor.matmul(prep[:], P.ones_row_r[0:1, :], rcpr[0:1, :],
                                      start=True, stop=True).then_inc(s_rp.h, 1)

            for h in range(NH):
                tensor.wait_ge(s_ld.h, 48 + (h + 1) * 64)
                if h >= 1:
                    tensor.wait_ge(s_ex.h, h * SC_H)
                for kq in range(8):
                    kk = h * 8 + kq
                    if kk >= 1:
                        tensor.wait_ge(s_rte.h, kk)
                    tensor.matmul(protq[0:64, :], P.rot64[0:64, :],
                                  qpe[h % 2][:, kq * TS:(kq + 1) * TS],
                                  start=True, stop=True).then_inc(s_rot.h, 1)
                tensor.wait_ge(s_rope.h, h + 1)
                for gi in range(NG):
                    kg = h * SC_H + gi * G
                    if kg >= 1:
                        tensor.wait_ge(s_ex.h, kg if gi == 0 else kg - 1)
                    if gi == 1:
                        tensor.wait_ge(s_rte.h, 8 * (h + 1))
                    for ci in range(gi * G, gi * G + G):
                        qb, kc, qv0, last = sched[ci]
                        q0 = qb * TS
                        tensor.matmul(psc[ci % 5][:, qv0:TS],
                                      kn[h % 2][:, kc * 128:(kc + 1) * 128],
                                      qn[h % 2][:, q0 + qv0: q0 + TS],
                                      start=True, stop=False)
                        tensor.matmul(psc[ci % 5][:, qv0:TS],
                                      kpe[:, kc * 128:(kc + 1) * 128],
                                      qpr[h % 2][:, q0 + qv0: q0 + TS],
                                      start=False, stop=True).then_inc(s_sc.h, 1)
                    if gi >= 2:
                        emit_av_group(gi - 2, h)
                emit_av_group(NG - 2, h)
                emit_av_group(NG - 1, h)

        @block.scalar
        def _(scalar):
            for h in range(NH):
                for ci, (qb, kc, qv0, last) in enumerate(sched):
                    k = h * SC_H + ci
                    gq = h * NS + qb
                    scalar.wait_ge(s_sc.h, k + 1)
                    if k >= 12:
                        scalar.wait_ge(s_av.h, k - 11)
                        scalar.wait_ge(s_dn.h, k - 11)
                    scalar.activation(pt[k % 12][:, qv0:TS], psc[ci % 5][:, qv0:TS],
                                      AF.Exp, bias=SM_SHIFT, scale=INV_SQ) \
                        .then_inc(s_ex.h, 1)

        @block.vector
        def _(vector):
            for h in range(NH):
                for kq in range(8):
                    kk = h * 8 + kq
                    vector.wait_ge(s_rot.h, kk + 1)
                    if kq == 0 and h >= 1:
                        vector.wait_ge(s_rope.h, h)
                    vector.tensor_copy(rotb[:, kq * TS:(kq + 1) * TS], protq[0:64, :]) \
                        .then_inc(s_rte.h, 1)
                for ci, (qb, kc, qv0, last) in enumerate(sched):
                    k = h * SC_H + ci
                    gq = h * NS + qb
                    vector.wait_ge(s_ex.h, k + 1)
                    if kc >= 4 * qb:
                        jj = kc - 4 * qb
                        vector.tensor_tensor(pt[k % 12][:, jj * 128:(jj + 1) * 128],
                                             pt[k % 12][:, jj * 128:(jj + 1) * 128],
                                             P.tril_b[:], Op.mult).then_inc(s_mk.h, 1)
                    if kc == 0:
                        vector.tensor_copy(den[qb % 2][:], pt[k % 12][:, 0:TS]) \
                            .then_inc(s_dn.h, 1)
                    else:
                        vector.tensor_tensor(den[qb % 2][:, qv0:TS],
                                             den[qb % 2][:, qv0:TS],
                                             pt[k % 12][:, qv0:TS], Op.add) \
                            .then_inc(s_dn.h, 1)
                    if last:
                        if gq >= 1:
                            vector.wait_ge(s_pd.h, gq)
                        vector.tensor_copy(denr[:], den[qb % 2][:]).then_inc(s_dc.h, 1)
                        vector.wait_ge(s_pd.h, gq + 1)
                        if gq >= 1:
                            vector.wait_ge(s_rp.h, gq)
                        vector.reciprocal(rcp[0:8, :], pden[0:8, :])
                        vector.tensor_copy(rcpr[0:8, :], rcp[0:8, :]).then_inc(s_rc.h, 1)
                        vector.wait_ge(s_rp.h, gq + 1)
                        vector.tensor_copy(rS[qb % 2][:], prep[:]).then_inc(s_rs.h, 1)
                        vector.wait_ge(s_av.h, k + 1)
                        if h >= 2 and qb == 0:
                            vector.wait_ge(s_st.h, (h - 1) * 16)
                        vector.tensor_tensor(ato[h % 2][:, qb * TS:(qb + 1) * TS],
                                             pav[qb % 2][:], rS[qb % 2][:], Op.mult) \
                            .then_inc(s_no.h, 1)


def emit_oproj(P, dins, dsc, tag="ao"):
    block, nc = P.block, P.nc
    with ExitStack() as st:
        E = st.enter_context
        owb = E(nc.sbuf_tensor("owb", [128, NC * 2048], BF16))
        emit_cast_weights(P, dins["owT"], owb, 2048, 2048, tag)
        P.barrier(tag + "w")
        s_ld, s_mo, s_ev, s_st = (P.sem() for _ in range(4))
        atb = [E(nc.sbuf_tensor(tag + "_at%d" % j, [128, NH * TS], BF16)) for j in range(2)]
        rbuf = [E(nc.sbuf_tensor(tag + "_r%d" % j, [128, NC * TS], F32)) for j in range(2)]
        ot = [E(nc.sbuf_tensor(tag + "_ot%d" % j, [128, TS], F32)) for j in range(2)]
        po = [E(nc.psum_tensor(tag + "_po%d" % j, [128, TS], F32)) for j in range(2)]

        @block.sync
        def _(sync):
            for s in range(NS):
                if s >= 2:
                    sync.wait_ge(s_mo.h, (s - 1) * NC)
                    sync.wait_ge(s_ev.h, (s - 1) * NC)
                for h in range(NH):
                    sync.dma_start(atb[s % 2][:, h * TS:(h + 1) * TS],
                                   dsc["atoT_sc"][:, h * T + s * TS: h * T + (s + 1) * TS]) \
                        .then_inc(s_ld.h, 16)
                src2 = r3(dsc["resid2"])[:, :, s * TS:(s + 1) * TS]
                sync.dma_start(r3(rbuf[s % 2].ap(), c=NC), src2).then_inc(s_ld.h, 16)
                for oc in range(NC):
                    ko = s * NC + oc
                    sync.wait_ge(s_ev.h, ko + 1)
                    dst = dsc["resid3"][:, oc * T + s * TS: oc * T + (s + 1) * TS]
                    sync.dma_start(dst, ot[ko % 2][:]).then_inc(s_st.h, 16)
            sync.wait_ge(s_st.h, NS * NC * 16)

        @block.tensor
        def _(tensor):
            for s in range(NS):
                tensor.wait_ge(s_ld.h, (s + 1) * 17 * 16)
                for oc in range(NC):
                    ko = s * NC + oc
                    if ko >= 2:
                        tensor.wait_ge(s_ev.h, ko - 1)
                    for h in range(NH):
                        ins = tensor.matmul(po[ko % 2][:],
                                            owb[:, h * 2048 + oc * 128: h * 2048 + (oc + 1) * 128],
                                            atb[s % 2][:, h * TS:(h + 1) * TS],
                                            start=(h == 0), stop=(h == NH - 1))
                    ins.then_inc(s_mo.h, 1)

        @block.vector
        def _(vector):
            for s in range(NS):
                for oc in range(NC):
                    ko = s * NC + oc
                    vector.wait_ge(s_mo.h, ko + 1)
                    if ko >= 2:
                        vector.wait_ge(s_st.h, (ko - 1) * 16)
                    vector.tensor_tensor(ot[ko % 2][:], po[ko % 2][:],
                                         rbuf[s % 2][:, oc * TS:(oc + 1) * TS], Op.add) \
                        .then_inc(s_ev.h, 1)


def build_program(phases=ALL_PHASES, debug=False, race_detect=False):
    nc = bass.Bass("TRN2", target_bir_lowering=False,
                   detect_race_conditions=race_detect)
    dins = {}
    for k, shp in GLOBAL_INPUTS.items():
        dins[k] = nc.dram_tensor(k, list(shp), F32, kind="ExternalInput").ap()
    for k, shp in GLOBAL_INPUTS_BF.items():
        dins[k] = nc.dram_tensor(k, list(shp), BF16, kind="ExternalInput").ap()
    dins["x0"] = nc.dram_tensor("x0", [T, D], F32, kind="ExternalInput").ap()
    out = nc.dram_tensor("out", [T, D], F32, kind="ExternalOutput").ap()
    skind = "ExternalOutput" if debug else "Internal"
    dsc = {}
    for k, shp in SCRATCH.items():
        dsc[k] = nc.dram_tensor(k, list(shp), F32, kind=skind).ap()
    for k, shp in SCRATCH_R.items():
        dsc[k] = nc.dram_tensor(k, list(shp), F32R, kind="Internal").ap()
    if FFN_DBG:
        dsc["dbg_lgt"] = nc.dram_tensor("dbg_lgt", [128, NS * 32], F32, kind="ExternalOutput").ap()
        dsc["dbg_wv"] = nc.dram_tensor("dbg_wv", [128, NS * 32], F32, kind="ExternalOutput").ap()
        dsc["dbg_wT"] = nc.dram_tensor("dbg_wT", [8, NS * TS], F32, kind="ExternalOutput").ap()
        dsc["dbg_zt"] = nc.dram_tensor("dbg_zt", [128, NS * TS], BF16, kind="ExternalOutput").ap()
        dsc["dbg_xat"] = nc.dram_tensor("dbg_xat", [128, NS * TS], BF16, kind="ExternalOutput").ap()
    for k, shp in SCRATCH_BF.items():
        dsc[k] = nc.dram_tensor(k, list(shp), BF16, kind=skind).ap()

    with ExitStack() as stack:
        block = stack.enter_context(nc.Block())
        P = Prog(nc, stack, block)
        P.dsc = dsc
        P.ident_f = stack.enter_context(nc.sbuf_tensor("ident_f", [128, 128], F32))
        P.ident_b = stack.enter_context(nc.sbuf_tensor("ident_b", [128, 128], BF16))
        P.ones_row = stack.enter_context(nc.sbuf_tensor("ones_rowS", [1, 128], F32))
        P.ones_col_b = stack.enter_context(nc.sbuf_tensor("ones_col_bS", [128, 1], BF16))
        P.ones_col_f = stack.enter_context(nc.sbuf_tensor("ones_col_fS", [128, 1], F32))
        P.ones_col_r = stack.enter_context(nc.sbuf_tensor("ones_col_rS", [128, 1], F32R))
        P.ones_row_r = stack.enter_context(nc.sbuf_tensor("ones_row_rS", [1, 128], F32R))
        P.ones8_f = stack.enter_context(nc.sbuf_tensor("ones8_fS", [128, 8], F32))
        P.ones8_r = stack.enter_context(nc.sbuf_tensor("ones8_rS", [128, 8], F32R))
        P.sel_replr = stack.enter_context(nc.sbuf_tensor("sel_replrS", [128, 2048], F32R))
        P.sel16 = stack.enter_context(nc.sbuf_tensor("sel16S", [16, 128], F32))
        P.sel_er = stack.enter_context(nc.sbuf_tensor("sel_erS", [8, 128], F32))
        P.tsel_b = stack.enter_context(nc.sbuf_tensor("tsel_bS", [128, 2048], BF16))
        P.tril_b = stack.enter_context(nc.sbuf_tensor("tril_bS", [128, 128], BF16))
        P.a_cols = stack.enter_context(nc.sbuf_tensor("a_colsS", [128, 256], F32))
        P.d_cols = stack.enter_context(nc.sbuf_tensor("d_colsS", [128, 16], F32))
        P.qa_lnw = stack.enter_context(nc.sbuf_tensor("qa_lnwS", [128, 12], F32))
        P.kva_lnw = stack.enter_context(nc.sbuf_tensor("kva_lnwS", [128, 4], F32))
        P.fb_cols = stack.enter_context(nc.sbuf_tensor("fb_colsS", [128, 16], F32))
        P.gate_b = stack.enter_context(nc.sbuf_tensor("gate_bS", [8, 1], F32))
        P.rot64 = stack.enter_context(nc.sbuf_tensor("rot64S", [64, 64], BF16))
        P.fconsts = stack.enter_context(nc.sbuf_tensor("fconstsS", [128, 8], F32))
        for i, v in enumerate([E_LN, E_RMS, SM_SHIFT, INV_SQ, -0.5, -1.0, 1.0, 0.0]):
            nc.const_aps.register(F32, float(v), P.fconsts[:, i:i + 1])

        sd = P.sem("dma0")
        consts = [("ident_f", P.ident_f, "ident_f32"), ("ident_b", P.ident_b, "ident_bf"),
                  ("ones_row", P.ones_row, "ones_row"), ("ones_col_b", P.ones_col_b, "ones_col_bf"),
                  ("ones_col_f", P.ones_col_f, "ones_col_f"),
                  ("ones8_f", P.ones8_f, "ones8_f"),
                  ("sel16", P.sel16, "sel16"),
                  ("sel_er", P.sel_er, "sel_er"), ("tsel_b", P.tsel_b, "tsel_bf"),
                  ("tril_b", P.tril_b, "tril_bf"), ("a_cols", P.a_cols, "a_log_cols"),
                  ("d_cols", P.d_cols, "d_cols"), ("qa_lnw", P.qa_lnw, "qa_lnw_cols"),
                  ("kva_lnw", P.kva_lnw, "kva_lnw_cols"), ("fb_cols", P.fb_cols, "fb_cols"),
                  ("gate_b", P.gate_b, "gate_b_col"), ("rot64", P.rot64, "rot64_bf"),
                  ("fconsts", P.fconsts, "fconsts")]

        @block.sync
        def _(sync):
            for _, sb, name in consts:
                sync.dma_start(sb[:], dins[name]).then_inc(sd.h, 16)

        nload = 16 * len(consts)

        @block.scalar
        def _(scalar):
            scalar.wait_ge(sd.h, nload)
            scalar.activation(P.a_cols[:], P.a_cols[:], AF.Exp)

        @block.vector
        def _(vector):
            vector.wait_ge(sd.h, nload)

        P.barrier("c0")

        with nc.sbuf_tensor("sel_replT", [128, 2048], F32) as sel_tmp:
            @block.sync
            def _(sync):
                sync.dma_start(sel_tmp[:], dins["sel_repl"]).then_inc(sd.h, 16)

            @block.vector
            def _(vector):
                vector.wait_ge(sd.h, nload + 16)
                vector.tensor_scalar(P.a_cols[:], P.a_cols[:], -1.0, None, Op.mult)
                vector.tensor_copy(P.sel_replr[:], sel_tmp[:])
                vector.tensor_copy(P.ones_col_r[:], P.ones_col_f[:])
                vector.tensor_copy(P.ones_row_r[0:1, :], P.ones_row[0:1, :])
                vector.tensor_copy(P.ones8_r[:], P.ones8_f[:])

            P.recycle()

        if "P0" in phases:
            emit_transpose_in(P, dins["x0"], dsc["resid0"])
            P.recycle()
        if "LN1" in phases:
            emit_ln(P, dsc["resid0"], dsc["xn1"], "ln1")
            P.recycle()
        with ExitStack() as mscope2:
            if "M2" in phases:
                P.bpat = mscope2.enter_context(nc.sbuf_tensor("bpatS", [128, T], F32))
                P.cpat = mscope2.enter_context(nc.sbuf_tensor("cpatS", [128, T], F32))
            with ExitStack() as mscope1:
                if "M1" in phases:
                    P.btT = mscope1.enter_context(nc.sbuf_tensor("btT", [16, T], F32))
                    P.ctT = mscope1.enter_context(nc.sbuf_tensor("ctT", [16, T], F32))
                    emit_xproj(P, dins, dsc)
                    P.recycle()
                if "M2" in phases:
                    emit_bcpat(P)
                    P.recycle()
            if "M3" in phases:
                emit_scan(P, dsc)
                P.recycle()
        if "LN2" in phases:
            emit_ln(P, dsc["resid1"], dsc["xn2"], "ln2")
            P.recycle()
        if "FFN1" in phases:
            emit_ffn(P, dins, dsc["xn2"], dsc["resid1"], dsc["resid2"], "f1")
            P.recycle()
        if "LN3" in phases:
            emit_ln(P, dsc["resid2"], dsc["xn3"], "ln3")
            P.recycle()
        if "AQ" in phases:
            emit_attn_q(P, dins, dsc)
            P.recycle()
        if "AKV" in phases:
            emit_attn_kv(P, dins, dsc)
            P.recycle()
        if "AB" in phases:
            emit_attn_core(P, dins, dsc)
            P.recycle()
        if "AO" in phases:
            emit_oproj(P, dins, dsc)
            P.recycle()
        if "LN4" in phases:
            emit_ln(P, dsc["resid3"], dsc["xn4"], "ln4")
            P.recycle()
        if "FFN2" in phases:
            emit_ffn(P, dins, dsc["xn4"], dsc["resid3"], dsc["resid4"], "f2")
            P.recycle()
        if "OUT" in phases:
            src = dsc["resid4"] if "FFN2" in phases else dsc["resid0"]
            emit_transpose_out(P, src, out)
            P.recycle()

    return nc


def kernel(**inputs):
    g = host_prepare(inputs)
    x = np.asarray(inputs["L0_attn_a_x"], np.float32)
    nc = build_program()
    in_maps = []
    for b in range(8):
        m = dict(g)
        m["x0"] = np.ascontiguousarray(x[b])
        in_maps.append(m)
    res = run_bass_kernel_spmd(nc, in_maps, core_ids=list(range(8)))
    outs = [np.asarray(res.results[b]["out"], np.float32) for b in range(8)]
    return np.stack(outs, 0)
